# version 01
# baseline (speedup 1.0000x reference)
import contextlib
import math

import numpy as np
import ml_dtypes

import concourse.bass as bass
import concourse.mybir as mybir
from concourse.bass_utils import run_bass_kernel_spmd

F32 = mybir.dt.float32
BF16 = mybir.dt.bfloat16
AF = mybir.ActivationFunctionType
ALU = mybir.AluOpType
AX = mybir.AxisListType

T = 2048
C = 1024
NCH = 8
NTT = 16
NQS = 4
LAT = 32
HD = 64
I_FF = 2048
NE = 8

EXP_SC1 = 1.0 / math.sqrt(LAT)
EXP_SC2 = 1.0 / math.sqrt(HD)

DEBUG = []
BF = ml_dtypes.bfloat16


def _host_consts():
    c = {}
    inv = 1.0 / 10000.0 ** (np.arange(0, LAT, 2, dtype=np.float64) / LAT)
    t = np.arange(T, dtype=np.float64)
    fr = np.outer(t, inv)
    cos1 = np.repeat(np.cos(fr), 2, axis=-1)[:, :LAT]
    sin1 = np.repeat(np.sin(fr), 2, axis=-1)[:, :LAT]
    c["tab1_c"] = np.tile(cos1.T.astype(np.float32), (4, 1)).astype(BF)
    c["tab1_s"] = np.tile(sin1.T.astype(np.float32), (4, 1)).astype(BF)
    R32 = np.zeros((LAT, LAT), np.float32)
    for i in range(LAT // 2):
        R32[2 * i + 1, 2 * i] = -1.0
        R32[2 * i, 2 * i + 1] = 1.0
    c["R1"] = np.kron(np.eye(4, dtype=np.float32), R32).astype(BF)
    inv2 = 1.0 / 100000.0 ** (np.arange(0, HD, 2, dtype=np.float64) / HD)
    fr2 = np.outer(t, inv2)
    C64 = np.concatenate([np.cos(fr2), np.cos(fr2)], axis=1).T.astype(np.float32)
    S64 = np.concatenate([np.sin(fr2), np.sin(fr2)], axis=1).T.astype(np.float32)
    c["tab2_c"] = np.tile(C64, (2, 1)).astype(BF)
    c["tab2_s"] = np.tile(S64, (2, 1)).astype(BF)
    R64 = np.zeros((HD, HD), np.float32)
    for d in range(32):
        R64[d + 32, d] = -1.0
        R64[d, d + 32] = 1.0
    c["R2"] = np.kron(np.eye(2, dtype=np.float32), R64).astype(BF)
    kk = np.arange(128)[:, None]
    qq = np.arange(512)[None, :]
    for j, d in enumerate([0, 128, 256, 384]):
        c[f"mask{j}"] = ((d + kk) <= qq).astype(BF)
    c["ident"] = np.eye(128, dtype=np.float32)
    c["ones_row"] = np.ones((1, 128), np.float32)
    c["ones_col_bf"] = np.ones((128, 1), np.float32).astype(BF)
    c["ones_col_f"] = np.ones((128, 1), np.float32)
    return c


def _prep(inp):
    w = {}
    f = np.ascontiguousarray
    w["caw"] = f(np.asarray(inp["L0_attn_c_attn_w"]).T)
    w["cab"] = f(np.asarray(inp["L0_attn_c_attn_b"]).reshape(24, 128).T)
    q2l = np.asarray(inp["L0_attn_q2l_w"])
    kv2l = np.asarray(inp["L0_attn_kv2l_w"])
    blk = np.zeros((256, 128), np.float32)
    for h in range(4):
        blk[h * 64:(h + 1) * 64, h * 32:(h + 1) * 32] = q2l.T
    w["W4q"] = f(blk)
    blk = np.zeros((256, 128), np.float32)
    for h in range(4):
        blk[h * 64:(h + 1) * 64, h * 32:(h + 1) * 32] = kv2l.T
    w["W4kv"] = f(blk)
    w["q2l_b4"] = f(np.tile(np.asarray(inp["L0_attn_q2l_b"]), 4).reshape(128, 1))
    w["kv2l_b4"] = f(np.tile(np.asarray(inp["L0_attn_kv2l_b"]), 4).reshape(128, 1))
    row = np.zeros((1, 132), np.float32)
    for h in range(4):
        row[0, h * 33:h * 33 + 32] = np.asarray(inp["L0_attn_kv2l_b"])
    w["kv2l_brow"] = row
    blkq = np.zeros((128, 128), np.float32)
    for h in range(4):
        blkq[h * 32:(h + 1) * 32, h * 32:(h + 1) * 32] = np.asarray(
            inp["L0_attn_wqk_w"]).T
    w["wqk4"] = f(blkq)
    w["wqk_b4"] = f(np.tile(np.asarray(inp["L0_attn_wqk_b"]), 4).reshape(128, 1))
    l2oT = np.asarray(inp["L0_attn_l2o_w"]).T
    w["l2o4"] = f(np.tile(l2oT, (4, 1)))
    w["l2o_b2"] = f(np.tile(np.asarray(inp["L0_attn_l2o_b"]), 2).reshape(128, 1))
    w["cpw"] = f(np.asarray(inp["L0_attn_cproj_w"]).T)
    w["cpb"] = f(np.asarray(inp["L0_attn_cproj_b"]).reshape(8, 128).T)
    w["ln1w"] = f(np.asarray(inp["L0_ffn_ln1_w"]).reshape(8, 128).T)
    w["ln2w"] = f(np.asarray(inp["L0_ffn_ln2_w"]).reshape(8, 128).T)
    w["Wq"] = f(np.asarray(inp["L0_ffn_Wq"]))
    w["Wkv"] = f(np.asarray(inp["L0_ffn_Wkv"]))
    w["Wk2"] = f(np.asarray(inp["L0_ffn_Wk_up"]).reshape(8, 2, 256, 64)
                 .transpose(0, 2, 1, 3).reshape(8, 256, 128))
    w["Wv2"] = f(np.asarray(inp["L0_ffn_Wv_up"]).reshape(8, 2, 256, 64)
                 .transpose(0, 2, 1, 3).reshape(8, 256, 128))
    w["Wo"] = f(np.asarray(inp["L0_ffn_Wo"]))
    w["Wr"] = f(np.asarray(inp["L0_ffn_Wr"]))
    w["r_bias"] = f(np.asarray(inp["L0_ffn_r_bias"]).reshape(8, 1))
    w["sw1"] = f(np.asarray(inp["L0_ffn_sw1"]))
    w["sw2"] = f(np.asarray(inp["L0_ffn_sw2"]))
    w["sw3"] = f(np.asarray(inp["L0_ffn_sw3"]))
    w["ew1"] = f(np.asarray(inp["L0_ffn_ew1"]))
    w["ew2"] = f(np.asarray(inp["L0_ffn_ew2"]))
    w["ew3"] = f(np.asarray(inp["L0_ffn_ew3"]))
    return w


class Ctx:
    def __init__(self):
        self.cnt = dict(pe=0, act=0, dve=0, gp=0, wl=0, xl=0, st=0)
        self.mark = {}

    def inc(self, ins, name):
        self.cnt[name] += 1
        ins.then_inc(self.sem[name], 1)
        return self.cnt[name]

    def dinc(self, ins, name):
        self.cnt[name] += 1
        ins.then_inc(self.sem[name], 16)
        return self.cnt[name]

    def chain(self, eng, ins, name="dve"):
        c = self.inc(ins, name)
        eng.wait_ge(self.sem[name], c)
        return c

    def wait(self, eng, name, val=None):
        v = self.cnt[name] if val is None else val
        if v <= 0:
            return
        if name in ("wl", "xl", "st"):
            eng.wait_ge(self.sem[name], 16 * v)
        else:
            eng.wait_ge(self.sem[name], v)


def _ar(H, off_kb, shape, dt=BF16, p0=0):
    e0 = int(off_kb * 1024) // 2
    n = shape[1] * (2 if dt == F32 else 1)
    v = H["AR"].ap()[:, e0:e0 + n]
    if dt == F32:
        v = v.bitcast(F32)
    return v[p0:p0 + shape[0], :]


def _p2(H, off_b, shape, dt=BF16, p0=0):
    e0 = off_b // 2
    n = shape[1] * (2 if dt == F32 else 1)
    v = H["P2"].ap()[:, e0:e0 + n]
    if dt == F32:
        v = v.bitcast(F32)
    return v[p0:p0 + shape[0], :]


def build_program(debug=()):
    nc = bass.Bass("TRN2", target_bir_lowering=False)
    cx = Ctx()
    consts = _host_consts()
    D = {}

    def din(name, shape, dt=F32):
        D[name] = nc.dram_tensor(name, list(shape), dt, kind="ExternalInput").ap()

    din("x", (T, C))
    din("caw", (C, 3 * C)); din("cab", (128, 24))
    din("W4q", (256, 128)); din("W4kv", (256, 128))
    din("q2l_b4", (128, 1)); din("kv2l_b4", (128, 1)); din("kv2l_brow", (1, 132))
    din("wqk4", (128, 128)); din("wqk_b4", (128, 1))
    din("l2o4", (128, 64)); din("l2o_b2", (128, 1))
    din("cpw", (C, C)); din("cpb", (128, 8))
    din("ln1w", (128, 8)); din("ln2w", (128, 8))
    din("Wq", (C, C)); din("Wkv", (C, 256))
    din("Wk2", (8, 256, 128)); din("Wv2", (8, 256, 128))
    din("Wo", (C, C)); din("Wr", (C, 8)); din("r_bias", (8, 1))
    din("sw1", (C, I_FF)); din("sw2", (I_FF, C)); din("sw3", (C, I_FF))
    din("ew1", (NE, C, I_FF)); din("ew2", (NE, I_FF, C)); din("ew3", (NE, C, I_FF))
    for k, v in consts.items():
        din("c_" + k, v.shape, BF16 if v.dtype == BF else F32)

    out = nc.dram_tensor("out", [T, C], F32, kind="ExternalOutput").ap()
    dbg = {}
    for name, shape in debug:
        dbg[name] = nc.dram_tensor("dbg_" + name, list(shape), F32,
                                   kind="ExternalOutput").ap()

    st = contextlib.ExitStack()
    with st:
        def sb(name, p, f, dt):
            return st.enter_context(nc.sbuf_tensor(name, [p, f], dt))

        rA = sb("rA", 128, NCH * T, F32)
        xb = sb("xb", 128, NCH * T, BF16)
        yT = sb("yT", 128, NCH * T, BF16)
        AR = sb("AR", 128, 23552, BF16)
        P2 = sb("P2", 128, 8320, BF16)
        wsl = sb("wsl", 128, 4096, BF16)
        s_W4q = sb("s_W4q", 128, 256, BF16)
        s_W4kv = sb("s_W4kv", 128, 256, BF16)
        s_wqk4 = sb("s_wqk4", 128, 128, BF16)
        s_l2o4 = sb("s_l2o4", 128, 64, BF16)
        s_R1 = sb("s_R1", 128, 128, BF16)
        s_R2 = sb("s_R2", 128, 128, BF16)
        s_Wr = sb("s_Wr", 128, 64, BF16)
        s_id = sb("s_id", 128, 128, F32)
        s_ones = sb("s_ones", 1, 128, F32)
        s_onesb = sb("s_onesb", 128, 1, BF16)
        s_onesf = sb("s_onesf", 128, 1, F32)
        s_bias = sb("s_bias", 128, 48, F32)
        s_sm = sb("s_sm", 128, 8, F32)
        s_rb = sb("s_rb", 8, 1, F32)
        s_bv4 = sb("s_bv4", 128, 132, F32)
        s_brow = sb("s_brow", 1, 132, F32)
        s_str16 = sb("s_str16", 16, 128, F32)
        s_ltm = sb("s_ltm", 128, 128, F32)
        s_sel = sb("s_sel", 128, 128, F32)
        s_lx = sb("s_lx", 128, 128, F32)
        s_p16 = sb("s_p16", 128, 16, F32)
        s_t16 = sb("s_t16", 128, 16, F32)
        s_a16 = sb("s_a16", 128, 16, F32)
        s_b16 = sb("s_b16", 128, 16, F32)
        s_r16 = sb("s_r16", 128, 16, F32)
        s_sc16 = sb("s_sc16", 128, 16, F32)
        s_u16 = sb("s_u16", 128, 16, F32)
        s_m2 = sb("s_m2", 2, 128, F32)

        psA = st.enter_context(nc.psum_tensor("psA", [128, 2048], F32))
        psB = st.enter_context(nc.psum_tensor("psB", [128, 2048], F32))

        sem = {}
        for name in ["pe", "act", "dve", "gp", "wl", "xl", "st"]:
            sem[name] = st.enter_context(nc.semaphore("s_" + name))
        cx.sem = sem

        block = st.enter_context(nc.Block())
        H = dict(locals())
        H["out"] = out

        ph0(nc, block, cx, H, D, dbg)
        attn1(nc, block, cx, H, D, dbg)
        ln1_rms1(nc, block, cx, H, D, dbg)
        attn2(nc, block, cx, H, D, dbg)
        rms2(nc, block, cx, H, D, dbg)
        router(nc, block, cx, H, D, dbg)
        moe(nc, block, cx, H, D, dbg)
        ln_final(nc, block, cx, H, D, dbg)
        output_transpose(nc, block, cx, H, D, dbg)

    return nc, dbg


def tap(block, cx, H, dbg, name, src_ap):
    if name not in dbg:
        return
    d = dbg[name]

    @block.gpsimd
    def _(gp):
        for k in ["pe", "act", "dve"]:
            cx.wait(gp, k)
        cx.dinc(gp.dma_start(out=d, in_=src_ap), "wl")
        cx.wait(gp, "wl")


def barrier_all(block, cx):
    for engname in ["tensor", "scalar", "vector", "gpsimd", "sync"]:
        @getattr(block, engname)
        def _(e):
            for k in ["pe", "act", "dve", "gp", "wl", "xl"]:
                cx.wait(e, k)


def wslot(H, slot):
    return H["wsl"].ap()[:, slot * 1024:(slot + 1) * 1024]


def _proj_in(cx, block, H, D, wname, o, slot, psum, rhs, sub=None, kchunks=8):
    @block.gpsimd
    def _(gp):
        rel = cx.mark.get(("slot", slot))
        if rel is not None:
            cx.wait(gp, "pe", rel)
        src = D[wname] if sub is None else D[wname][sub]
        ap = src.rearrange("(cc p) o -> p cc o", p=128)[
            :, 0:kchunks, o * 128:(o + 1) * 128]
        dst = wslot(H, slot).rearrange("p (cc o) -> p cc o", cc=8)[:, 0:kchunks, :]
        cx.dinc(gp.dma_start(out=dst, in_=ap), "wl")
    wl_cnt = cx.cnt["wl"]

    @block.tensor
    def _(tensor):
        cx.wait(tensor, "wl", wl_cnt)
        cx.wait(tensor, "act")
        cx.wait(tensor, "dve")
        w = wslot(H, slot)
        for cc in range(kchunks):
            for n in range(4):
                ins = tensor.matmul(psum[:, n * 512:(n + 1) * 512],
                                    w[:, cc * 128:(cc + 1) * 128],
                                    rhs(cc, n),
                                    start=(cc == 0), stop=(cc == kchunks - 1))
        cx.mark[("slot", slot)] = cx.inc(ins, "pe")


def ph0(nc, block, cx, H, D, dbg):
    rA, xb, psA, psB = H["rA"], H["xb"], H["psA"], H["psB"]
    stage = _ar(H, 12, (128, 2048), F32)

    @block.sync
    def _(sync):
        for nm, dst in [("c_ident", H["s_id"]), ("c_ones_row", H["s_ones"]),
                        ("c_R1", H["s_R1"]), ("c_R2", H["s_R2"]),
                        ("c_ones_col_bf", H["s_onesb"]), ("r_bias", H["s_rb"]),
                        ("c_ones_col_f", H["s_onesf"])]:
            cx.dinc(sync.dma_start(out=dst[:], in_=D[nm]), "xl")
        msk = _ar(H, 28, (128, 2048), BF16)
        for j in range(4):
            cx.dinc(sync.dma_start(out=msk[:, j * 512:(j + 1) * 512],
                                   in_=D[f"c_mask{j}"]), "xl")
        cx.dinc(sync.dma_start(out=H["s_bias"][:, 0:24], in_=D["cab"]), "xl")
        cx.dinc(sync.dma_start(out=H["s_bias"][:, 24:32], in_=D["cpb"]), "xl")
        cx.dinc(sync.dma_start(out=H["s_bias"][:, 32:40], in_=D["ln1w"]), "xl")
        cx.dinc(sync.dma_start(out=H["s_bias"][:, 40:48], in_=D["ln2w"]), "xl")
        for i, nm in enumerate(["q2l_b4", "kv2l_b4", "wqk_b4", "l2o_b2"]):
            cx.dinc(sync.dma_start(out=H["s_sm"][:, i:i + 1], in_=D[nm]), "xl")
        cx.dinc(sync.dma_start(out=H["s_brow"][:], in_=D["kv2l_brow"]), "xl")
        tabs = _ar(H, 0, (128, 2 * T), BF16)
        for i, nm in enumerate(["tab1_c", "tab1_s"]):
            cx.dinc(sync.dma_start(out=tabs[:, i * T:(i + 1) * T],
                                   in_=D["c_" + nm]), "xl")
        cx.dinc(sync.dma_start(out=stage[:, 0:1024], in_=D["x"][0:128, :]), "xl")
    xl_base = cx.cnt["xl"]

    @block.gpsimd
    def _(gp):
        cx.dinc(gp.dma_start(out=H["s_W4q"][:].rearrange("p (c m) -> p c m", c=2),
                             in_=D["W4q"].rearrange("(c p) m -> p c m", p=128)),
                "wl")
        cx.dinc(gp.dma_start(out=H["s_W4kv"][:].rearrange("p (c m) -> p c m", c=2),
                             in_=D["W4kv"].rearrange("(c p) m -> p c m", p=128)),
                "wl")
        cx.dinc(gp.dma_start(out=H["s_wqk4"][:], in_=D["wqk4"]), "wl")
        cx.dinc(gp.dma_start(out=H["s_l2o4"][:], in_=D["l2o4"]), "wl")
        cx.dinc(gp.dma_start(out=H["s_Wr"][:].rearrange("p (c m) -> p c m", c=8),
                             in_=D["Wr"].rearrange("(c p) m -> p c m", p=128)),
                "wl")

    @block.tensor
    def _(tensor):
        cx.wait(tensor, "xl")
        cx.inc(tensor.matmul(psB[:, 0:132], H["s_ones"][:], H["s_brow"][:],
                             start=True, stop=True), "pe")

    @block.vector
    def _(v):
        cx.wait(v, "pe")
        cx.inc(v.tensor_copy(H["s_bv4"][:], psB[:, 0:132]), "dve")

    tr_done, ev_done = {}, {}
    for tt in range(NTT):
        buf = tt % 2

        @block.sync
        def _(sync, tt=tt, buf=buf):
            if tt + 1 < NTT:
                if tt >= 1:
                    cx.wait(sync, "pe", tr_done[tt - 1])
                cx.dinc(sync.dma_start(
                    out=stage[:, (1 - buf) * 1024:(2 - buf) * 1024],
                    in_=D["x"][(tt + 1) * 128:(tt + 2) * 128, :]), "xl")

        @block.tensor
        def _(tensor, tt=tt, buf=buf):
            cx.wait(tensor, "xl", xl_base + tt)
            cx.wait(tensor, "dve")
            if tt > 0:
                cx.wait(tensor, "act", ev_done[tt - 1])
            for c in range(NCH):
                half = psA if c < 4 else psB
                ins = tensor.transpose(
                    half[:, (c % 4) * 512:(c % 4) * 512 + 128],
                    stage[:, buf * 1024 + c * 128: buf * 1024 + (c + 1) * 128],
                    H["s_id"][:])
            tr_done[tt] = cx.inc(ins, "pe")

        @block.scalar
        def _(scalar, tt=tt):
            cx.wait(scalar, "pe", tr_done[tt])
            for half, c0 in [(psA, 0), (psB, 4)]:
                view = half[:, 0:2048].rearrange("p (c n) -> p c n", c=4)[
                    :, :, 0:128]
                dst = rA.ap().rearrange("p (c t) -> p c t", c=8)[
                    :, c0:c0 + 4, tt * 128:(tt + 1) * 128]
                ins = scalar.copy(dst, view)
            ev_done[tt] = cx.inc(ins, "act")

        @block.vector
        def _(v, tt=tt):
            cx.wait(v, "act", ev_done[tt])
            dst = xb.ap().rearrange("p (c t) -> p c t", c=8)[
                :, :, tt * 128:(tt + 1) * 128]
            src = rA.ap().rearrange("p (c t) -> p c t", c=8)[
                :, :, tt * 128:(tt + 1) * 128]
            cx.inc(v.tensor_copy(dst, src), "dve")

    barrier_all(block, cx)
    tap(block, cx, H, dbg, "xT0", rA.ap()[:, 0:2048])


def _stabilizer(block, cx, H, a_src, b_src, coef, dst_col):
    psA = H["psA"]

    @block.vector
    def _(v):
        cx.wait(v, "act")
        cx.wait(v, "dve")
        v.tensor_reduce(H["s_sm"][:, 6:7], a_src, AX.X, ALU.max,
                        apply_absolute_value=True)
        cx.inc(v.tensor_reduce(H["s_sm"][:, 7:8], b_src, AX.X, ALU.max,
                               apply_absolute_value=True), "dve")

    @block.tensor
    def _(tensor):
        cx.wait(tensor, "dve")
        cx.inc(tensor.transpose(psA[0:2, 0:128], H["s_sm"][:, 6:8],
                                H["s_id"][:]), "pe")

    @block.vector
    def _(v):
        cx.wait(v, "pe")
        cx.inc(v.tensor_reduce(H["s_m2"][:, 0:1], psA[0:2, 0:128],
                               AX.X, ALU.max), "dve")

    @block.scalar
    def _(scalar):
        cx.wait(scalar, "dve")
        cx.inc(scalar.activation(H["s_m2"][0:2, 1:2], H["s_m2"][0:2, 0:1],
                                 AF.Ln), "act")

    @block.tensor
    def _(tensor):
        cx.wait(tensor, "act")
        cx.inc(tensor.matmul(psA[0:1, 131:132], H["s_onesf"][0:2, 0:1],
                             H["s_m2"][0:2, 1:2], start=True, stop=True), "pe")

    @block.scalar
    def _(scalar):
        cx.wait(scalar, "pe")
        cx.inc(scalar.activation(H["s_m2"][0:1, 2:3], psA[0:1, 131:132],
                                 AF.Exp), "act")

    @block.vector
    def _(v):
        cx.wait(v, "act")
        cx.inc(v.tensor_scalar(H["s_m2"][0:1, 3:4], H["s_m2"][0:1, 2:3],
                               -coef, None, ALU.mult), "dve")

    @block.tensor
    def _(tensor):
        cx.wait(tensor, "dve")
        cx.inc(tensor.matmul(psA[:, 130:131], H["s_ones"][:],
                             H["s_m2"][0:1, 3:4], start=True, stop=True), "pe")

    @block.vector
    def _(v):
        cx.wait(v, "pe")
        cx.inc(v.tensor_copy(H["s_sm"][:, dst_col:dst_col + 1],
                             psA[:, 130:131]), "dve")


def attn1(nc, block, cx, H, D, dbg):
    rA, xb, yT, psA, psB = H["rA"], H["xb"], H["yT"], H["psA"], H["psB"]
    tab_c = _ar(H, 0, (128, T), BF16)
    tab_s = _ar(H, 4, (128, T), BF16)
    scr = _ar(H, 8, (128, 2048), BF16)
    qlat = _ar(H, 12, (128, 2048), F32)
    vb2 = _ar(H, 12, (128, 2048), BF16)
    tmpf = _ar(H, 20, (128, 2048), F32)
    msk = _ar(H, 28, (128, 2048), BF16)
    ylab = _ar(H, 32, (128, 2048), BF16)
    lnrow = _ar(H, 40, (1, 512), F32)
    recr = _ar(H, 42, (1, 512), F32)
    recb = _ar(H, 44, (64, 512), F32)
    qfk = _p2(H, 0, (128, 2048), BF16)
    k4 = _p2(H, 4096, (128, 2048), BF16)
    v4 = _p2(H, 8192, (128, 2112), BF16)
    pt = _p2(H, 12416, (128, 2048), BF16)

    def xbr(cc, n):
        return xb.ap()[:, cc * T + n * 512: cc * T + (n + 1) * 512]

    for g in range(4):
        for path, w4, bias_i in [("q", H["s_W4q"], 0), ("k", H["s_W4kv"], 1)]:
            sec = 0 if path == "q" else 8
            for sub in range(2):
                o = sec + 2 * g + sub
                _proj_in(cx, block, H, D, "caw", o, (2 * g + sub) % 4, psA, xbr)

                @block.scalar
                def _(scalar, o=o):
                    cx.wait(scalar, "pe")
                    cx.inc(scalar.activation(scr[:], psA[:, 0:2048], AF.Identity,
                                             bias=H["s_bias"][:, o:o + 1]),
                           "act")

                @block.tensor
                def _(tensor, w4=w4, sub=sub):
                    cx.wait(tensor, "act")
                    for n in range(4):
                        ins = tensor.matmul(psB[:, n * 512:(n + 1) * 512],
                                            w4[:, sub * 128:(sub + 1) * 128],
                                            scr[:, n * 512:(n + 1) * 512],
                                            start=(sub == 0), stop=(sub == 1))
                    cx.inc(ins, "pe")

            @block.scalar
            def _(scalar, bias_i=bias_i):
                cx.wait(scalar, "pe")
                cx.inc(scalar.activation(qlat, psB[:, 0:2048], AF.Identity,
                                         bias=H["s_sm"][:, bias_i:bias_i + 1]),
                       "act")

            @block.vector
            def _(v):
                cx.wait(v, "act")
                cx.inc(v.tensor_copy(scr[:], qlat), "dve")

            @block.tensor
            def _(tensor):
                cx.wait(tensor, "dve")
                for n in range(4):
                    ins = tensor.matmul(psA[:, n * 512:(n + 1) * 512],
                                        H["s_R1"][:],
                                        scr[:, n * 512:(n + 1) * 512],
                                        start=True, stop=True)
                cx.inc(ins, "pe")

            @block.gpsimd
            def _(gp):
                cx.wait(gp, "act")
                cx.inc(gp.tensor_tensor(qlat, qlat, tab_c, ALU.mult), "gp")

            @block.vector
            def _(v, path=path):
                cx.wait(v, "pe")
                v.tensor_tensor(tmpf, psA[:, 0:2048], tab_s, ALU.mult)
                cx.wait(v, "gp")
                dest = scr[:] if path == "q" else k4
                cx.inc(v.tensor_tensor(dest, tmpf, qlat, ALU.add), "dve")

            if path == "q":
                @block.tensor
                def _(tensor):
                    cx.wait(tensor, "dve")
                    for n in range(4):
                        ins = tensor.matmul(psB[:, n * 512:(n + 1) * 512],
                                            H["s_wqk4"][:],
                                            scr[:, n * 512:(n + 1) * 512],
                                            start=True, stop=True)
                    cx.inc(ins, "pe")

                @block.scalar
                def _(scalar):
                    cx.wait(scalar, "pe")
                    cx.inc(scalar.activation(qfk, psB[:, 0:2048], AF.Identity,
                                             bias=H["s_sm"][:, 2:3]), "act")

        for sub in range(2):
            o = 16 + 2 * g + sub
            _proj_in(cx, block, H, D, "caw", o, (2 * g + sub) % 4, psA, xbr)

            @block.scalar
            def _(scalar, sub=sub, o=o):
                cx.wait(scalar, "pe")
                dst = scr[:] if sub == 0 else vb2[:]
                cx.inc(scalar.activation(dst, psA[:, 0:2048], AF.Identity,
                                         bias=H["s_bias"][:, o:o + 1]), "act")

        @block.tensor
        def _(tensor):
            cx.wait(tensor, "act")
            cx.wait(tensor, "dve")
            for tt in range(NTT):
                for s2, buf in [(0, scr[:]), (1, vb2[:])]:
                    ins = tensor.matmul(
                        psB[:, tt * 128:(tt + 1) * 128],
                        buf[:, tt * 128:(tt + 1) * 128],
                        H["s_W4kv"][:, s2 * 128:(s2 + 1) * 128],
                        start=(s2 == 0), stop=(s2 == 1))
            cx.inc(ins, "pe")

        @block.vector
        def _(v):
            v.memset(v4.rearrange("p (tt h l) -> p tt h l", tt=16, h=4)[
                :, :, :, 32:33], 1.0)
            cx.wait(v, "pe")
            src = psB[:, 0:2048].rearrange("p (tt h l) -> p tt h l", tt=16, h=4)
            dst = v4.rearrange("p (tt h l) -> p tt h l", tt=16, h=4)[
                :, :, :, 0:32]
            b = H["s_bv4"].ap().rearrange("p (h l) -> p h l", h=4)[
                :, :, 0:32].unsqueeze(1).broadcast_to([128, 16, 4, 32])
            cx.inc(v.tensor_tensor(dst, src, b, ALU.add), "dve")

        _stabilizer(block, cx, H, qfk, k4, 1.02 * math.sqrt(LAT), 4)
        negM = H["s_sm"][:, 4:5]

        for qs in range(NQS):
            kmax = qs * 4 + 3
            for kt in range(kmax + 1):
                @block.tensor
                def _(tensor, qs=qs, kt=kt, g=g):
                    if kt == 0 and qs == 0:
                        cx.wait(tensor, "dve")
                        cx.wait(tensor, "act")
                    m = cx.mark.get(("exp", g))
                    if m is not None:
                        cx.wait(tensor, "act", m)
                    m = cx.mark.get(("rec", g))
                    if m is not None:
                        cx.wait(tensor, "dve", m)
                    for h in range(4):
                        ins = tensor.matmul(
                            psA[:, h * 512:(h + 1) * 512],
                            k4[32 * h:32 * h + 32, kt * 128:(kt + 1) * 128],
                            qfk[32 * h:32 * h + 32, qs * 512:(qs + 1) * 512],
                            start=True, stop=True, tile_position=(32 * h, 0))
                    cx.mark[("S", g)] = cx.inc(ins, "pe")

                @block.scalar
                def _(scalar, g=g):
                    cx.wait(scalar, "pe", cx.mark[("S", g)])
                    m = cx.mark.get(("pv", g))
                    if m is not None:
                        cx.wait(scalar, "pe", m)
                    m = cx.mark.get(("msk", g))
                    if m is not None:
                        cx.wait(scalar, "dve", m)
                    for h in range(4):
                        ins = scalar.activation(pt[:, h * 512:(h + 1) * 512],
                                                psA[:, h * 512:(h + 1) * 512],
                                                AF.Exp, bias=negM,
                                                scale=EXP_SC1)
                    cx.mark[("exp", g)] = cx.inc(ins, "act")

                diag = kt - qs * 4
                if diag >= 0:
                    @block.vector
                    def _(v, diag=diag, g=g):
                        cx.wait(v, "act", cx.mark[("exp", g)])
                        for h in range(4):
                            ins = v.tensor_tensor(
                                pt[:, h * 512:(h + 1) * 512],
                                pt[:, h * 512:(h + 1) * 512],
                                msk[:, diag * 512:(diag + 1) * 512], ALU.mult)
                        cx.mark[("msk", g)] = cx.inc(ins, "dve")

                @block.tensor
                def _(tensor, qs=qs, kt=kt, diag=diag, g=g, kmax=kmax):
                    if diag >= 0:
                        cx.wait(tensor, "dve", cx.mark[("msk", g)])
                    else:
                        cx.wait(tensor, "act", cx.mark[("exp", g)])
                    if kt == 0:
                        m = cx.mark.get(("yl", g))
                        if m is not None:
                            cx.wait(tensor, "dve", m)
                    for h in range(4):
                        ins = tensor.matmul(
                            psB[0:33, h * 512:(h + 1) * 512],
                            v4[:, kt * 132 + h * 33: kt * 132 + h * 33 + 33],
                            pt[:, h * 512:(h + 1) * 512],
                            start=(kt == 0), stop=(kt == kmax))
                    cx.mark[("pv", g)] = cx.inc(ins, "pe")

            for h in range(4):
                @block.scalar
                def _(scalar, h=h, g=g):
                    cx.wait(scalar, "pe")
                    scalar.activation(lnrow, psB[32:33, h * 512:(h + 1) * 512],
                                      AF.Ln)
                    cx.inc(scalar.activation(recr, lnrow, AF.Exp, scale=-1.0),
                           "act")

                @block.tensor
                def _(tensor, h=h):
                    cx.wait(tensor, "act")
                    cx.inc(tensor.matmul(psA[64:96, h * 512:(h + 1) * 512],
                                         H["s_ones"][:, 0:32], recr,
                                         start=True, stop=True), "pe")

                @block.vector
                def _(v, h=h, qs=qs, g=g):
                    cx.wait(v, "pe")
                    cx.chain(v, v.tensor_copy(recb[0:32, :],
                                              psA[64:96, h * 512:(h + 1) * 512]))
                    ins = v.tensor_tensor(
                        ylab[32 * h:32 * h + 32, qs * 512:(qs + 1) * 512],
                        psB[0:32, h * 512:(h + 1) * 512],
                        recb[0:32, :], ALU.mult)
                    m = cx.inc(ins, "dve")
                    cx.mark[("yl", g)] = m
                    cx.mark[("rec", g)] = m

        for qs in range(NQS):
            @block.tensor
            def _(tensor, qs=qs, g=g):
                cx.wait(tensor, "dve", cx.mark[("yl", g)])
                if qs > 0:
                    cx.wait(tensor, "act", cx.mark[("yT", g)])
                for h in range(4):
                    ins = tensor.matmul(
                        psA[0:64, h * 512:(h + 1) * 512],
                        H["s_l2o4"][32 * h:32 * h + 32, :],
                        ylab[32 * h:32 * h + 32, qs * 512:(qs + 1) * 512],
                        start=True, stop=True, tile_position=(32 * h, 0))
                cx.mark[("l2o", g)] = cx.inc(ins, "pe")

            @block.scalar
            def _(scalar, qs=qs, g=g):
                cx.wait(scalar, "pe", cx.mark[("l2o", g)])
                for h in range(4):
                    hg = 4 * g + h
                    chunk = hg // 2
                    r0 = 64 * (hg % 2)
                    ins = scalar.activation(
                        yT.ap()[r0:r0 + 64,
                                chunk * T + qs * 512: chunk * T + (qs + 1) * 512],
                        psA[0:64, h * 512:(h + 1) * 512], AF.Identity,
                        bias=H["s_sm"][r0:r0 + 64, 3:4])
                cx.mark[("yT", g)] = cx.inc(ins, "act")

    def ytr(cc, n):
        return yT.ap()[:, cc * T + n * 512: cc * T + (n + 1) * 512]

    for o in range(NCH):
        ps = psA if o % 2 == 0 else psB
        _proj_in(cx, block, H, D, "cpw", o, o % 4, ps, ytr)

        @block.scalar
        def _(scalar, o=o, ps=ps):
            cx.wait(scalar, "pe")
            cx.wait(scalar, "dve")
            cx.inc(scalar.activation(tmpf, ps[:, 0:2048], AF.Identity,
                                     bias=H["s_bias"][:, 24 + o:25 + o]), "act")

        @block.vector
        def _(v, o=o):
            cx.wait(v, "act")
            cx.inc(v.tensor_tensor(rA.ap()[:, o * T:(o + 1) * T],
                                   rA.ap()[:, o * T:(o + 1) * T],
                                   tmpf, ALU.add), "dve")

    barrier_all(block, cx)
    tap(block, cx, H, dbg, "a1_sm", H["s_sm"].ap()[:, 0:8])
    tap(block, cx, H, dbg, "a1_m2", H["s_m2"].ap()[0:2, 0:8])
    tap(block, cx, H, dbg, "a1_neg", H["s_sm"].ap()[:, 4:5])
    tap(block, cx, H, dbg, "a1_yl", ylab)
    tap(block, cx, H, dbg, "a1_pt", pt)
    tap(block, cx, H, dbg, "a1_v4", v4)
    tap(block, cx, H, dbg, "a1_qfk", qfk)
    tap(block, cx, H, dbg, "a1_k4", k4)
    tap(block, cx, H, dbg, "a1_yT0", yT.ap()[:, 0:2048])
    tap(block, cx, H, dbg, "x1pre0", rA.ap()[:, 0:2048])


def _stats_rows_bcast(block, cx, H, want_mean, eps, src_bf):
    psA, psB = H["psA"], H["psB"]
    sq = _ar(H, 8, (128, 2048), BF16)
    meanb = _ar(H, 12, (128, 2048), F32)
    rstdb = _ar(H, 20, (128, 2048), F32)
    tmprow = _ar(H, 0, (1, 2048), F32)

    for c in range(NCH):
        @block.scalar
        def _(scalar, c=c):
            if c == 0:
                cx.wait(scalar, "dve")
            cx.wait(scalar, "pe")
            cx.inc(scalar.activation(sq[:], src_bf(c), AF.Square), "act")

        @block.tensor
        def _(tensor, c=c):
            cx.wait(tensor, "act")
            for n in range(4):
                ins = tensor.matmul(psB[0:1, n * 512:(n + 1) * 512],
                                    H["s_onesb"][:],
                                    sq[:, n * 512:(n + 1) * 512],
                                    start=(c == 0), stop=(c == NCH - 1))
            if want_mean:
                for n in range(4):
                    ins = tensor.matmul(psA[0:1, n * 512:(n + 1) * 512],
                                        H["s_onesb"][:],
                                        src_bf(c)[:, n * 512:(n + 1) * 512],
                                        start=(c == 0), stop=(c == NCH - 1))
            cx.inc(ins, "pe")

    @block.vector
    def _(v):
        cx.wait(v, "pe")
        if want_mean:
            cx.chain(v, v.tensor_scalar(meanb[0:1, :], psA[0:1, 0:2048],
                                        1.0 / C, None, ALU.mult))
            cx.chain(v, v.tensor_scalar(rstdb[0:1, :], psB[0:1, 0:2048],
                                        1.0 / C, None, ALU.mult))
            cx.chain(v, v.tensor_tensor(tmprow, meanb[0:1, :], meanb[0:1, :],
                                        ALU.mult))
            cx.chain(v, v.tensor_tensor(rstdb[0:1, :], rstdb[0:1, :], tmprow,
                                        ALU.subtract))
        else:
            cx.chain(v, v.tensor_scalar(rstdb[0:1, :], psB[0:1, 0:2048],
                                        1.0 / C, None, ALU.mult))
        cx.inc(v.tensor_scalar(rstdb[0:1, :], rstdb[0:1, :], eps, None,
                               ALU.add), "dve")

    @block.scalar
    def _(scalar):
        cx.wait(scalar, "dve")
        cx.chain(scalar, scalar.activation(rstdb[0:1, :], rstdb[0:1, :],
                                           AF.Ln), "act")
        cx.inc(scalar.activation(rstdb[0:1, :], rstdb[0:1, :], AF.Exp,
                                 scale=-0.5), "act")

    @block.tensor
    def _(tensor):
        cx.wait(tensor, "act")
        cx.wait(tensor, "dve")
        for n in range(4):
            ins = tensor.matmul(psB[:, n * 512:(n + 1) * 512], H["s_ones"][:],
                                rstdb[0:1, n * 512:(n + 1) * 512],
                                start=True, stop=True)
            if want_mean:
                ins = tensor.matmul(psA[:, n * 512:(n + 1) * 512],
                                    H["s_ones"][:],
                                    meanb[0:1, n * 512:(n + 1) * 512],
                                    start=True, stop=True)
        cx.inc(ins, "pe")

    @block.scalar
    def _(scalar):
        cx.wait(scalar, "pe")
        if want_mean:
            scalar.copy(meanb, psA[:, 0:2048])
        cx.inc(scalar.copy(rstdb, psB[:, 0:2048]), "act")

    return meanb, rstdb


def ln1_rms1(nc, block, cx, H, D, dbg):
    rA, xb = H["rA"], H["xb"]
    tmpf = _ar(H, 0, (128, 2048), F32)

    def mk_src(c):
        return xb.ap()[:, c * T:(c + 1) * T]

    for c in range(NCH):
        @block.vector
        def _(v, c=c):
            if c == 0:
                cx.wait(v, "pe")
                cx.wait(v, "act")
            cx.inc(v.tensor_copy(mk_src(c), rA.ap()[:, c * T:(c + 1) * T]),
                   "dve")
    meanb, rstdb = _stats_rows_bcast(block, cx, H, True, 1e-5, mk_src)

    for c in range(NCH):
        @block.vector
        def _(v, c=c):
            cx.wait(v, "act")
            v.tensor_tensor(tmpf, rA.ap()[:, c * T:(c + 1) * T], meanb,
                            ALU.subtract)
            cx.inc(v.tensor_tensor(rA.ap()[:, c * T:(c + 1) * T], tmpf, rstdb,
                                   ALU.mult), "dve")
    barrier_all(block, cx)
    tap(block, cx, H, dbg, "x1_0", rA.ap()[:, 0:2048])

    for c in range(NCH):
        @block.vector
        def _(v, c=c):
            cx.inc(v.tensor_copy(xb.ap()[:, c * T:(c + 1) * T],
                                 rA.ap()[:, c * T:(c + 1) * T]), "dve")
    _, rstdb = _stats_rows_bcast(block, cx, H, False, 1e-6, mk_src)

    for c in range(NCH):
        @block.vector
        def _(v, c=c):
            cx.wait(v, "act")
            v.tensor_tensor(tmpf, rA.ap()[:, c * T:(c + 1) * T], rstdb,
                            ALU.mult)
            cx.inc(v.tensor_scalar(xb.ap()[:, c * T:(c + 1) * T], tmpf,
                                   H["s_bias"][:, 32 + c:33 + c], None,
                                   ALU.mult), "dve")
    barrier_all(block, cx)
    tap(block, cx, H, dbg, "xn1_0", xb.ap()[:, 0:2048])


def attn2(nc, block, cx, H, D, dbg):
    rA, xb, yT, psA, psB = H["rA"], H["xb"], H["yT"], H["psA"], H["psB"]
    tab_c = _ar(H, 0, (128, T), BF16)
    tab_s = _ar(H, 4, (128, T), BF16)
    scr = _ar(H, 8, (128, 2048), BF16)
    qlat = _ar(H, 12, (128, 2048), F32)
    tmpf = _ar(H, 20, (128, 2048), F32)
    msk = _ar(H, 28, (128, 2048), BF16)
    skv = _ar(H, 32, (128, 2 * T), BF16)
    lnrow = _ar(H, 40, (1, 512), F32)
    recr = _ar(H, 42, (1, 512), F32)
    recb = _ar(H, 44, (64, 512), F32)
    q2 = _p2(H, 0, (128, 2048), BF16)
    k2 = _p2(H, 4096, (128, 2048), BF16)
    v2 = _p2(H, 8192, (128, 2112), BF16)
    pt = _p2(H, 12416, (128, 2048), BF16)

    def xbr(cc, n):
        return xb.ap()[:, cc * T + n * 512: cc * T + (n + 1) * 512]

    @block.sync
    def _(sync):
        for k in ["pe", "act", "dve", "gp"]:
            cx.wait(sync, k)
        cx.dinc(sync.dma_start(out=tab_c, in_=D["c_tab2_c"]), "xl")
        cx.dinc(sync.dma_start(out=tab_s, in_=D["c_tab2_s"]), "xl")
    tab_ld = cx.cnt["xl"]

    for o2 in range(2):
        ps = psA if o2 == 0 else psB
        _proj_in(cx, block, H, D, "Wkv", o2, o2, ps, xbr)

        @block.scalar
        def _(scalar, o2=o2, ps=ps):
            cx.wait(scalar, "pe")
            cx.inc(scalar.activation(skv[:, o2 * T:(o2 + 1) * T],
                                     ps[:, 0:2048], AF.Copy, scale=0.25),
                   "act")

    for p in range(NE):
        @block.gpsimd
        def _(gp, p=p):
            for slot in (2, 3):
                rel = cx.mark.get(("slot", slot))
                if rel is not None:
                    cx.wait(gp, "pe", rel)
            d2 = wslot(H, 2).rearrange("p (c m) -> p c m", c=8)[:, 0:2, :]
            cx.dinc(gp.dma_start(out=d2, in_=D["Wk2"][p].rearrange(
                "(c p) m -> p c m", p=128)), "wl")
            d3 = wslot(H, 3).rearrange("p (c m) -> p c m", c=8)[:, 0:2, :]
            cx.dinc(gp.dma_start(out=d3, in_=D["Wv2"][p].rearrange(
                "(c p) m -> p c m", p=128)), "wl")
        kv_wl = cx.cnt["wl"]

        _proj_in(cx, block, H, D, "Wq", p, p % 2, psA, xbr)

        for path in ("q", "k"):
            if path == "q":
                @block.scalar
                def _(scalar):
                    cx.wait(scalar, "pe")
                    cx.wait(scalar, "xl", tab_ld)
                    cx.inc(scalar.copy(qlat, psA[:, 0:2048]), "act")
            else:
                @block.tensor
                def _(tensor):
                    cx.wait(tensor, "wl", kv_wl)
                    cx.wait(tensor, "act")
                    cx.wait(tensor, "dve")
                    w = wslot(H, 2)
                    for lc in range(2):
                        for n in range(4):
                            ins = tensor.matmul(
                                psA[:, n * 512:(n + 1) * 512],
                                w[:, lc * 128:(lc + 1) * 128],
                                skv[:, lc * T + n * 512: lc * T + (n + 1) * 512],
                                start=(lc == 0), stop=(lc == 1))
                    cx.mark[("slot", 2)] = cx.inc(ins, "pe")

                @block.scalar
                def _(scalar):
                    cx.wait(scalar, "pe")
                    cx.inc(scalar.copy(qlat, psA[:, 0:2048]), "act")

            @block.vector
            def _(v):
                cx.wait(v, "act")
                cx.inc(v.tensor_copy(scr[:], qlat), "dve")

            @block.tensor
            def _(tensor):
                cx.wait(tensor, "dve")
                for n in range(4):
                    ins = tensor.matmul(psB[:, n * 512:(n + 1) * 512],
                                        H["s_R2"][:],
                                        scr[:, n * 512:(n + 1) * 512],
                                        start=True, stop=True)
                cx.inc(ins, "pe")

            @block.gpsimd
            def _(gp):
                cx.wait(gp, "act")
                cx.wait(gp, "xl", tab_ld)
                cx.inc(gp.tensor_tensor(qlat, qlat, tab_c, ALU.mult), "gp")

            @block.vector
            def _(v, path=path):
                cx.wait(v, "pe")
                cx.wait(v, "xl", tab_ld)
                v.tensor_tensor(tmpf, psB[:, 0:2048], tab_s, ALU.mult)
                cx.wait(v, "gp")
                dest = q2 if path == "q" else k2
                cx.inc(v.tensor_tensor(dest, tmpf, qlat, ALU.add), "dve")

        @block.tensor
        def _(tensor):
            cx.wait(tensor, "wl", kv_wl)
            cx.wait(tensor, "dve")
            w = wslot(H, 3)
            for tt in range(NTT):
                for lc in range(2):
                    ins = tensor.matmul(
                        psB[:, tt * 128:(tt + 1) * 128],
                        skv[:, lc * T + tt * 128: lc * T + (tt + 1) * 128],
                        w[:, lc * 128:(lc + 1) * 128],
                        start=(lc == 0), stop=(lc == 1))
            cx.mark[("slot", 3)] = cx.inc(ins, "pe")

        @block.vector
        def _(v):
            v2v = v2[:, 0:2080].rearrange("p (tt h l) -> p tt h l", tt=16, h=2)
            v.memset(v2v[:, :, :, 64:65], 1.0)
            cx.wait(v, "pe")
            src = psB[:, 0:2048].rearrange("p (tt h l) -> p tt h l", tt=16, h=2)
            cx.inc(v.tensor_copy(v2v[:, :, :, 0:64], src), "dve")

        _stabilizer(block, cx, H, q2, k2, 1.02 * math.sqrt(HD), 5)
        negM = H["s_sm"][:, 5:6]

        for qs in range(NQS):
            kmax = qs * 4 + 3
            for kt in range(kmax + 1):
                @block.tensor
                def _(tensor, qs=qs, kt=kt, p=p):
                    if kt == 0 and qs == 0:
                        cx.wait(tensor, "dve")
                        cx.wait(tensor, "act")
                    m = cx.mark.get(("exp2", p))
                    if m is not None:
                        cx.wait(tensor, "act", m)
                    m = cx.mark.get(("rec2", p))
                    if m is not None:
                        cx.wait(tensor, "dve", m)
                    for a in range(2):
                        ins = tensor.matmul(
                            psA[:, a * 512:(a + 1) * 512],
                            k2[64 * a:64 * a + 64, kt * 128:(kt + 1) * 128],
                            q2[64 * a:64 * a + 64, qs * 512:(qs + 1) * 512],
                            start=True, stop=True, tile_position=(64 * a, 0))
                    cx.mark[("S2", p)] = cx.inc(ins, "pe")

                @block.scalar
                def _(scalar, p=p):
                    cx.wait(scalar, "pe", cx.mark[("S2", p)])
                    m = cx.mark.get(("pv2", p))
                    if m is not None:
                        cx.wait(scalar, "pe", m)
                    m = cx.mark.get(("msk2", p))
                    if m is not None:
                        cx.wait(scalar, "dve", m)
                    for a in range(2):
                        ins = scalar.activation(pt[:, a * 512:(a + 1) * 512],
                                                psA[:, a * 512:(a + 1) * 512],
                                                AF.Exp, bias=negM,
                                                scale=EXP_SC2)
                    cx.mark[("exp2", p)] = cx.inc(ins, "act")

                diag = kt - qs * 4
                if diag >= 0:
                    @block.vector
                    def _(v, diag=diag, p=p):
                        cx.wait(v, "act", cx.mark[("exp2", p)])
                        for a in range(2):
                            ins = v.tensor_tensor(
                                pt[:, a * 512:(a + 1) * 512],
                                pt[:, a * 512:(a + 1) * 512],
                                msk[:, diag * 512:(diag + 1) * 512], ALU.mult)
                        cx.mark[("msk2", p)] = cx.inc(ins, "dve")

                @block.tensor
                def _(tensor, qs=qs, kt=kt, diag=diag, p=p, kmax=kmax):
                    if diag >= 0:
                        cx.wait(tensor, "dve", cx.mark[("msk2", p)])
                    else:
                        cx.wait(tensor, "act", cx.mark[("exp2", p)])
                    if kt == 0:
                        m = cx.mark.get(("yl2", p))
                        if m is not None:
                            cx.wait(tensor, "dve", m)
                    for a in range(2):
                        ins = tensor.matmul(
                            psB[0:65, a * 512:(a + 1) * 512],
                            v2[:, kt * 130 + a * 65: kt * 130 + a * 65 + 65],
                            pt[:, a * 512:(a + 1) * 512],
                            start=(kt == 0), stop=(kt == kmax))
                    cx.mark[("pv2", p)] = cx.inc(ins, "pe")

            for a in range(2):
                @block.scalar
                def _(scalar, a=a, p=p):
                    cx.wait(scalar, "pe")
                    scalar.activation(lnrow, psB[64:65, a * 512:(a + 1) * 512],
                                      AF.Ln)
                    cx.inc(scalar.activation(recr, lnrow, AF.Exp, scale=-1.0),
                           "act")

                @block.tensor
                def _(tensor, a=a):
                    cx.wait(tensor, "act")
                    cx.inc(tensor.matmul(psA[64:128, a * 512:(a + 1) * 512],
                                         H["s_ones"][:, 0:64], recr,
                                         start=True, stop=True), "pe")

                @block.vector
                def _(v, a=a, qs=qs, p=p):
                    cx.wait(v, "pe")
                    cx.chain(v, v.tensor_copy(recb[:],
                                              psA[64:128, a * 512:(a + 1) * 512]))
                    ins = v.tensor_tensor(
                        yT.ap()[64 * a:64 * a + 64,
                                p * T + qs * 512: p * T + (qs + 1) * 512],
                        psB[0:64, a * 512:(a + 1) * 512], recb[:], ALU.mult)
                    m = cx.inc(ins, "dve")
                    cx.mark[("yl2", p)] = m
                    cx.mark[("rec2", p)] = m

    ych = _ar(H, 12, (128, 2048), F32)

    def ytr(cc, n):
        return yT.ap()[:, cc * T + n * 512: cc * T + (n + 1) * 512]

    for o in range(NCH):
        ps = psA if o % 2 == 0 else psB
        _proj_in(cx, block, H, D, "Wo", o, o % 4, ps, ytr)

        @block.scalar
        def _(scalar, ps=ps):
            cx.wait(scalar, "pe")
            cx.wait(scalar, "dve")
            cx.inc(scalar.copy(tmpf, ps[:, 0:2048]), "act")

        @block.vector
        def _(v, o=o):
            cx.wait(v, "act")
            v.tensor_tensor(ych, tmpf, rA.ap()[:, o * T:(o + 1) * T], ALU.add)
            v.tensor_copy(xb.ap()[:, o * T:(o + 1) * T], ych)
            cx.inc(v.tensor_tensor(rA.ap()[:, o * T:(o + 1) * T],
                                   rA.ap()[:, o * T:(o + 1) * T], ych,
                                   ALU.add), "dve")

    barrier_all(block, cx)
    tap(block, cx, H, dbg, "a2_kv", skv[:, 0:2048])
    tap(block, cx, H, dbg, "a2_q2", q2)
    tap(block, cx, H, dbg, "a2_k2", k2)
    tap(block, cx, H, dbg, "a2_v2", v2[:, 0:2080])
    tap(block, cx, H, dbg, "a2_neg", H["s_sm"].ap()[:, 5:6])
    tap(block, cx, H, dbg, "a2_yT0", yT.ap()[:, 0:2048])
    tap(block, cx, H, dbg, "yb0", xb.ap()[:, 0:2048])


def rms2(nc, block, cx, H, D, dbg):
    xb = H["xb"]
    tmpf = _ar(H, 0, (128, 2048), F32)

    def mk_src(c):
        return xb.ap()[:, c * T:(c + 1) * T]

    _, rstdb = _stats_rows_bcast(block, cx, H, False, 1e-6, mk_src)
    for c in range(NCH):
        @block.vector
        def _(v, c=c):
            cx.wait(v, "act")
            v.tensor_tensor(tmpf, xb.ap()[:, c * T:(c + 1) * T], rstdb,
                            ALU.mult)
            cx.inc(v.tensor_scalar(xb.ap()[:, c * T:(c + 1) * T], tmpf,
                                   H["s_bias"][:, 40 + c:41 + c], None,
                                   ALU.mult), "dve")
    barrier_all(block, cx)
    tap(block, cx, H, dbg, "yn0", xb.ap()[:, 0:2048])


def router(nc, block, cx, H, D, dbg):
    xb, psA, psB = H["xb"], H["psA"], H["psB"]
    lgt = _p2(H, 0, (8, 2048), F32)

    @block.tensor
    def _(tensor):
        cx.wait(tensor, "dve")
        for cc in range(NCH):
            for n in range(4):
                ins = tensor.matmul(psA[0:8, n * 512:(n + 1) * 512],
                                    H["s_Wr"][:, cc * 8:(cc + 1) * 8],
                                    xb.ap()[:, cc * T + n * 512:
                                            cc * T + (n + 1) * 512],
                                    start=(cc == 0), stop=(cc == NCH - 1))
        cx.inc(ins, "pe")

    @block.scalar
    def _(scalar):
        cx.wait(scalar, "pe")
        cx.inc(scalar.activation(lgt, psA[0:8, 0:2048], AF.Identity,
                                 bias=H["s_rb"][:]), "act")

    @block.vector
    def _(v):
        cx.wait(v, "act")
        cx.inc(v.tensor_scalar(lgt, lgt, 50.0, -50.0, ALU.min, ALU.max), "dve")

    lgt32 = _p2(H, 0, (32, 2048), F32)

    @block.tensor
    def _(tensor):
        cx.wait(tensor, "dve")
        for tt in range(NTT):
            ins = tensor.transpose(psB[:, tt * 32:(tt + 1) * 32],
                                   lgt32[:, tt * 128:(tt + 1) * 128],
                                   H["s_id"][0:32, 0:32])
        cx.inc(ins, "pe")

    @block.vector
    def _(v):
        cx.wait(v, "pe")
        v.tensor_copy(H["s_ltm"][:].rearrange("p (t e) -> p t e", e=8),
                      psB[:, 0:512].rearrange("p (t e) -> p t e", e=32)[
                          :, :, 0:8])
        lt3 = H["s_ltm"].ap().rearrange("p (t e) -> p t e", e=8)
        lx4 = H["s_lx"].ap()[:, 0:64].rearrange("p (t e) -> p t e", e=4)
        lx2 = H["s_lx"].ap()[:, 64:96].rearrange("p (t e) -> p t e", e=2)
        cx.chain(v, v.tensor_tensor(lx4, lt3[:, :, 0:4], lt3[:, :, 4:8],
                                    ALU.max))
        cx.chain(v, v.tensor_tensor(lx2, lx4[:, :, 0:2], lx4[:, :, 2:4],
                                    ALU.max))
        cx.chain(v, v.tensor_tensor(H["s_p16"].ap().unsqueeze(2),
                                    lx2[:, :, 0:1], lx2[:, :, 1:2], ALU.max))
        sel3 = H["s_sel"].ap().rearrange("p (t e) -> p t e", e=8)
        cx.chain(v, v.memset(H["s_r16"][:], 0.0))
        for e in range(NE):
            lt_e = lt3[:, :, e:e + 1].squeeze(2)
            cx.chain(v, v.tensor_tensor(H["s_t16"][:], lt_e, H["s_p16"][:],
                                        ALU.is_ge))
            cx.chain(v, v.tensor_scalar(H["s_u16"][:], H["s_r16"][:], -1.0,
                                        1.0, ALU.mult, ALU.add))
            cx.chain(v, v.tensor_tensor(sel3[:, :, e:e + 1].squeeze(2),
                                        H["s_t16"][:], H["s_u16"][:],
                                        ALU.mult))
            cx.chain(v, v.tensor_tensor(H["s_r16"][:], H["s_r16"][:],
                                        sel3[:, :, e:e + 1].squeeze(2),
                                        ALU.add))
        cx.inc(v.tensor_copy(H["s_sc16"][:], H["s_p16"][:]), "dve")

    @block.scalar
    def _(scalar):
        cx.wait(scalar, "dve")
        cx.inc(scalar.activation(H["s_t16"][:], H["s_p16"][:], AF.Exp,
                                 scale=-1.0), "act")

    @block.vector
    def _(v):
        cx.wait(v, "act")
        cx.chain(v, v.tensor_scalar(H["s_u16"][:], H["s_t16"][:], 1.0, None,
                                    ALU.add))
        cx.chain(v, v.reciprocal(H["s_p16"][:], H["s_u16"][:]))
        cx.chain(v, v.tensor_scalar(H["s_p16"][:], H["s_p16"][:], 1e-8,
                                    1.0 - 1e-8, ALU.max, ALU.min))
        cx.chain(v, v.tensor_scalar(H["s_t16"][:], H["s_p16"][:], 0.5 + 1e-8,
                                    None, ALU.add))
        cx.chain(v, v.tensor_scalar(H["s_u16"][:], H["s_t16"][:], 0.5, 2.0,
                                    ALU.max, ALU.min))
        cx.chain(v, v.reciprocal(H["s_a16"][:], H["s_u16"][:]))
        cx.chain(v, v.tensor_tensor(H["s_b16"][:], H["s_p16"][:],
                                    H["s_a16"][:], ALU.mult))
        cx.inc(v.tensor_scalar(H["s_a16"][:], H["s_a16"][:], 0.5, None,
                               ALU.mult), "dve")
    barrier_all(block, cx)
    tap(block, cx, H, dbg, "r_ltm", H["s_ltm"].ap()[:, 0:128])
    tap(block, cx, H, dbg, "r_eq", H["s_lx"].ap()[:, 0:128])
    tap(block, cx, H, dbg, "r_pmax", H["s_sc16"].ap()[:, 0:16])
    tap(block, cx, H, dbg, "r_lgt", _p2(H, 0, (8, 2048), F32))
    tap(block, cx, H, dbg, "sel", H["s_sel"].ap()[:, 0:128])
    tap(block, cx, H, dbg, "p16", H["s_p16"].ap()[:, 0:16])


def moe(nc, block, cx, H, D, dbg):
    rA, xb, psA, psB = H["rA"], H["xb"], H["psA"], H["psB"]
    s_h = _ar(H, 0, (128, 8 * T), BF16)
    srow = _ar(H, 32, (1, 2048), F32)
    scale_bc = _p2(H, 0, (128, 2048), F32)
    s_sil = _p2(H, 8320, (128, 2048), F32)

    def xbr(cc, n):
        return xb.ap()[:, cc * T + n * 512: cc * T + (n + 1) * 512]

    for e in range(-1, NE):
        w1n, w2n, w3n = ("sw1", "sw2", "sw3") if e < 0 else ("ew1", "ew2", "ew3")
        sub = None if e < 0 else e

        @block.vector
        def _(v, e=e):
            cx.wait(v, "gp")
            cx.wait(v, "act")
            cx.wait(v, "pe")
            if e < 0:
                cx.inc(v.tensor_copy(H["s_sc16"][:], H["s_a16"][:]), "dve")
            else:
                sel3 = H["s_sel"].ap().rearrange("p (t e) -> p t e", e=8)
                cx.chain(v, v.tensor_tensor(H["s_sc16"][:],
                                            sel3[:, :, e:e + 1].squeeze(2),
                                            H["s_b16"][:], ALU.mult))
                cx.inc(v.tensor_copy(H["s_t16"][:], H["s_sc16"][:]), "dve")

        @block.tensor
        def _(tensor):
            cx.wait(tensor, "dve")
            cx.inc(tensor.transpose(psA[0:16, 0:128], H["s_sc16"][:],
                                    H["s_id"][:]), "pe")

        @block.vector
        def _(v):
            cx.wait(v, "pe")
            cx.inc(v.tensor_copy(H["s_str16"][:], psA[0:16, 0:128]), "dve")

        @block.sync
        def _(sync):
            cx.wait(sync, "dve")
            cx.dinc(sync.dma_start(out=srow, in_=H["s_str16"][:]), "xl")
        row_ld = cx.cnt["xl"]

        @block.tensor
        def _(tensor):
            cx.wait(tensor, "xl", row_ld)
            for n in range(4):
                ins = tensor.matmul(psA[:, n * 512:(n + 1) * 512],
                                    H["s_ones"][:],
                                    srow[:, n * 512:(n + 1) * 512],
                                    start=True, stop=True)
            cx.inc(ins, "pe")

        @block.scalar
        def _(scalar):
            cx.wait(scalar, "pe")
            cx.inc(scalar.copy(scale_bc, psA[:, 0:2048]), "act")

        for ih in range(2):
            for i in range(8):
                ic = ih * 8 + i
                _proj_in(cx, block, H, D, w1n, ic, (2 * i) % 4, psA, xbr,
                         sub=sub)
                _proj_in(cx, block, H, D, w3n, ic, (2 * i + 1) % 4, psB, xbr,
                         sub=sub)

                @block.scalar
                def _(scalar):
                    cx.wait(scalar, "pe")
                    cx.wait(scalar, "gp")
                    cx.inc(scalar.activation(s_sil, psA[:, 0:2048], AF.Silu),
                           "act")

                @block.gpsimd
                def _(gp):
                    cx.wait(gp, "act")
                    cx.inc(gp.tensor_tensor(s_sil, s_sil, scale_bc, ALU.mult),
                           "gp")

                @block.vector
                def _(v, i=i):
                    cx.wait(v, "gp")
                    cx.wait(v, "pe")
                    cx.inc(v.tensor_tensor(s_h[:, i * T:(i + 1) * T], s_sil,
                                           psB[:, 0:2048], ALU.mult), "dve")

            for c in range(NCH):
                ps = psA if c % 2 == 0 else psB
                slot = c % 2

                @block.gpsimd
                def _(gp, c=c, ih=ih, slot=slot, w2n=w2n, sub=sub):
                    rel = cx.mark.get(("slot", slot))
                    if rel is not None:
                        cx.wait(gp, "pe", rel)
                    src = D[w2n] if sub is None else D[w2n][sub]
                    ap = src.rearrange("(cc p) o -> p cc o", p=128)[
                        :, ih * 8:(ih + 1) * 8, c * 128:(c + 1) * 128]
                    dst = wslot(H, slot).rearrange("p (cc o) -> p cc o", cc=8)
                    cx.dinc(gp.dma_start(out=dst, in_=ap), "wl")
                w2_wl = cx.cnt["wl"]

                @block.tensor
                def _(tensor, c=c, ps=ps, slot=slot, w2_wl=w2_wl):
                    cx.wait(tensor, "wl", w2_wl)
                    cx.wait(tensor, "dve")
                    w = wslot(H, slot)
                    for i in range(8):
                        for n in range(4):
                            ins = tensor.matmul(
                                ps[:, n * 512:(n + 1) * 512],
                                w[:, i * 128:(i + 1) * 128],
                                s_h[:, i * T + n * 512: i * T + (n + 1) * 512],
                                start=(i == 0), stop=(i == 7))
                    cx.mark[("slot", slot)] = cx.inc(ins, "pe")

                @block.vector
                def _(v, c=c, ps=ps):
                    cx.wait(v, "pe")
                    cx.inc(v.tensor_tensor(rA.ap()[:, c * T:(c + 1) * T],
                                           rA.ap()[:, c * T:(c + 1) * T],
                                           ps[:, 0:2048], ALU.add), "dve")

    barrier_all(block, cx)
    tap(block, cx, H, dbg, "F0", rA.ap()[:, 0:2048])


def ln_final(nc, block, cx, H, D, dbg):
    rA, xb = H["rA"], H["xb"]
    tmpf = _ar(H, 0, (128, 2048), F32)

    def mk_src(c):
        return xb.ap()[:, c * T:(c + 1) * T]

    for c in range(NCH):
        @block.vector
        def _(v, c=c):
            if c == 0:
                cx.wait(v, "pe")
                cx.wait(v, "act")
            cx.inc(v.tensor_copy(mk_src(c), rA.ap()[:, c * T:(c + 1) * T]),
                   "dve")
    meanb, rstdb = _stats_rows_bcast(block, cx, H, True, 1e-5, mk_src)

    for c in range(NCH):
        @block.vector
        def _(v, c=c):
            cx.wait(v, "act")
            v.tensor_tensor(tmpf, rA.ap()[:, c * T:(c + 1) * T], meanb,
                            ALU.subtract)
            cx.inc(v.tensor_tensor(rA.ap()[:, c * T:(c + 1) * T], tmpf, rstdb,
                                   ALU.mult), "dve")
    barrier_all(block, cx)


def output_transpose(nc, block, cx, H, D, dbg):
    rA, psA, psB = H["rA"], H["psA"], H["psB"]
    out = H["out"]
    stage = _ar(H, 12, (128, 2048), F32)
    st_done, ev_done, tr_done = {}, {}, {}
    for tt in range(NTT):
        buf = tt % 2

        @block.tensor
        def _(tensor, tt=tt):
            if tt >= 1:
                cx.wait(tensor, "act", ev_done[tt - 1])
            for c in range(NCH):
                half = psA if c < 4 else psB
                ins = tensor.transpose(
                    half[:, (c % 4) * 512:(c % 4) * 512 + 128],
                    rA.ap()[:, c * T + tt * 128: c * T + (tt + 1) * 128],
                    H["s_id"][:])
            tr_done[tt] = cx.inc(ins, "pe")

        @block.scalar
        def _(scalar, tt=tt, buf=buf):
            cx.wait(scalar, "pe", tr_done[tt])
            if tt >= 2:
                cx.wait(scalar, "st", st_done[tt - 2])
            for half, c0 in [(psA, 0), (psB, 4)]:
                view = half[:, 0:2048].rearrange("p (c n) -> p c n", c=4)[
                    :, :, 0:128]
                dst = stage.rearrange("p (b c n) -> p b c n", b=2, c=8)[
                    :, buf, c0:c0 + 4, :]
                ins = scalar.copy(dst, view)
            ev_done[tt] = cx.inc(ins, "act")

        @block.sync
        def _(sync, tt=tt, buf=buf):
            cx.wait(sync, "act", ev_done[tt])
            cx.dinc(sync.dma_start(out=out[tt * 128:(tt + 1) * 128, :],
                                   in_=stage[:, buf * 1024:(buf + 1) * 1024]),
                    "st")
            st_done[tt] = cx.cnt["st"]

    @block.sync
    def _(sync):
        cx.wait(sync, "st")


def kernel(**inputs):
    w = _prep(inputs)
    consts = _host_consts()
    x_full = np.asarray(inputs["L0_attn_x"])
    shared = dict(w)
    for k, v in consts.items():
        shared["c_" + k] = v

    nc, dbg = build_program(debug=[(n, _DBG_SHAPES[n]) for n in DEBUG])
    in_maps = []
    for b in range(8):
        m = dict(shared)
        m["x"] = np.ascontiguousarray(x_full[b])
        in_maps.append(m)
    res = run_bass_kernel_spmd(nc, in_maps, core_ids=list(range(8)))
    outs = np.stack([np.ascontiguousarray(r["out"]) for r in res.results],
                    axis=0)
    kernel._last = res
    return outs


_DBG_SHAPES = {
    "r_eq": (128, 128),
    "r_pmax": (128, 16),
    "r_ltm": (128, 128),
    "r_lgt": (8, 2048),
    "a2_kv": (128, 2048),
    "a2_q2": (128, 2048),
    "a2_k2": (128, 2048),
    "a2_v2": (128, 2080),
    "a2_neg": (128, 1),
    "a2_yT0": (128, 2048),
    "a1_sm": (128, 8),
    "a1_m2": (2, 8),
    "a1_neg": (128, 1),
    "a1_yl": (128, 2048),
    "a1_pt": (128, 2048),
    "a1_v4": (128, 2112),
    "xT0": (128, 2048),
    "a1_qfk": (128, 2048),
    "a1_k4": (128, 2048),
    "a1_yT0": (128, 2048),
    "x1pre0": (128, 2048),
    "x1_0": (128, 2048),
    "xn1_0": (128, 2048),
    "yb0": (128, 2048),
    "yn0": (128, 2048),
    "sel": (128, 128),
    "p16": (128, 16),
    "F0": (128, 2048),
}


# version 02
# speedup vs baseline: 1.0428x; 1.0428x over previous
import contextlib
import math

import numpy as np
import ml_dtypes

import concourse.bass as bass
import concourse.mybir as mybir
from concourse.bass_utils import run_bass_kernel_spmd

F32 = mybir.dt.float32
BF16 = mybir.dt.bfloat16
AF = mybir.ActivationFunctionType
ALU = mybir.AluOpType
AX = mybir.AxisListType

T = 2048
C = 1024
NCH = 8
NTT = 16
NQS = 4
LAT = 32
HD = 64
I_FF = 2048
NE = 8

EXP_SC1 = 1.0 / math.sqrt(LAT)
EXP_SC2 = 1.0 / math.sqrt(HD)

DEBUG = []
BF = ml_dtypes.bfloat16


def _host_consts():
    c = {}
    inv = 1.0 / 10000.0 ** (np.arange(0, LAT, 2, dtype=np.float64) / LAT)
    t = np.arange(T, dtype=np.float64)
    fr = np.outer(t, inv)
    cos1 = np.repeat(np.cos(fr), 2, axis=-1)[:, :LAT]
    sin1 = np.repeat(np.sin(fr), 2, axis=-1)[:, :LAT]
    c["tab1_c"] = np.tile(cos1.T.astype(np.float32), (4, 1)).astype(BF)
    c["tab1_s"] = np.tile(sin1.T.astype(np.float32), (4, 1)).astype(BF)
    R32 = np.zeros((LAT, LAT), np.float32)
    for i in range(LAT // 2):
        R32[2 * i + 1, 2 * i] = -1.0
        R32[2 * i, 2 * i + 1] = 1.0
    c["R1"] = np.kron(np.eye(4, dtype=np.float32), R32).astype(BF)
    inv2 = 1.0 / 100000.0 ** (np.arange(0, HD, 2, dtype=np.float64) / HD)
    fr2 = np.outer(t, inv2)
    C64 = np.concatenate([np.cos(fr2), np.cos(fr2)], axis=1).T.astype(np.float32)
    S64 = np.concatenate([np.sin(fr2), np.sin(fr2)], axis=1).T.astype(np.float32)
    c["tab2_c"] = np.tile(C64, (2, 1)).astype(BF)
    c["tab2_s"] = np.tile(S64, (2, 1)).astype(BF)
    R64 = np.zeros((HD, HD), np.float32)
    for d in range(32):
        R64[d + 32, d] = -1.0
        R64[d, d + 32] = 1.0
    c["R2"] = np.kron(np.eye(2, dtype=np.float32), R64).astype(BF)
    kk = np.arange(128)[:, None]
    qq = np.arange(512)[None, :]
    for j, d in enumerate([0, 128, 256, 384]):
        c[f"mask{j}"] = ((d + kk) <= qq).astype(BF)
    c["ident"] = np.eye(128, dtype=np.float32)
    c["ones_row"] = np.ones((1, 128), np.float32)
    c["ones_col_bf"] = np.ones((128, 1), np.float32).astype(BF)
    c["ones_col_f"] = np.ones((128, 1), np.float32)
    return c


def _prep(inp):
    w = {}
    f = np.ascontiguousarray
    w["caw"] = f(np.asarray(inp["L0_attn_c_attn_w"]).T)
    w["cab"] = f(np.asarray(inp["L0_attn_c_attn_b"]).reshape(24, 128).T)
    q2l = np.asarray(inp["L0_attn_q2l_w"])
    kv2l = np.asarray(inp["L0_attn_kv2l_w"])
    blk = np.zeros((256, 128), np.float32)
    for h in range(4):
        blk[h * 64:(h + 1) * 64, h * 32:(h + 1) * 32] = q2l.T
    w["W4q"] = f(blk)
    blk = np.zeros((256, 128), np.float32)
    for h in range(4):
        blk[h * 64:(h + 1) * 64, h * 32:(h + 1) * 32] = kv2l.T
    w["W4kv"] = f(blk)
    w["q2l_b4"] = f(np.tile(np.asarray(inp["L0_attn_q2l_b"]), 4).reshape(128, 1))
    w["kv2l_b4"] = f(np.tile(np.asarray(inp["L0_attn_kv2l_b"]), 4).reshape(128, 1))
    row = np.zeros((1, 132), np.float32)
    for h in range(4):
        row[0, h * 33:h * 33 + 32] = np.asarray(inp["L0_attn_kv2l_b"])
    w["kv2l_brow"] = row
    blkq = np.zeros((128, 128), np.float32)
    for h in range(4):
        blkq[h * 32:(h + 1) * 32, h * 32:(h + 1) * 32] = np.asarray(
            inp["L0_attn_wqk_w"]).T
    w["wqk4"] = f(blkq)
    w["wqk_b4"] = f(np.tile(np.asarray(inp["L0_attn_wqk_b"]), 4).reshape(128, 1))
    l2oT = np.asarray(inp["L0_attn_l2o_w"]).T
    w["l2o4"] = f(np.tile(l2oT, (4, 1)))
    w["l2o_b2"] = f(np.tile(np.asarray(inp["L0_attn_l2o_b"]), 2).reshape(128, 1))
    w["cpw"] = f(np.asarray(inp["L0_attn_cproj_w"]).T)
    w["cpb"] = f(np.asarray(inp["L0_attn_cproj_b"]).reshape(8, 128).T)
    w["ln1w"] = f(np.asarray(inp["L0_ffn_ln1_w"]).reshape(8, 128).T)
    w["ln2w"] = f(np.asarray(inp["L0_ffn_ln2_w"]).reshape(8, 128).T)
    w["Wq"] = f(np.asarray(inp["L0_ffn_Wq"]))
    w["Wkv"] = f(np.asarray(inp["L0_ffn_Wkv"]))
    w["Wk2"] = f(np.asarray(inp["L0_ffn_Wk_up"]).reshape(8, 2, 256, 64)
                 .transpose(0, 2, 1, 3).reshape(8, 256, 128))
    w["Wv2"] = f(np.asarray(inp["L0_ffn_Wv_up"]).reshape(8, 2, 256, 64)
                 .transpose(0, 2, 1, 3).reshape(8, 256, 128))
    w["Wo"] = f(np.asarray(inp["L0_ffn_Wo"]))
    w["Wr"] = f(np.asarray(inp["L0_ffn_Wr"]))
    w["r_bias"] = f(np.asarray(inp["L0_ffn_r_bias"]).reshape(8, 1))
    w["sw1"] = f(np.asarray(inp["L0_ffn_sw1"]))
    w["sw2"] = f(np.asarray(inp["L0_ffn_sw2"]))
    w["sw3"] = f(np.asarray(inp["L0_ffn_sw3"]))
    w["ew1"] = f(np.asarray(inp["L0_ffn_ew1"]))
    w["ew2"] = f(np.asarray(inp["L0_ffn_ew2"]))
    w["ew3"] = f(np.asarray(inp["L0_ffn_ew3"]))
    return w


class Ctx:
    def __init__(self):
        self.cnt = dict(pe=0, act=0, dve=0, gp=0, wl=0, xl=0, st=0)
        self.mark = {}

    def inc(self, ins, name):
        self.cnt[name] += 1
        ins.then_inc(self.sem[name], 1)
        return self.cnt[name]

    def dinc(self, ins, name):
        self.cnt[name] += 1
        ins.then_inc(self.sem[name], 16)
        return self.cnt[name]

    def chain(self, eng, ins, name="dve"):
        c = self.inc(ins, name)
        eng.wait_ge(self.sem[name], c)
        return c

    def wait(self, eng, name, val=None):
        v = self.cnt[name] if val is None else val
        if v <= 0:
            return
        if name in ("wl", "xl", "st"):
            eng.wait_ge(self.sem[name], 16 * v)
        else:
            eng.wait_ge(self.sem[name], v)


def _ar(H, off_kb, shape, dt=BF16, p0=0):
    e0 = int(off_kb * 1024) // 2
    n = shape[1] * (2 if dt == F32 else 1)
    v = H["AR"].ap()[:, e0:e0 + n]
    if dt == F32:
        v = v.bitcast(F32)
    return v[p0:p0 + shape[0], :]


def _p2(H, off_b, shape, dt=BF16, p0=0):
    e0 = off_b // 2
    n = shape[1] * (2 if dt == F32 else 1)
    v = H["P2"].ap()[:, e0:e0 + n]
    if dt == F32:
        v = v.bitcast(F32)
    return v[p0:p0 + shape[0], :]


def build_program(debug=()):
    nc = bass.Bass("TRN2", target_bir_lowering=False)
    cx = Ctx()
    consts = _host_consts()
    D = {}

    def din(name, shape, dt=F32):
        D[name] = nc.dram_tensor(name, list(shape), dt, kind="ExternalInput").ap()

    din("x", (T, C))
    din("caw", (C, 3 * C)); din("cab", (128, 24))
    din("W4q", (256, 128)); din("W4kv", (256, 128))
    din("q2l_b4", (128, 1)); din("kv2l_b4", (128, 1)); din("kv2l_brow", (1, 132))
    din("wqk4", (128, 128)); din("wqk_b4", (128, 1))
    din("l2o4", (128, 64)); din("l2o_b2", (128, 1))
    din("cpw", (C, C)); din("cpb", (128, 8))
    din("ln1w", (128, 8)); din("ln2w", (128, 8))
    din("Wq", (C, C)); din("Wkv", (C, 256))
    din("Wk2", (8, 256, 128)); din("Wv2", (8, 256, 128))
    din("Wo", (C, C)); din("Wr", (C, 8)); din("r_bias", (8, 1))
    din("sw1", (C, I_FF)); din("sw2", (I_FF, C)); din("sw3", (C, I_FF))
    din("ew1", (NE, C, I_FF)); din("ew2", (NE, I_FF, C)); din("ew3", (NE, C, I_FF))
    for k, v in consts.items():
        din("c_" + k, v.shape, BF16 if v.dtype == BF else F32)

    out = nc.dram_tensor("out", [T, C], F32, kind="ExternalOutput").ap()
    dbg = {}
    for name, shape in debug:
        dbg[name] = nc.dram_tensor("dbg_" + name, list(shape), F32,
                                   kind="ExternalOutput").ap()

    st = contextlib.ExitStack()
    with st:
        def sb(name, p, f, dt):
            return st.enter_context(nc.sbuf_tensor(name, [p, f], dt))

        rA = sb("rA", 128, NCH * T, F32)
        xb = sb("xb", 128, NCH * T, BF16)
        yT = sb("yT", 128, NCH * T, BF16)
        AR = sb("AR", 128, 23552, BF16)
        P2 = sb("P2", 128, 8320, BF16)
        wsl = sb("wsl", 128, 4096, BF16)
        s_W4q = sb("s_W4q", 128, 256, BF16)
        s_W4kv = sb("s_W4kv", 128, 256, BF16)
        s_wqk4 = sb("s_wqk4", 128, 128, BF16)
        s_l2o4 = sb("s_l2o4", 128, 64, BF16)
        s_R1 = sb("s_R1", 128, 128, BF16)
        s_R2 = sb("s_R2", 128, 128, BF16)
        s_Wr = sb("s_Wr", 128, 64, BF16)
        s_id = sb("s_id", 128, 128, F32)
        s_ones = sb("s_ones", 1, 128, F32)
        s_onesb = sb("s_onesb", 128, 1, BF16)
        s_onesf = sb("s_onesf", 128, 1, F32)
        s_bias = sb("s_bias", 128, 48, F32)
        s_sm = sb("s_sm", 128, 8, F32)
        s_rb = sb("s_rb", 8, 1, F32)
        s_bv4 = sb("s_bv4", 128, 132, F32)
        s_brow = sb("s_brow", 1, 132, F32)
        s_str16 = sb("s_str16", 16, 128, F32)
        s_ltm = sb("s_ltm", 128, 128, F32)
        s_sel = sb("s_sel", 128, 128, F32)
        s_lx = sb("s_lx", 128, 128, F32)
        s_p16 = sb("s_p16", 128, 16, F32)
        s_t16 = sb("s_t16", 128, 16, F32)
        s_a16 = sb("s_a16", 128, 16, F32)
        s_b16 = sb("s_b16", 128, 16, F32)
        s_r16 = sb("s_r16", 128, 16, F32)
        s_sc16 = sb("s_sc16", 128, 16, F32)
        s_u16 = sb("s_u16", 128, 16, F32)
        s_m2 = sb("s_m2", 2, 128, F32)

        psA = st.enter_context(nc.psum_tensor("psA", [128, 2048], F32))
        psB = st.enter_context(nc.psum_tensor("psB", [128, 2048], F32))

        sem = {}
        for name in ["pe", "act", "dve", "gp", "wl", "xl", "st"]:
            sem[name] = st.enter_context(nc.semaphore("s_" + name))
        cx.sem = sem

        block = st.enter_context(nc.Block())
        H = dict(locals())
        H["out"] = out

        ph0(nc, block, cx, H, D, dbg)
        attn1(nc, block, cx, H, D, dbg)
        ln1_rms1(nc, block, cx, H, D, dbg)
        attn2(nc, block, cx, H, D, dbg)
        rms2(nc, block, cx, H, D, dbg)
        router(nc, block, cx, H, D, dbg)
        moe(nc, block, cx, H, D, dbg)
        ln_final(nc, block, cx, H, D, dbg)
        output_transpose(nc, block, cx, H, D, dbg)

    return nc, dbg


def tap(block, cx, H, dbg, name, src_ap):
    if name not in dbg:
        return
    d = dbg[name]

    @block.gpsimd
    def _(gp):
        for k in ["pe", "act", "dve"]:
            cx.wait(gp, k)
        cx.dinc(gp.dma_start(out=d, in_=src_ap), "wl")
        cx.wait(gp, "wl")


def barrier_all(block, cx):
    for engname in ["tensor", "scalar", "vector", "gpsimd", "sync"]:
        @getattr(block, engname)
        def _(e):
            for k in ["pe", "act", "dve", "gp", "wl", "xl"]:
                cx.wait(e, k)


def wslot(H, slot):
    return H["wsl"].ap()[:, slot * 1024:(slot + 1) * 1024]


class WS:
    def __init__(self, cx, block, H, D, loaders, depth=3):
        self.cx, self.block, self.H, self.D = cx, block, H, D
        self.loaders = loaders
        self.nxt = 0
        self.wl = {}
        for _ in range(min(depth, len(loaders))):
            self.emit()

    def emit(self):
        k = self.nxt
        if k < len(self.loaders):
            self.wl[k] = self.loaders[k](k % 4)
            self.nxt += 1

    def mm(self, k, psum, rhs, kchunks=8, extra_wait=None):
        cx, block, H = self.cx, self.block, self.H
        slot = k % 4
        wl_cnt = self.wl[k]

        @block.tensor
        def _(tensor):
            cx.wait(tensor, "wl", wl_cnt)
            cx.wait(tensor, "act")
            cx.wait(tensor, "dve")
            if extra_wait:
                extra_wait(tensor)
            w = wslot(H, slot)
            for cc in range(kchunks):
                for n in range(4):
                    ins = tensor.matmul(psum[:, n * 512:(n + 1) * 512],
                                        w[:, cc * 128:(cc + 1) * 128],
                                        rhs(cc, n),
                                        start=(cc == 0),
                                        stop=(cc == kchunks - 1))
            cx.mark[("slot", slot)] = cx.inc(ins, "pe")
        self.emit()


def mk_loader(cx, block, H, D, wname, o, sub=None, kchunks=8):
    def load(slot):
        @block.gpsimd
        def _(gp):
            rel = cx.mark.get(("slot", slot))
            if rel is not None:
                cx.wait(gp, "pe", rel)
            s = D[wname] if sub is None else D[wname][sub]
            ap = s.rearrange("(cc p) o -> p cc o", p=128)[
                :, 0:kchunks, o * 128:(o + 1) * 128]
            dst = wslot(H, slot).rearrange("p (cc o) -> p cc o", cc=8)[
                :, 0:kchunks, :]
            cx.dinc(gp.dma_start(out=dst, in_=ap), "wl")
        return cx.cnt["wl"]
    return load


def mk_loader_w2(cx, block, H, D, wname, ih, c, sub=None):
    def load(slot):
        @block.gpsimd
        def _(gp):
            rel = cx.mark.get(("slot", slot))
            if rel is not None:
                cx.wait(gp, "pe", rel)
            s = D[wname] if sub is None else D[wname][sub]
            ap = s.rearrange("(cc p) o -> p cc o", p=128)[
                :, ih * 8:(ih + 1) * 8, c * 128:(c + 1) * 128]
            dst = wslot(H, slot).rearrange("p (cc o) -> p cc o", cc=8)
            cx.dinc(gp.dma_start(out=dst, in_=ap), "wl")
        return cx.cnt["wl"]
    return load


def ph0(nc, block, cx, H, D, dbg):
    rA, xb, psA, psB = H["rA"], H["xb"], H["psA"], H["psB"]
    stage = _ar(H, 12, (128, 2048), F32)

    @block.sync
    def _(sync):
        for nm, dst in [("c_ident", H["s_id"]), ("c_ones_row", H["s_ones"]),
                        ("c_R1", H["s_R1"]), ("c_R2", H["s_R2"]),
                        ("c_ones_col_bf", H["s_onesb"]), ("r_bias", H["s_rb"]),
                        ("c_ones_col_f", H["s_onesf"])]:
            cx.dinc(sync.dma_start(out=dst[:], in_=D[nm]), "xl")
        msk = _ar(H, 28, (128, 2048), BF16)
        for j in range(4):
            cx.dinc(sync.dma_start(out=msk[:, j * 512:(j + 1) * 512],
                                   in_=D[f"c_mask{j}"]), "xl")
        cx.dinc(sync.dma_start(out=H["s_bias"][:, 0:24], in_=D["cab"]), "xl")
        cx.dinc(sync.dma_start(out=H["s_bias"][:, 24:32], in_=D["cpb"]), "xl")
        cx.dinc(sync.dma_start(out=H["s_bias"][:, 32:40], in_=D["ln1w"]), "xl")
        cx.dinc(sync.dma_start(out=H["s_bias"][:, 40:48], in_=D["ln2w"]), "xl")
        for i, nm in enumerate(["q2l_b4", "kv2l_b4", "wqk_b4", "l2o_b2"]):
            cx.dinc(sync.dma_start(out=H["s_sm"][:, i:i + 1], in_=D[nm]), "xl")
        cx.dinc(sync.dma_start(out=H["s_brow"][:], in_=D["kv2l_brow"]), "xl")
        tabs = _ar(H, 0, (128, 2 * T), BF16)
        for i, nm in enumerate(["tab1_c", "tab1_s"]):
            cx.dinc(sync.dma_start(out=tabs[:, i * T:(i + 1) * T],
                                   in_=D["c_" + nm]), "xl")
        cx.dinc(sync.dma_start(out=stage[:, 0:1024], in_=D["x"][0:128, :]), "xl")
    xl_base = cx.cnt["xl"]

    @block.gpsimd
    def _(gp):
        cx.dinc(gp.dma_start(out=H["s_W4q"][:].rearrange("p (c m) -> p c m", c=2),
                             in_=D["W4q"].rearrange("(c p) m -> p c m", p=128)),
                "wl")
        cx.dinc(gp.dma_start(out=H["s_W4kv"][:].rearrange("p (c m) -> p c m", c=2),
                             in_=D["W4kv"].rearrange("(c p) m -> p c m", p=128)),
                "wl")
        cx.dinc(gp.dma_start(out=H["s_wqk4"][:], in_=D["wqk4"]), "wl")
        cx.dinc(gp.dma_start(out=H["s_l2o4"][:], in_=D["l2o4"]), "wl")
        cx.dinc(gp.dma_start(out=H["s_Wr"][:].rearrange("p (c m) -> p c m", c=8),
                             in_=D["Wr"].rearrange("(c p) m -> p c m", p=128)),
                "wl")

    @block.tensor
    def _(tensor):
        cx.wait(tensor, "xl")
        cx.inc(tensor.matmul(psB[:, 0:132], H["s_ones"][:], H["s_brow"][:],
                             start=True, stop=True), "pe")

    @block.vector
    def _(v):
        cx.wait(v, "pe")
        cx.inc(v.tensor_copy(H["s_bv4"][:], psB[:, 0:132]), "dve")

    tr_done, ev_done = {}, {}
    for tt in range(NTT):
        buf = tt % 2

        @block.sync
        def _(sync, tt=tt, buf=buf):
            if tt + 1 < NTT:
                if tt >= 1:
                    cx.wait(sync, "pe", tr_done[tt - 1])
                cx.dinc(sync.dma_start(
                    out=stage[:, (1 - buf) * 1024:(2 - buf) * 1024],
                    in_=D["x"][(tt + 1) * 128:(tt + 2) * 128, :]), "xl")

        @block.tensor
        def _(tensor, tt=tt, buf=buf):
            cx.wait(tensor, "xl", xl_base + tt)
            cx.wait(tensor, "dve")
            if tt > 0:
                cx.wait(tensor, "act", ev_done[tt - 1])
            for c in range(NCH):
                half = psA if c < 4 else psB
                ins = tensor.transpose(
                    half[:, (c % 4) * 512:(c % 4) * 512 + 128],
                    stage[:, buf * 1024 + c * 128: buf * 1024 + (c + 1) * 128],
                    H["s_id"][:])
            tr_done[tt] = cx.inc(ins, "pe")

        @block.scalar
        def _(scalar, tt=tt):
            cx.wait(scalar, "pe", tr_done[tt])
            for half, c0 in [(psA, 0), (psB, 4)]:
                view = half[:, 0:2048].rearrange("p (c n) -> p c n", c=4)[
                    :, :, 0:128]
                dst = rA.ap().rearrange("p (c t) -> p c t", c=8)[
                    :, c0:c0 + 4, tt * 128:(tt + 1) * 128]
                ins = scalar.copy(dst, view)
            ev_done[tt] = cx.inc(ins, "act")

        @block.vector
        def _(v, tt=tt):
            cx.wait(v, "act", ev_done[tt])
            dst = xb.ap().rearrange("p (c t) -> p c t", c=8)[
                :, :, tt * 128:(tt + 1) * 128]
            src = rA.ap().rearrange("p (c t) -> p c t", c=8)[
                :, :, tt * 128:(tt + 1) * 128]
            cx.inc(v.tensor_copy(dst, src), "dve")

    barrier_all(block, cx)
    tap(block, cx, H, dbg, "xT0", rA.ap()[:, 0:2048])


def _stabilizer(block, cx, H, a_src, b_src, coef, dst_col):
    psA = H["psA"]

    @block.vector
    def _(v):
        cx.wait(v, "act")
        cx.wait(v, "dve")
        v.tensor_reduce(H["s_sm"][:, 6:7], a_src, AX.X, ALU.max,
                        apply_absolute_value=True)
        cx.inc(v.tensor_reduce(H["s_sm"][:, 7:8], b_src, AX.X, ALU.max,
                               apply_absolute_value=True), "dve")

    @block.tensor
    def _(tensor):
        cx.wait(tensor, "dve")
        cx.inc(tensor.transpose(psA[0:2, 0:128], H["s_sm"][:, 6:8],
                                H["s_id"][:]), "pe")

    @block.vector
    def _(v):
        cx.wait(v, "pe")
        cx.inc(v.tensor_reduce(H["s_m2"][:, 0:1], psA[0:2, 0:128],
                               AX.X, ALU.max), "dve")

    @block.scalar
    def _(scalar):
        cx.wait(scalar, "dve")
        cx.inc(scalar.activation(H["s_m2"][0:2, 1:2], H["s_m2"][0:2, 0:1],
                                 AF.Ln), "act")

    @block.tensor
    def _(tensor):
        cx.wait(tensor, "act")
        cx.inc(tensor.matmul(psA[0:1, 131:132], H["s_onesf"][0:2, 0:1],
                             H["s_m2"][0:2, 1:2], start=True, stop=True), "pe")

    @block.scalar
    def _(scalar):
        cx.wait(scalar, "pe")
        cx.inc(scalar.activation(H["s_m2"][0:1, 2:3], psA[0:1, 131:132],
                                 AF.Exp), "act")

    @block.vector
    def _(v):
        cx.wait(v, "act")
        cx.inc(v.tensor_scalar(H["s_m2"][0:1, 3:4], H["s_m2"][0:1, 2:3],
                               -coef, None, ALU.mult), "dve")

    @block.tensor
    def _(tensor):
        cx.wait(tensor, "dve")
        cx.inc(tensor.matmul(psA[:, 130:131], H["s_ones"][:],
                             H["s_m2"][0:1, 3:4], start=True, stop=True), "pe")

    @block.vector
    def _(v):
        cx.wait(v, "pe")
        cx.inc(v.tensor_copy(H["s_sm"][:, dst_col:dst_col + 1],
                             psA[:, 130:131]), "dve")


def attn1(nc, block, cx, H, D, dbg):
    rA, xb, yT, psA, psB = H["rA"], H["xb"], H["yT"], H["psA"], H["psB"]
    tab_c = _ar(H, 0, (128, T), BF16)
    tab_s = _ar(H, 4, (128, T), BF16)
    scr = _ar(H, 8, (128, 2048), BF16)
    qlat = _ar(H, 12, (128, 2048), F32)
    vb2 = _ar(H, 12, (128, 2048), BF16)
    tmpf = _ar(H, 20, (128, 2048), F32)
    msk = _ar(H, 28, (128, 2048), BF16)
    ylab = _ar(H, 32, (128, 2048), BF16)
    lnrow = _ar(H, 40, (1, 512), F32)
    recr = _ar(H, 42, (1, 512), F32)
    recb = _ar(H, 44, (64, 512), F32)
    qfk = _p2(H, 0, (128, 2048), BF16)
    k4 = _p2(H, 4096, (128, 2048), BF16)
    v4 = _p2(H, 8192, (128, 2112), BF16)
    pt = _p2(H, 12416, (128, 2048), BF16)

    def xbr(cc, n):
        return xb.ap()[:, cc * T + n * 512: cc * T + (n + 1) * 512]

    loaders = []
    for g in range(4):
        for sec in (0, 8, 16):
            for sub in range(2):
                loaders.append(mk_loader(cx, block, H, D, "caw",
                                         sec + 2 * g + sub))
    for o in range(NCH):
        loaders.append(mk_loader(cx, block, H, D, "cpw", o))
    ws = WS(cx, block, H, D, loaders)

    for g in range(4):
        for path, w4, bias_i in [("q", H["s_W4q"], 0), ("k", H["s_W4kv"], 1)]:
            sec = 0 if path == "q" else 8
            for sub in range(2):
                o = sec + 2 * g + sub
                ws.mm(g * 6 + (0 if path == "q" else 2) + sub, psA, xbr)

                @block.scalar
                def _(scalar, o=o):
                    cx.wait(scalar, "pe")
                    cx.inc(scalar.activation(scr[:], psA[:, 0:2048], AF.Identity,
                                             bias=H["s_bias"][:, o:o + 1]),
                           "act")

                @block.tensor
                def _(tensor, w4=w4, sub=sub):
                    cx.wait(tensor, "act")
                    for n in range(4):
                        ins = tensor.matmul(psB[:, n * 512:(n + 1) * 512],
                                            w4[:, sub * 128:(sub + 1) * 128],
                                            scr[:, n * 512:(n + 1) * 512],
                                            start=(sub == 0), stop=(sub == 1))
                    cx.inc(ins, "pe")

            @block.scalar
            def _(scalar, bias_i=bias_i):
                cx.wait(scalar, "pe")
                cx.inc(scalar.activation(qlat, psB[:, 0:2048], AF.Identity,
                                         bias=H["s_sm"][:, bias_i:bias_i + 1]),
                       "act")

            @block.vector
            def _(v):
                cx.wait(v, "act")
                cx.inc(v.tensor_copy(scr[:], qlat), "dve")

            @block.tensor
            def _(tensor):
                cx.wait(tensor, "dve")
                for n in range(4):
                    ins = tensor.matmul(psA[:, n * 512:(n + 1) * 512],
                                        H["s_R1"][:],
                                        scr[:, n * 512:(n + 1) * 512],
                                        start=True, stop=True)
                cx.inc(ins, "pe")

            @block.vector
            def _(v, path=path):
                cx.wait(v, "pe")
                v.tensor_tensor(tmpf, psA[:, 0:2048], tab_s, ALU.mult)
                cx.wait(v, "act")
                v.tensor_tensor(qlat, qlat, tab_c, ALU.mult)
                dest = scr[:] if path == "q" else k4
                cx.inc(v.tensor_tensor(dest, tmpf, qlat, ALU.add), "dve")

            if path == "q":
                @block.tensor
                def _(tensor):
                    cx.wait(tensor, "dve")
                    for n in range(4):
                        ins = tensor.matmul(psB[:, n * 512:(n + 1) * 512],
                                            H["s_wqk4"][:],
                                            scr[:, n * 512:(n + 1) * 512],
                                            start=True, stop=True)
                    cx.inc(ins, "pe")

                @block.scalar
                def _(scalar):
                    cx.wait(scalar, "pe")
                    cx.inc(scalar.activation(qfk, psB[:, 0:2048], AF.Identity,
                                             bias=H["s_sm"][:, 2:3]), "act")

        for sub in range(2):
            o = 16 + 2 * g + sub
            ws.mm(g * 6 + 4 + sub, psA, xbr)

            @block.scalar
            def _(scalar, sub=sub, o=o):
                cx.wait(scalar, "pe")
                dst = scr[:] if sub == 0 else vb2[:]
                cx.inc(scalar.activation(dst, psA[:, 0:2048], AF.Identity,
                                         bias=H["s_bias"][:, o:o + 1]), "act")

        @block.tensor
        def _(tensor):
            cx.wait(tensor, "act")
            cx.wait(tensor, "dve")
            for tt in range(NTT):
                for s2, buf in [(0, scr[:]), (1, vb2[:])]:
                    ins = tensor.matmul(
                        psB[:, tt * 128:(tt + 1) * 128],
                        buf[:, tt * 128:(tt + 1) * 128],
                        H["s_W4kv"][:, s2 * 128:(s2 + 1) * 128],
                        start=(s2 == 0), stop=(s2 == 1))
            cx.inc(ins, "pe")

        @block.vector
        def _(v):
            v.memset(v4.rearrange("p (tt h l) -> p tt h l", tt=16, h=4)[
                :, :, :, 32:33], 1.0)
            cx.wait(v, "pe")
            src = psB[:, 0:2048].rearrange("p (tt h l) -> p tt h l", tt=16, h=4)
            dst = v4.rearrange("p (tt h l) -> p tt h l", tt=16, h=4)[
                :, :, :, 0:32]
            b = H["s_bv4"].ap().rearrange("p (h l) -> p h l", h=4)[
                :, :, 0:32].unsqueeze(1).broadcast_to([128, 16, 4, 32])
            cx.inc(v.tensor_tensor(dst, src, b, ALU.add), "dve")

        _stabilizer(block, cx, H, qfk, k4, 1.02 * math.sqrt(LAT), 4)
        negM = H["s_sm"][:, 4:5]

        for qs in range(NQS):
            kmax = qs * 4 + 3
            for kt in range(kmax + 1):
                @block.tensor
                def _(tensor, qs=qs, kt=kt, g=g):
                    if kt == 0 and qs == 0:
                        cx.wait(tensor, "dve")
                        cx.wait(tensor, "act")
                    m = cx.mark.get(("exp", g))
                    if m is not None:
                        cx.wait(tensor, "act", m)
                    m = cx.mark.get(("rec", g))
                    if m is not None:
                        cx.wait(tensor, "dve", m)
                    for h in range(4):
                        ins = tensor.matmul(
                            psA[:, h * 512:(h + 1) * 512],
                            k4[32 * h:32 * h + 32, kt * 128:(kt + 1) * 128],
                            qfk[32 * h:32 * h + 32, qs * 512:(qs + 1) * 512],
                            start=True, stop=True, tile_position=(32 * h, 0))
                    cx.mark[("S", g)] = cx.inc(ins, "pe")

                @block.scalar
                def _(scalar, g=g):
                    cx.wait(scalar, "pe", cx.mark[("S", g)])
                    m = cx.mark.get(("pv", g))
                    if m is not None:
                        cx.wait(scalar, "pe", m)
                    m = cx.mark.get(("msk", g))
                    if m is not None:
                        cx.wait(scalar, "dve", m)
                    for h in range(4):
                        ins = scalar.activation(pt[:, h * 512:(h + 1) * 512],
                                                psA[:, h * 512:(h + 1) * 512],
                                                AF.Exp, bias=negM,
                                                scale=EXP_SC1)
                    cx.mark[("exp", g)] = cx.inc(ins, "act")

                diag = kt - qs * 4
                if diag >= 0:
                    @block.vector
                    def _(v, diag=diag, g=g):
                        cx.wait(v, "act", cx.mark[("exp", g)])
                        for h in range(4):
                            ins = v.tensor_tensor(
                                pt[:, h * 512:(h + 1) * 512],
                                pt[:, h * 512:(h + 1) * 512],
                                msk[:, diag * 512:(diag + 1) * 512], ALU.mult)
                        cx.mark[("msk", g)] = cx.inc(ins, "dve")

                @block.tensor
                def _(tensor, qs=qs, kt=kt, diag=diag, g=g, kmax=kmax):
                    if diag >= 0:
                        cx.wait(tensor, "dve", cx.mark[("msk", g)])
                    else:
                        cx.wait(tensor, "act", cx.mark[("exp", g)])
                    if kt == 0:
                        m = cx.mark.get(("yl", g))
                        if m is not None:
                            cx.wait(tensor, "dve", m)
                    for h in range(4):
                        ins = tensor.matmul(
                            psB[0:33, h * 512:(h + 1) * 512],
                            v4[:, kt * 132 + h * 33: kt * 132 + h * 33 + 33],
                            pt[:, h * 512:(h + 1) * 512],
                            start=(kt == 0), stop=(kt == kmax))
                    cx.mark[("pv", g)] = cx.inc(ins, "pe")

            for h in range(4):
                @block.scalar
                def _(scalar, h=h, g=g):
                    cx.wait(scalar, "pe")
                    scalar.activation(lnrow, psB[32:33, h * 512:(h + 1) * 512],
                                      AF.Ln)
                    cx.inc(scalar.activation(recr, lnrow, AF.Exp, scale=-1.0),
                           "act")

                @block.tensor
                def _(tensor, h=h):
                    cx.wait(tensor, "act")
                    cx.inc(tensor.matmul(psA[64:96, h * 512:(h + 1) * 512],
                                         H["s_ones"][:, 0:32], recr,
                                         start=True, stop=True), "pe")

                @block.vector
                def _(v, h=h, qs=qs, g=g):
                    cx.wait(v, "pe")
                    cx.chain(v, v.tensor_copy(recb[0:32, :],
                                              psA[64:96, h * 512:(h + 1) * 512]))
                    ins = v.tensor_tensor(
                        ylab[32 * h:32 * h + 32, qs * 512:(qs + 1) * 512],
                        psB[0:32, h * 512:(h + 1) * 512],
                        recb[0:32, :], ALU.mult)
                    m = cx.inc(ins, "dve")
                    cx.mark[("yl", g)] = m
                    cx.mark[("rec", g)] = m

        for qs in range(NQS):
            @block.tensor
            def _(tensor, qs=qs, g=g):
                cx.wait(tensor, "dve", cx.mark[("yl", g)])
                if qs > 0:
                    cx.wait(tensor, "act", cx.mark[("yT", g)])
                for h in range(4):
                    ins = tensor.matmul(
                        psA[0:64, h * 512:(h + 1) * 512],
                        H["s_l2o4"][32 * h:32 * h + 32, :],
                        ylab[32 * h:32 * h + 32, qs * 512:(qs + 1) * 512],
                        start=True, stop=True, tile_position=(32 * h, 0))
                cx.mark[("l2o", g)] = cx.inc(ins, "pe")

            @block.scalar
            def _(scalar, qs=qs, g=g):
                cx.wait(scalar, "pe", cx.mark[("l2o", g)])
                for h in range(4):
                    hg = 4 * g + h
                    chunk = hg // 2
                    r0 = 64 * (hg % 2)
                    ins = scalar.activation(
                        yT.ap()[r0:r0 + 64,
                                chunk * T + qs * 512: chunk * T + (qs + 1) * 512],
                        psA[0:64, h * 512:(h + 1) * 512], AF.Identity,
                        bias=H["s_sm"][r0:r0 + 64, 3:4])
                cx.mark[("yT", g)] = cx.inc(ins, "act")

    def ytr(cc, n):
        return yT.ap()[:, cc * T + n * 512: cc * T + (n + 1) * 512]

    for o in range(NCH):
        ps = psA if o % 2 == 0 else psB
        ws.mm(24 + o, ps, ytr)

        @block.scalar
        def _(scalar, o=o, ps=ps):
            cx.wait(scalar, "pe")
            cx.wait(scalar, "dve")
            cx.inc(scalar.activation(tmpf, ps[:, 0:2048], AF.Identity,
                                     bias=H["s_bias"][:, 24 + o:25 + o]), "act")

        @block.vector
        def _(v, o=o):
            cx.wait(v, "act")
            cx.inc(v.tensor_tensor(rA.ap()[:, o * T:(o + 1) * T],
                                   rA.ap()[:, o * T:(o + 1) * T],
                                   tmpf, ALU.add), "dve")

    barrier_all(block, cx)
    tap(block, cx, H, dbg, "a1_sm", H["s_sm"].ap()[:, 0:8])
    tap(block, cx, H, dbg, "a1_m2", H["s_m2"].ap()[0:2, 0:8])
    tap(block, cx, H, dbg, "a1_neg", H["s_sm"].ap()[:, 4:5])
    tap(block, cx, H, dbg, "a1_yl", ylab)
    tap(block, cx, H, dbg, "a1_pt", pt)
    tap(block, cx, H, dbg, "a1_v4", v4)
    tap(block, cx, H, dbg, "a1_qfk", qfk)
    tap(block, cx, H, dbg, "a1_k4", k4)
    tap(block, cx, H, dbg, "a1_yT0", yT.ap()[:, 0:2048])
    tap(block, cx, H, dbg, "x1pre0", rA.ap()[:, 0:2048])


def _stats_rows_bcast(block, cx, H, want_mean, eps, src_bf):
    psA, psB = H["psA"], H["psB"]
    sq = _ar(H, 8, (128, 2048), BF16)
    meanb = _ar(H, 12, (128, 2048), F32)
    rstdb = _ar(H, 20, (128, 2048), F32)
    tmprow = _ar(H, 0, (1, 2048), F32)

    for c in range(NCH):
        @block.scalar
        def _(scalar, c=c):
            if c == 0:
                cx.wait(scalar, "dve")
            cx.wait(scalar, "pe")
            cx.inc(scalar.activation(sq[:], src_bf(c), AF.Square), "act")

        @block.tensor
        def _(tensor, c=c):
            cx.wait(tensor, "act")
            for n in range(4):
                ins = tensor.matmul(psB[0:1, n * 512:(n + 1) * 512],
                                    H["s_onesb"][:],
                                    sq[:, n * 512:(n + 1) * 512],
                                    start=(c == 0), stop=(c == NCH - 1))
            if want_mean:
                for n in range(4):
                    ins = tensor.matmul(psA[0:1, n * 512:(n + 1) * 512],
                                        H["s_onesb"][:],
                                        src_bf(c)[:, n * 512:(n + 1) * 512],
                                        start=(c == 0), stop=(c == NCH - 1))
            cx.inc(ins, "pe")

    @block.vector
    def _(v):
        cx.wait(v, "pe")
        if want_mean:
            cx.chain(v, v.tensor_scalar(meanb[0:1, :], psA[0:1, 0:2048],
                                        1.0 / C, None, ALU.mult))
            cx.chain(v, v.tensor_scalar(rstdb[0:1, :], psB[0:1, 0:2048],
                                        1.0 / C, None, ALU.mult))
            cx.chain(v, v.tensor_tensor(tmprow, meanb[0:1, :], meanb[0:1, :],
                                        ALU.mult))
            cx.chain(v, v.tensor_tensor(rstdb[0:1, :], rstdb[0:1, :], tmprow,
                                        ALU.subtract))
        else:
            cx.chain(v, v.tensor_scalar(rstdb[0:1, :], psB[0:1, 0:2048],
                                        1.0 / C, None, ALU.mult))
        cx.inc(v.tensor_scalar(rstdb[0:1, :], rstdb[0:1, :], eps, None,
                               ALU.add), "dve")

    @block.scalar
    def _(scalar):
        cx.wait(scalar, "dve")
        cx.chain(scalar, scalar.activation(rstdb[0:1, :], rstdb[0:1, :],
                                           AF.Ln), "act")
        cx.inc(scalar.activation(rstdb[0:1, :], rstdb[0:1, :], AF.Exp,
                                 scale=-0.5), "act")

    @block.tensor
    def _(tensor):
        cx.wait(tensor, "act")
        cx.wait(tensor, "dve")
        for n in range(4):
            ins = tensor.matmul(psB[:, n * 512:(n + 1) * 512], H["s_ones"][:],
                                rstdb[0:1, n * 512:(n + 1) * 512],
                                start=True, stop=True)
            if want_mean:
                ins = tensor.matmul(psA[:, n * 512:(n + 1) * 512],
                                    H["s_ones"][:],
                                    meanb[0:1, n * 512:(n + 1) * 512],
                                    start=True, stop=True)
        cx.inc(ins, "pe")

    @block.scalar
    def _(scalar):
        cx.wait(scalar, "pe")
        if want_mean:
            scalar.copy(meanb, psA[:, 0:2048])
        cx.inc(scalar.copy(rstdb, psB[:, 0:2048]), "act")

    return meanb, rstdb


def ln1_rms1(nc, block, cx, H, D, dbg):
    rA, xb = H["rA"], H["xb"]
    tmpf = _ar(H, 0, (128, 2048), F32)

    def mk_src(c):
        return xb.ap()[:, c * T:(c + 1) * T]

    for c in range(NCH):
        @block.vector
        def _(v, c=c):
            if c == 0:
                cx.wait(v, "pe")
                cx.wait(v, "act")
            cx.inc(v.tensor_copy(mk_src(c), rA.ap()[:, c * T:(c + 1) * T]),
                   "dve")
    meanb, rstdb = _stats_rows_bcast(block, cx, H, True, 1e-5, mk_src)

    for c in range(NCH):
        @block.vector
        def _(v, c=c):
            cx.wait(v, "act")
            v.tensor_tensor(tmpf, rA.ap()[:, c * T:(c + 1) * T], meanb,
                            ALU.subtract)
            cx.inc(v.tensor_tensor(rA.ap()[:, c * T:(c + 1) * T], tmpf, rstdb,
                                   ALU.mult), "dve")
    barrier_all(block, cx)
    tap(block, cx, H, dbg, "x1_0", rA.ap()[:, 0:2048])

    for c in range(NCH):
        @block.vector
        def _(v, c=c):
            cx.inc(v.tensor_copy(xb.ap()[:, c * T:(c + 1) * T],
                                 rA.ap()[:, c * T:(c + 1) * T]), "dve")
    _, rstdb = _stats_rows_bcast(block, cx, H, False, 1e-6, mk_src)

    for c in range(NCH):
        @block.vector
        def _(v, c=c):
            cx.wait(v, "act")
            v.tensor_tensor(tmpf, rA.ap()[:, c * T:(c + 1) * T], rstdb,
                            ALU.mult)
            cx.inc(v.tensor_scalar(xb.ap()[:, c * T:(c + 1) * T], tmpf,
                                   H["s_bias"][:, 32 + c:33 + c], None,
                                   ALU.mult), "dve")
    barrier_all(block, cx)
    tap(block, cx, H, dbg, "xn1_0", xb.ap()[:, 0:2048])


def attn2(nc, block, cx, H, D, dbg):
    rA, xb, yT, psA, psB = H["rA"], H["xb"], H["yT"], H["psA"], H["psB"]
    tab_c = _ar(H, 0, (128, T), BF16)
    tab_s = _ar(H, 4, (128, T), BF16)
    scr = _ar(H, 8, (128, 2048), BF16)
    qlat = _ar(H, 12, (128, 2048), F32)
    tmpf = _ar(H, 20, (128, 2048), F32)
    msk = _ar(H, 28, (128, 2048), BF16)
    skv = _ar(H, 32, (128, 2 * T), BF16)
    lnrow = _ar(H, 40, (1, 512), F32)
    recr = _ar(H, 42, (1, 512), F32)
    recb = _ar(H, 44, (64, 512), F32)
    q2 = _p2(H, 0, (128, 2048), BF16)
    k2 = _p2(H, 4096, (128, 2048), BF16)
    v2 = _p2(H, 8192, (128, 2112), BF16)
    pt = _p2(H, 12416, (128, 2048), BF16)

    def xbr(cc, n):
        return xb.ap()[:, cc * T + n * 512: cc * T + (n + 1) * 512]

    @block.sync
    def _(sync):
        for k in ["pe", "act", "dve", "gp"]:
            cx.wait(sync, k)
        cx.dinc(sync.dma_start(out=tab_c, in_=D["c_tab2_c"]), "xl")
        cx.dinc(sync.dma_start(out=tab_s, in_=D["c_tab2_s"]), "xl")
    tab_ld = cx.cnt["xl"]

    loaders = [mk_loader(cx, block, H, D, "Wkv", 0),
               mk_loader(cx, block, H, D, "Wkv", 1)]
    for p in range(NE):
        loaders.append(mk_loader(cx, block, H, D, "Wq", p))
        loaders.append(mk_loader(cx, block, H, D, "Wk2", 0, sub=p, kchunks=2))
        loaders.append(mk_loader(cx, block, H, D, "Wv2", 0, sub=p, kchunks=2))
    for o in range(NCH):
        loaders.append(mk_loader(cx, block, H, D, "Wo", o))
    ws = WS(cx, block, H, D, loaders)

    for o2 in range(2):
        ps = psA if o2 == 0 else psB
        ws.mm(o2, ps, xbr)

        @block.scalar
        def _(scalar, o2=o2, ps=ps):
            cx.wait(scalar, "pe")
            cx.inc(scalar.activation(skv[:, o2 * T:(o2 + 1) * T],
                                     ps[:, 0:2048], AF.Copy, scale=0.25),
                   "act")

    for p in range(NE):
        kq = 2 + 3 * p
        ws.mm(kq, psA, xbr)

        for path in ("q", "k"):
            if path == "q":
                @block.scalar
                def _(scalar):
                    cx.wait(scalar, "pe")
                    cx.wait(scalar, "xl", tab_ld)
                    cx.inc(scalar.copy(qlat, psA[:, 0:2048]), "act")
            else:
                def skvr(cc, n):
                    return skv[:, cc * T + n * 512: cc * T + (n + 1) * 512]
                ws.mm(2 + 3 * p + 1, psA, skvr, kchunks=2)

                @block.scalar
                def _(scalar):
                    cx.wait(scalar, "pe")
                    cx.inc(scalar.copy(qlat, psA[:, 0:2048]), "act")

            @block.vector
            def _(v):
                cx.wait(v, "act")
                cx.inc(v.tensor_copy(scr[:], qlat), "dve")

            @block.tensor
            def _(tensor):
                cx.wait(tensor, "dve")
                for n in range(4):
                    ins = tensor.matmul(psB[:, n * 512:(n + 1) * 512],
                                        H["s_R2"][:],
                                        scr[:, n * 512:(n + 1) * 512],
                                        start=True, stop=True)
                cx.inc(ins, "pe")

            @block.vector
            def _(v, path=path):
                cx.wait(v, "pe")
                cx.wait(v, "xl", tab_ld)
                v.tensor_tensor(tmpf, psB[:, 0:2048], tab_s, ALU.mult)
                cx.wait(v, "act")
                v.tensor_tensor(qlat, qlat, tab_c, ALU.mult)
                dest = q2 if path == "q" else k2
                cx.inc(v.tensor_tensor(dest, tmpf, qlat, ALU.add), "dve")

        kv2 = 2 + 3 * p + 2
        slot_v2 = kv2 % 4

        @block.tensor
        def _(tensor, kv2=kv2, slot_v2=slot_v2):
            cx.wait(tensor, "wl", ws.wl[kv2])
            cx.wait(tensor, "dve")
            w = wslot(H, slot_v2)
            for tt in range(NTT):
                for lc in range(2):
                    ins = tensor.matmul(
                        psB[:, tt * 128:(tt + 1) * 128],
                        skv[:, lc * T + tt * 128: lc * T + (tt + 1) * 128],
                        w[:, lc * 128:(lc + 1) * 128],
                        start=(lc == 0), stop=(lc == 1))
            cx.mark[("slot", slot_v2)] = cx.inc(ins, "pe")
        ws.emit()

        @block.vector
        def _(v):
            v2v = v2[:, 0:2080].rearrange("p (tt h l) -> p tt h l", tt=16, h=2)
            v.memset(v2v[:, :, :, 64:65], 1.0)
            cx.wait(v, "pe")
            src = psB[:, 0:2048].rearrange("p (tt h l) -> p tt h l", tt=16, h=2)
            cx.inc(v.tensor_copy(v2v[:, :, :, 0:64], src), "dve")

        _stabilizer(block, cx, H, q2, k2, 1.02 * math.sqrt(HD), 5)
        negM = H["s_sm"][:, 5:6]

        for qs in range(NQS):
            kmax = qs * 4 + 3
            for kt in range(kmax + 1):
                @block.tensor
                def _(tensor, qs=qs, kt=kt, p=p):
                    if kt == 0 and qs == 0:
                        cx.wait(tensor, "dve")
                        cx.wait(tensor, "act")
                    m = cx.mark.get(("exp2", p))
                    if m is not None:
                        cx.wait(tensor, "act", m)
                    m = cx.mark.get(("rec2", p))
                    if m is not None:
                        cx.wait(tensor, "dve", m)
                    for a in range(2):
                        ins = tensor.matmul(
                            psA[:, a * 512:(a + 1) * 512],
                            k2[64 * a:64 * a + 64, kt * 128:(kt + 1) * 128],
                            q2[64 * a:64 * a + 64, qs * 512:(qs + 1) * 512],
                            start=True, stop=True, tile_position=(64 * a, 0))
                    cx.mark[("S2", p)] = cx.inc(ins, "pe")

                @block.scalar
                def _(scalar, p=p):
                    cx.wait(scalar, "pe", cx.mark[("S2", p)])
                    m = cx.mark.get(("pv2", p))
                    if m is not None:
                        cx.wait(scalar, "pe", m)
                    m = cx.mark.get(("msk2", p))
                    if m is not None:
                        cx.wait(scalar, "dve", m)
                    for a in range(2):
                        ins = scalar.activation(pt[:, a * 512:(a + 1) * 512],
                                                psA[:, a * 512:(a + 1) * 512],
                                                AF.Exp, bias=negM,
                                                scale=EXP_SC2)
                    cx.mark[("exp2", p)] = cx.inc(ins, "act")

                diag = kt - qs * 4
                if diag >= 0:
                    @block.vector
                    def _(v, diag=diag, p=p):
                        cx.wait(v, "act", cx.mark[("exp2", p)])
                        for a in range(2):
                            ins = v.tensor_tensor(
                                pt[:, a * 512:(a + 1) * 512],
                                pt[:, a * 512:(a + 1) * 512],
                                msk[:, diag * 512:(diag + 1) * 512], ALU.mult)
                        cx.mark[("msk2", p)] = cx.inc(ins, "dve")

                @block.tensor
                def _(tensor, qs=qs, kt=kt, diag=diag, p=p, kmax=kmax):
                    if diag >= 0:
                        cx.wait(tensor, "dve", cx.mark[("msk2", p)])
                    else:
                        cx.wait(tensor, "act", cx.mark[("exp2", p)])
                    if kt == 0:
                        m = cx.mark.get(("yl2", p))
                        if m is not None:
                            cx.wait(tensor, "dve", m)
                    for a in range(2):
                        ins = tensor.matmul(
                            psB[0:65, a * 512:(a + 1) * 512],
                            v2[:, kt * 130 + a * 65: kt * 130 + a * 65 + 65],
                            pt[:, a * 512:(a + 1) * 512],
                            start=(kt == 0), stop=(kt == kmax))
                    cx.mark[("pv2", p)] = cx.inc(ins, "pe")

            for a in range(2):
                @block.scalar
                def _(scalar, a=a, p=p):
                    cx.wait(scalar, "pe")
                    scalar.activation(lnrow, psB[64:65, a * 512:(a + 1) * 512],
                                      AF.Ln)
                    cx.inc(scalar.activation(recr, lnrow, AF.Exp, scale=-1.0),
                           "act")

                @block.tensor
                def _(tensor, a=a):
                    cx.wait(tensor, "act")
                    cx.inc(tensor.matmul(psA[64:128, a * 512:(a + 1) * 512],
                                         H["s_ones"][:, 0:64], recr,
                                         start=True, stop=True), "pe")

                @block.vector
                def _(v, a=a, qs=qs, p=p):
                    cx.wait(v, "pe")
                    cx.chain(v, v.tensor_copy(recb[:],
                                              psA[64:128, a * 512:(a + 1) * 512]))
                    ins = v.tensor_tensor(
                        yT.ap()[64 * a:64 * a + 64,
                                p * T + qs * 512: p * T + (qs + 1) * 512],
                        psB[0:64, a * 512:(a + 1) * 512], recb[:], ALU.mult)
                    m = cx.inc(ins, "dve")
                    cx.mark[("yl2", p)] = m
                    cx.mark[("rec2", p)] = m

    ych = _ar(H, 12, (128, 2048), F32)

    def ytr(cc, n):
        return yT.ap()[:, cc * T + n * 512: cc * T + (n + 1) * 512]

    for o in range(NCH):
        ps = psA if o % 2 == 0 else psB
        ws.mm(2 + 24 + o, ps, ytr)

        @block.scalar
        def _(scalar, ps=ps):
            cx.wait(scalar, "pe")
            cx.wait(scalar, "dve")
            cx.inc(scalar.copy(tmpf, ps[:, 0:2048]), "act")

        @block.vector
        def _(v, o=o):
            cx.wait(v, "act")
            v.tensor_tensor(ych, tmpf, rA.ap()[:, o * T:(o + 1) * T], ALU.add)
            v.tensor_copy(xb.ap()[:, o * T:(o + 1) * T], ych)
            cx.inc(v.tensor_tensor(rA.ap()[:, o * T:(o + 1) * T],
                                   rA.ap()[:, o * T:(o + 1) * T], ych,
                                   ALU.add), "dve")

    barrier_all(block, cx)
    tap(block, cx, H, dbg, "a2_kv", skv[:, 0:2048])
    tap(block, cx, H, dbg, "a2_q2", q2)
    tap(block, cx, H, dbg, "a2_k2", k2)
    tap(block, cx, H, dbg, "a2_v2", v2[:, 0:2080])
    tap(block, cx, H, dbg, "a2_neg", H["s_sm"].ap()[:, 5:6])
    tap(block, cx, H, dbg, "a2_yT0", yT.ap()[:, 0:2048])
    tap(block, cx, H, dbg, "yb0", xb.ap()[:, 0:2048])


def rms2(nc, block, cx, H, D, dbg):
    xb = H["xb"]
    tmpf = _ar(H, 0, (128, 2048), F32)

    def mk_src(c):
        return xb.ap()[:, c * T:(c + 1) * T]

    _, rstdb = _stats_rows_bcast(block, cx, H, False, 1e-6, mk_src)
    for c in range(NCH):
        @block.vector
        def _(v, c=c):
            cx.wait(v, "act")
            v.tensor_tensor(tmpf, xb.ap()[:, c * T:(c + 1) * T], rstdb,
                            ALU.mult)
            cx.inc(v.tensor_scalar(xb.ap()[:, c * T:(c + 1) * T], tmpf,
                                   H["s_bias"][:, 40 + c:41 + c], None,
                                   ALU.mult), "dve")
    barrier_all(block, cx)
    tap(block, cx, H, dbg, "yn0", xb.ap()[:, 0:2048])


def router(nc, block, cx, H, D, dbg):
    xb, psA, psB = H["xb"], H["psA"], H["psB"]
    lgt = _p2(H, 0, (8, 2048), F32)

    @block.tensor
    def _(tensor):
        cx.wait(tensor, "dve")
        for cc in range(NCH):
            for n in range(4):
                ins = tensor.matmul(psA[0:8, n * 512:(n + 1) * 512],
                                    H["s_Wr"][:, cc * 8:(cc + 1) * 8],
                                    xb.ap()[:, cc * T + n * 512:
                                            cc * T + (n + 1) * 512],
                                    start=(cc == 0), stop=(cc == NCH - 1))
        cx.inc(ins, "pe")

    @block.scalar
    def _(scalar):
        cx.wait(scalar, "pe")
        cx.inc(scalar.activation(lgt, psA[0:8, 0:2048], AF.Identity,
                                 bias=H["s_rb"][:]), "act")

    @block.vector
    def _(v):
        cx.wait(v, "act")
        cx.inc(v.tensor_scalar(lgt, lgt, 50.0, -50.0, ALU.min, ALU.max), "dve")

    lgt32 = _p2(H, 0, (32, 2048), F32)

    @block.tensor
    def _(tensor):
        cx.wait(tensor, "dve")
        for tt in range(NTT):
            ins = tensor.transpose(psB[:, tt * 32:(tt + 1) * 32],
                                   lgt32[:, tt * 128:(tt + 1) * 128],
                                   H["s_id"][0:32, 0:32])
        cx.inc(ins, "pe")

    @block.vector
    def _(v):
        cx.wait(v, "pe")
        v.tensor_copy(H["s_ltm"][:].rearrange("p (t e) -> p t e", e=8),
                      psB[:, 0:512].rearrange("p (t e) -> p t e", e=32)[
                          :, :, 0:8])
        lt3 = H["s_ltm"].ap().rearrange("p (t e) -> p t e", e=8)
        lx4 = H["s_lx"].ap()[:, 0:64].rearrange("p (t e) -> p t e", e=4)
        lx2 = H["s_lx"].ap()[:, 64:96].rearrange("p (t e) -> p t e", e=2)
        cx.chain(v, v.tensor_tensor(lx4, lt3[:, :, 0:4], lt3[:, :, 4:8],
                                    ALU.max))
        cx.chain(v, v.tensor_tensor(lx2, lx4[:, :, 0:2], lx4[:, :, 2:4],
                                    ALU.max))
        cx.chain(v, v.tensor_tensor(H["s_p16"].ap().unsqueeze(2),
                                    lx2[:, :, 0:1], lx2[:, :, 1:2], ALU.max))
        sel3 = H["s_sel"].ap().rearrange("p (t e) -> p t e", e=8)
        cx.chain(v, v.memset(H["s_r16"][:], 0.0))
        for e in range(NE):
            lt_e = lt3[:, :, e:e + 1].squeeze(2)
            cx.chain(v, v.tensor_tensor(H["s_t16"][:], lt_e, H["s_p16"][:],
                                        ALU.is_ge))
            cx.chain(v, v.tensor_scalar(H["s_u16"][:], H["s_r16"][:], -1.0,
                                        1.0, ALU.mult, ALU.add))
            cx.chain(v, v.tensor_tensor(sel3[:, :, e:e + 1].squeeze(2),
                                        H["s_t16"][:], H["s_u16"][:],
                                        ALU.mult))
            cx.chain(v, v.tensor_tensor(H["s_r16"][:], H["s_r16"][:],
                                        sel3[:, :, e:e + 1].squeeze(2),
                                        ALU.add))
        cx.inc(v.tensor_copy(H["s_sc16"][:], H["s_p16"][:]), "dve")

    @block.scalar
    def _(scalar):
        cx.wait(scalar, "dve")
        cx.inc(scalar.activation(H["s_t16"][:], H["s_p16"][:], AF.Exp,
                                 scale=-1.0), "act")

    @block.vector
    def _(v):
        cx.wait(v, "act")
        cx.chain(v, v.tensor_scalar(H["s_u16"][:], H["s_t16"][:], 1.0, None,
                                    ALU.add))
        cx.chain(v, v.reciprocal(H["s_p16"][:], H["s_u16"][:]))
        cx.chain(v, v.tensor_scalar(H["s_p16"][:], H["s_p16"][:], 1e-8,
                                    1.0 - 1e-8, ALU.max, ALU.min))
        cx.chain(v, v.tensor_scalar(H["s_t16"][:], H["s_p16"][:], 0.5 + 1e-8,
                                    None, ALU.add))
        cx.chain(v, v.tensor_scalar(H["s_u16"][:], H["s_t16"][:], 0.5, 2.0,
                                    ALU.max, ALU.min))
        cx.chain(v, v.reciprocal(H["s_a16"][:], H["s_u16"][:]))
        cx.chain(v, v.tensor_tensor(H["s_b16"][:], H["s_p16"][:],
                                    H["s_a16"][:], ALU.mult))
        cx.inc(v.tensor_scalar(H["s_a16"][:], H["s_a16"][:], 0.5, None,
                               ALU.mult), "dve")
    barrier_all(block, cx)
    tap(block, cx, H, dbg, "r_ltm", H["s_ltm"].ap()[:, 0:128])
    tap(block, cx, H, dbg, "r_eq", H["s_lx"].ap()[:, 0:128])
    tap(block, cx, H, dbg, "r_pmax", H["s_sc16"].ap()[:, 0:16])
    tap(block, cx, H, dbg, "r_lgt", _p2(H, 0, (8, 2048), F32))
    tap(block, cx, H, dbg, "sel", H["s_sel"].ap()[:, 0:128])
    tap(block, cx, H, dbg, "p16", H["s_p16"].ap()[:, 0:16])


def moe(nc, block, cx, H, D, dbg):
    rA, xb, psA, psB = H["rA"], H["xb"], H["psA"], H["psB"]
    s_h = _ar(H, 0, (128, 8 * T), BF16)
    srow = _ar(H, 32, (1, 2048), F32)
    scale_bc = _p2(H, 0, (128, 2048), F32)
    s_sil = _p2(H, 8320, (128, 2048), F32)

    def xbr(cc, n):
        return xb.ap()[:, cc * T + n * 512: cc * T + (n + 1) * 512]

    loaders = []
    for e in range(-1, NE):
        w1n, w2n, w3n = ("sw1", "sw2", "sw3") if e < 0 else ("ew1", "ew2", "ew3")
        sub = None if e < 0 else e
        for ih in range(2):
            for i in range(8):
                ic = ih * 8 + i
                loaders.append(mk_loader(cx, block, H, D, w1n, ic, sub=sub))
                loaders.append(mk_loader(cx, block, H, D, w3n, ic, sub=sub))
            for c in range(NCH):
                loaders.append(mk_loader_w2(cx, block, H, D, w2n, ih, c,
                                            sub=sub))
    ws = WS(cx, block, H, D, loaders)
    KPE = 48

    for e in range(-1, NE):
        w1n, w2n, w3n = ("sw1", "sw2", "sw3") if e < 0 else ("ew1", "ew2", "ew3")
        sub = None if e < 0 else e
        kbase = (e + 1) * KPE

        @block.vector
        def _(v, e=e):
            cx.wait(v, "act")
            cx.wait(v, "pe")
            if e < 0:
                cx.inc(v.tensor_copy(H["s_sc16"][:], H["s_a16"][:]), "dve")
            else:
                sel3 = H["s_sel"].ap().rearrange("p (t e) -> p t e", e=8)
                cx.chain(v, v.tensor_tensor(H["s_sc16"][:],
                                            sel3[:, :, e:e + 1].squeeze(2),
                                            H["s_b16"][:], ALU.mult))
                cx.inc(v.tensor_copy(H["s_t16"][:], H["s_sc16"][:]), "dve")

        @block.tensor
        def _(tensor):
            cx.wait(tensor, "dve")
            cx.inc(tensor.transpose(psA[0:16, 0:128], H["s_sc16"][:],
                                    H["s_id"][:]), "pe")

        @block.vector
        def _(v):
            cx.wait(v, "pe")
            cx.inc(v.tensor_copy(H["s_str16"][:], psA[0:16, 0:128]), "dve")

        @block.sync
        def _(sync):
            cx.wait(sync, "dve")
            cx.dinc(sync.dma_start(out=srow, in_=H["s_str16"][:]), "xl")
        row_ld = cx.cnt["xl"]

        @block.tensor
        def _(tensor):
            cx.wait(tensor, "xl", row_ld)
            for n in range(4):
                ins = tensor.matmul(psA[:, n * 512:(n + 1) * 512],
                                    H["s_ones"][:],
                                    srow[:, n * 512:(n + 1) * 512],
                                    start=True, stop=True)
            cx.inc(ins, "pe")

        @block.scalar
        def _(scalar):
            cx.wait(scalar, "pe")
            cx.inc(scalar.copy(scale_bc, psA[:, 0:2048]), "act")

        for ih in range(2):
            for i in range(8):
                k0 = kbase + ih * 24 + 2 * i
                ws.mm(k0, psA, xbr)
                ws.mm(k0 + 1, psB, xbr)

                @block.scalar
                def _(scalar):
                    cx.wait(scalar, "pe")
                    cx.wait(scalar, "dve")
                    cx.inc(scalar.activation(s_sil, psA[:, 0:2048], AF.Silu),
                           "act")

                @block.vector
                def _(v, i=i):
                    cx.wait(v, "act")
                    cx.wait(v, "pe")
                    v.tensor_tensor(s_sil, s_sil, scale_bc, ALU.mult)
                    cx.inc(v.tensor_tensor(s_h[:, i * T:(i + 1) * T], s_sil,
                                           psB[:, 0:2048], ALU.mult), "dve")

            for c in range(NCH):
                ps = psA if c % 2 == 0 else psB
                k2i = kbase + ih * 24 + 16 + c

                def shr(cc, n):
                    return s_h[:, cc * T + n * 512: cc * T + (n + 1) * 512]
                ws.mm(k2i, ps, shr)

                @block.vector
                def _(v, c=c, ps=ps):
                    cx.wait(v, "pe")
                    cx.inc(v.tensor_tensor(rA.ap()[:, c * T:(c + 1) * T],
                                           rA.ap()[:, c * T:(c + 1) * T],
                                           ps[:, 0:2048], ALU.add), "dve")

    barrier_all(block, cx)
    tap(block, cx, H, dbg, "F0", rA.ap()[:, 0:2048])


def ln_final(nc, block, cx, H, D, dbg):
    rA, xb = H["rA"], H["xb"]
    tmpf = _ar(H, 0, (128, 2048), F32)

    def mk_src(c):
        return xb.ap()[:, c * T:(c + 1) * T]

    for c in range(NCH):
        @block.vector
        def _(v, c=c):
            if c == 0:
                cx.wait(v, "pe")
                cx.wait(v, "act")
            cx.inc(v.tensor_copy(mk_src(c), rA.ap()[:, c * T:(c + 1) * T]),
                   "dve")
    meanb, rstdb = _stats_rows_bcast(block, cx, H, True, 1e-5, mk_src)

    for c in range(NCH):
        @block.vector
        def _(v, c=c):
            cx.wait(v, "act")
            v.tensor_tensor(tmpf, rA.ap()[:, c * T:(c + 1) * T], meanb,
                            ALU.subtract)
            cx.inc(v.tensor_tensor(rA.ap()[:, c * T:(c + 1) * T], tmpf, rstdb,
                                   ALU.mult), "dve")
    barrier_all(block, cx)


def output_transpose(nc, block, cx, H, D, dbg):
    rA, psA, psB = H["rA"], H["psA"], H["psB"]
    out = H["out"]
    stage = _ar(H, 12, (128, 2048), F32)
    st_done, ev_done, tr_done = {}, {}, {}
    for tt in range(NTT):
        buf = tt % 2

        @block.tensor
        def _(tensor, tt=tt):
            if tt >= 1:
                cx.wait(tensor, "act", ev_done[tt - 1])
            for c in range(NCH):
                half = psA if c < 4 else psB
                ins = tensor.transpose(
                    half[:, (c % 4) * 512:(c % 4) * 512 + 128],
                    rA.ap()[:, c * T + tt * 128: c * T + (tt + 1) * 128],
                    H["s_id"][:])
            tr_done[tt] = cx.inc(ins, "pe")

        @block.scalar
        def _(scalar, tt=tt, buf=buf):
            cx.wait(scalar, "pe", tr_done[tt])
            if tt >= 2:
                cx.wait(scalar, "st", st_done[tt - 2])
            for half, c0 in [(psA, 0), (psB, 4)]:
                view = half[:, 0:2048].rearrange("p (c n) -> p c n", c=4)[
                    :, :, 0:128]
                dst = stage.rearrange("p (b c n) -> p b c n", b=2, c=8)[
                    :, buf, c0:c0 + 4, :]
                ins = scalar.copy(dst, view)
            ev_done[tt] = cx.inc(ins, "act")

        @block.sync
        def _(sync, tt=tt, buf=buf):
            cx.wait(sync, "act", ev_done[tt])
            cx.dinc(sync.dma_start(out=out[tt * 128:(tt + 1) * 128, :],
                                   in_=stage[:, buf * 1024:(buf + 1) * 1024]),
                    "st")
            st_done[tt] = cx.cnt["st"]

    @block.sync
    def _(sync):
        cx.wait(sync, "st")


def kernel(**inputs):
    w = _prep(inputs)
    consts = _host_consts()
    x_full = np.asarray(inputs["L0_attn_x"])
    shared = dict(w)
    for k, v in consts.items():
        shared["c_" + k] = v

    nc, dbg = build_program(debug=[(n, _DBG_SHAPES[n]) for n in DEBUG])
    in_maps = []
    for b in range(8):
        m = dict(shared)
        m["x"] = np.ascontiguousarray(x_full[b])
        in_maps.append(m)
    res = run_bass_kernel_spmd(nc, in_maps, core_ids=list(range(8)))
    outs = np.stack([np.ascontiguousarray(r["out"]) for r in res.results],
                    axis=0)
    kernel._last = res
    return outs


_DBG_SHAPES = {
    "r_eq": (128, 128),
    "r_pmax": (128, 16),
    "r_ltm": (128, 128),
    "r_lgt": (8, 2048),
    "a2_kv": (128, 2048),
    "a2_q2": (128, 2048),
    "a2_k2": (128, 2048),
    "a2_v2": (128, 2080),
    "a2_neg": (128, 1),
    "a2_yT0": (128, 2048),
    "a1_sm": (128, 8),
    "a1_m2": (2, 8),
    "a1_neg": (128, 1),
    "a1_yl": (128, 2048),
    "a1_pt": (128, 2048),
    "a1_v4": (128, 2112),
    "xT0": (128, 2048),
    "a1_qfk": (128, 2048),
    "a1_k4": (128, 2048),
    "a1_yT0": (128, 2048),
    "x1pre0": (128, 2048),
    "x1_0": (128, 2048),
    "xn1_0": (128, 2048),
    "yb0": (128, 2048),
    "yn0": (128, 2048),
    "sel": (128, 128),
    "p16": (128, 16),
    "F0": (128, 2048),
}


# version 03
# speedup vs baseline: 1.1261x; 1.0799x over previous
import contextlib
import math

import numpy as np
import ml_dtypes

import concourse.bass as bass
import concourse.mybir as mybir
from concourse.bass_utils import run_bass_kernel_spmd

F32 = mybir.dt.float32
BF16 = mybir.dt.bfloat16
AF = mybir.ActivationFunctionType
ALU = mybir.AluOpType
AX = mybir.AxisListType

T = 2048
C = 1024
NCH = 8
NTT = 16
NQS = 4
LAT = 32
HD = 64
I_FF = 2048
NE = 8

EXP_SC1 = 1.0 / math.sqrt(LAT)
EXP_SC2 = 1.0 / math.sqrt(HD)

DEBUG = []
BF = ml_dtypes.bfloat16


def _host_consts():
    c = {}
    inv = 1.0 / 10000.0 ** (np.arange(0, LAT, 2, dtype=np.float64) / LAT)
    t = np.arange(T, dtype=np.float64)
    fr = np.outer(t, inv)
    cos1 = np.repeat(np.cos(fr), 2, axis=-1)[:, :LAT]
    sin1 = np.repeat(np.sin(fr), 2, axis=-1)[:, :LAT]
    c["tab1_c"] = np.tile(cos1.T.astype(np.float32), (4, 1)).astype(BF)
    c["tab1_s"] = np.tile(sin1.T.astype(np.float32), (4, 1)).astype(BF)
    R32 = np.zeros((LAT, LAT), np.float32)
    for i in range(LAT // 2):
        R32[2 * i + 1, 2 * i] = -1.0
        R32[2 * i, 2 * i + 1] = 1.0
    c["R1"] = np.kron(np.eye(4, dtype=np.float32), R32).astype(BF)
    inv2 = 1.0 / 100000.0 ** (np.arange(0, HD, 2, dtype=np.float64) / HD)
    fr2 = np.outer(t, inv2)
    C64 = np.concatenate([np.cos(fr2), np.cos(fr2)], axis=1).T.astype(np.float32)
    S64 = np.concatenate([np.sin(fr2), np.sin(fr2)], axis=1).T.astype(np.float32)
    c["tab2_c"] = np.tile(C64, (2, 1)).astype(BF)
    c["tab2_s"] = np.tile(S64, (2, 1)).astype(BF)
    R64 = np.zeros((HD, HD), np.float32)
    for d in range(32):
        R64[d + 32, d] = -1.0
        R64[d, d + 32] = 1.0
    c["R2"] = np.kron(np.eye(2, dtype=np.float32), R64).astype(BF)
    kk = np.arange(128)[:, None]
    qq = np.arange(512)[None, :]
    for j, d in enumerate([0, 128, 256, 384]):
        c[f"mask{j}"] = ((d + kk) <= qq).astype(BF)
    c["ident"] = np.eye(128, dtype=np.float32)
    c["ones_row"] = np.ones((1, 128), np.float32)
    c["ones_col_bf"] = np.ones((128, 1), np.float32).astype(BF)
    c["ones_col_f"] = np.ones((128, 1), np.float32)
    return c


def _prep(inp):
    w = {}
    f = np.ascontiguousarray
    w["caw"] = f(np.asarray(inp["L0_attn_c_attn_w"]).T)
    w["cab"] = f(np.asarray(inp["L0_attn_c_attn_b"]).reshape(24, 128).T)
    q2l = np.asarray(inp["L0_attn_q2l_w"])
    kv2l = np.asarray(inp["L0_attn_kv2l_w"])
    blk = np.zeros((256, 128), np.float32)
    for h in range(4):
        blk[h * 64:(h + 1) * 64, h * 32:(h + 1) * 32] = q2l.T
    w["W4q"] = f(blk)
    blk = np.zeros((256, 128), np.float32)
    for h in range(4):
        blk[h * 64:(h + 1) * 64, h * 32:(h + 1) * 32] = kv2l.T
    w["W4kv"] = f(blk)
    w["q2l_b4"] = f(np.tile(np.asarray(inp["L0_attn_q2l_b"]), 4).reshape(128, 1))
    w["kv2l_b4"] = f(np.tile(np.asarray(inp["L0_attn_kv2l_b"]), 4).reshape(128, 1))
    row = np.zeros((1, 132), np.float32)
    for h in range(4):
        row[0, h * 33:h * 33 + 32] = np.asarray(inp["L0_attn_kv2l_b"])
    w["kv2l_brow"] = row
    blkq = np.zeros((128, 128), np.float32)
    for h in range(4):
        blkq[h * 32:(h + 1) * 32, h * 32:(h + 1) * 32] = np.asarray(
            inp["L0_attn_wqk_w"]).T
    w["wqk4"] = f(blkq)
    w["wqk_b4"] = f(np.tile(np.asarray(inp["L0_attn_wqk_b"]), 4).reshape(128, 1))
    l2oT = np.asarray(inp["L0_attn_l2o_w"]).T
    w["l2o4"] = f(np.tile(l2oT, (4, 1)))
    w["l2o_b2"] = f(np.tile(np.asarray(inp["L0_attn_l2o_b"]), 2).reshape(128, 1))
    w["cpw"] = f(np.asarray(inp["L0_attn_cproj_w"]).T)
    w["cpb"] = f(np.asarray(inp["L0_attn_cproj_b"]).reshape(8, 128).T)
    w["ln1w"] = f(np.asarray(inp["L0_ffn_ln1_w"]).reshape(8, 128).T)
    w["ln2w"] = f(np.asarray(inp["L0_ffn_ln2_w"]).reshape(8, 128).T)
    w["Wq"] = f(np.asarray(inp["L0_ffn_Wq"]))
    w["Wkv"] = f(np.asarray(inp["L0_ffn_Wkv"]))
    w["Wk2"] = f(np.asarray(inp["L0_ffn_Wk_up"]).reshape(8, 2, 256, 64)
                 .transpose(0, 2, 1, 3).reshape(8, 256, 128))
    w["Wv2"] = f(np.asarray(inp["L0_ffn_Wv_up"]).reshape(8, 2, 256, 64)
                 .transpose(0, 2, 1, 3).reshape(8, 256, 128))
    w["Wo"] = f(np.asarray(inp["L0_ffn_Wo"]))
    w["Wr"] = f(np.asarray(inp["L0_ffn_Wr"]))
    w["r_bias"] = f(np.asarray(inp["L0_ffn_r_bias"]).reshape(8, 1))
    w["sw1"] = f(np.asarray(inp["L0_ffn_sw1"]))
    w["sw2"] = f(np.asarray(inp["L0_ffn_sw2"]))
    w["sw3"] = f(np.asarray(inp["L0_ffn_sw3"]))
    w["ew1"] = f(np.asarray(inp["L0_ffn_ew1"]))
    w["ew2"] = f(np.asarray(inp["L0_ffn_ew2"]))
    w["ew3"] = f(np.asarray(inp["L0_ffn_ew3"]))
    return w


class Ctx:
    def __init__(self):
        self.cnt = dict(pe=0, act=0, dve=0, gp=0, wl=0, xl=0, st=0)
        self.mark = {}

    def inc(self, ins, name):
        self.cnt[name] += 1
        ins.then_inc(self.sem[name], 1)
        return self.cnt[name]

    def dinc(self, ins, name):
        self.cnt[name] += 1
        ins.then_inc(self.sem[name], 16)
        return self.cnt[name]

    def chain(self, eng, ins, name="dve"):
        c = self.inc(ins, name)
        eng.wait_ge(self.sem[name], c)
        return c

    def wait(self, eng, name, val=None):
        v = self.cnt[name] if val is None else val
        if v <= 0:
            return
        if name in ("wl", "xl", "st"):
            eng.wait_ge(self.sem[name], 16 * v)
        else:
            eng.wait_ge(self.sem[name], v)


def _ar(H, off_kb, shape, dt=BF16, p0=0):
    e0 = int(off_kb * 1024) // 2
    n = shape[1] * (2 if dt == F32 else 1)
    v = H["AR"].ap()[:, e0:e0 + n]
    if dt == F32:
        v = v.bitcast(F32)
    return v[p0:p0 + shape[0], :]


def _p2(H, off_b, shape, dt=BF16, p0=0):
    e0 = off_b // 2
    n = shape[1] * (2 if dt == F32 else 1)
    v = H["P2"].ap()[:, e0:e0 + n]
    if dt == F32:
        v = v.bitcast(F32)
    return v[p0:p0 + shape[0], :]


def build_program(debug=()):
    nc = bass.Bass("TRN2", target_bir_lowering=False)
    cx = Ctx()
    consts = _host_consts()
    D = {}

    def din(name, shape, dt=F32):
        D[name] = nc.dram_tensor(name, list(shape), dt, kind="ExternalInput").ap()

    din("x", (T, C))
    din("caw", (C, 3 * C)); din("cab", (128, 24))
    din("W4q", (256, 128)); din("W4kv", (256, 128))
    din("q2l_b4", (128, 1)); din("kv2l_b4", (128, 1)); din("kv2l_brow", (1, 132))
    din("wqk4", (128, 128)); din("wqk_b4", (128, 1))
    din("l2o4", (128, 64)); din("l2o_b2", (128, 1))
    din("cpw", (C, C)); din("cpb", (128, 8))
    din("ln1w", (128, 8)); din("ln2w", (128, 8))
    din("Wq", (C, C)); din("Wkv", (C, 256))
    din("Wk2", (8, 256, 128)); din("Wv2", (8, 256, 128))
    din("Wo", (C, C)); din("Wr", (C, 8)); din("r_bias", (8, 1))
    din("sw1", (C, I_FF)); din("sw2", (I_FF, C)); din("sw3", (C, I_FF))
    din("ew1", (NE, C, I_FF)); din("ew2", (NE, I_FF, C)); din("ew3", (NE, C, I_FF))
    for k, v in consts.items():
        din("c_" + k, v.shape, BF16 if v.dtype == BF else F32)

    out = nc.dram_tensor("out", [T, C], F32, kind="ExternalOutput").ap()
    dbg = {}
    for name, shape in debug:
        dbg[name] = nc.dram_tensor("dbg_" + name, list(shape), F32,
                                   kind="ExternalOutput").ap()

    st = contextlib.ExitStack()
    with st:
        def sb(name, p, f, dt):
            return st.enter_context(nc.sbuf_tensor(name, [p, f], dt))

        rA = sb("rA", 128, NCH * T, F32)
        xb = sb("xb", 128, NCH * T, BF16)
        yT = sb("yT", 128, NCH * T, BF16)
        AR = sb("AR", 128, 23552, BF16)
        P2 = sb("P2", 128, 8320, BF16)
        wsl = sb("wsl", 128, 4096, BF16)
        s_W4q = sb("s_W4q", 128, 256, BF16)
        s_W4kv = sb("s_W4kv", 128, 256, BF16)
        s_wqk4 = sb("s_wqk4", 128, 128, BF16)
        s_l2o4 = sb("s_l2o4", 128, 64, BF16)
        s_R1 = sb("s_R1", 128, 128, BF16)
        s_R2 = sb("s_R2", 128, 128, BF16)
        s_Wr = sb("s_Wr", 128, 64, BF16)
        s_id = sb("s_id", 128, 128, F32)
        s_ones = sb("s_ones", 1, 128, F32)
        s_onesb = sb("s_onesb", 128, 1, BF16)
        s_onesf = sb("s_onesf", 128, 1, F32)
        s_bias = sb("s_bias", 128, 48, F32)
        s_sm = sb("s_sm", 128, 8, F32)
        s_rb = sb("s_rb", 8, 1, F32)
        s_bv4 = sb("s_bv4", 128, 132, F32)
        s_brow = sb("s_brow", 1, 132, F32)
        s_str16 = sb("s_str16", 16, 128, F32)
        s_ltm = sb("s_ltm", 128, 128, F32)
        s_sel = sb("s_sel", 128, 128, F32)
        s_lx = sb("s_lx", 128, 128, F32)
        s_p16 = sb("s_p16", 128, 16, F32)
        s_t16 = sb("s_t16", 128, 16, F32)
        s_a16 = sb("s_a16", 128, 16, F32)
        s_b16 = sb("s_b16", 128, 16, F32)
        s_r16 = sb("s_r16", 128, 16, F32)
        s_sc16 = sb("s_sc16", 128, 16, F32)
        s_u16 = sb("s_u16", 128, 16, F32)
        s_m2 = sb("s_m2", 2, 128, F32)

        psA = st.enter_context(nc.psum_tensor("psA", [128, 2048], F32))
        psB = st.enter_context(nc.psum_tensor("psB", [128, 2048], F32))

        sem = {}
        for name in ["pe", "act", "dve", "gp", "wl", "xl", "st"]:
            sem[name] = st.enter_context(nc.semaphore("s_" + name))
        cx.sem = sem

        block = st.enter_context(nc.Block())
        H = dict(locals())
        H["out"] = out

        ph0(nc, block, cx, H, D, dbg)
        attn1(nc, block, cx, H, D, dbg)
        ln1_rms1(nc, block, cx, H, D, dbg)
        attn2(nc, block, cx, H, D, dbg)
        rms2(nc, block, cx, H, D, dbg)
        router(nc, block, cx, H, D, dbg)
        moe(nc, block, cx, H, D, dbg)
        ln_final(nc, block, cx, H, D, dbg)
        output_transpose(nc, block, cx, H, D, dbg)

    return nc, dbg


def tap(block, cx, H, dbg, name, src_ap):
    if name not in dbg:
        return
    d = dbg[name]

    @block.gpsimd
    def _(gp):
        for k in ["pe", "act", "dve"]:
            cx.wait(gp, k)
        cx.dinc(gp.dma_start(out=d, in_=src_ap), "wl")
        cx.wait(gp, "wl")


def barrier_all(block, cx):
    for engname in ["tensor", "scalar", "vector", "gpsimd", "sync"]:
        @getattr(block, engname)
        def _(e):
            for k in ["pe", "act", "dve", "gp", "wl", "xl"]:
                cx.wait(e, k)


def wslot(H, slot):
    return H["wsl"].ap()[:, slot * 1024:(slot + 1) * 1024]


class WS:
    def __init__(self, cx, block, H, D, loaders, depth=3):
        self.cx, self.block, self.H, self.D = cx, block, H, D
        self.loaders = loaders
        self.nxt = 0
        self.wl = {}
        for _ in range(min(depth, len(loaders))):
            self.emit()

    def emit(self):
        k = self.nxt
        if k < len(self.loaders):
            self.wl[k] = self.loaders[k](k % 4)
            self.nxt += 1

    def mm(self, k, psum, rhs, kchunks=8, act_val=None, dve_val=None):
        cx, block, H = self.cx, self.block, self.H
        slot = k % 4
        wl_cnt = self.wl[k]

        @block.tensor
        def _(tensor):
            cx.wait(tensor, "wl", wl_cnt)
            if act_val != 0:
                cx.wait(tensor, "act", act_val)
            if dve_val != 0:
                cx.wait(tensor, "dve", dve_val)
            w = wslot(H, slot)
            for cc in range(kchunks):
                for n in range(4):
                    ins = tensor.matmul(psum[:, n * 512:(n + 1) * 512],
                                        w[:, cc * 128:(cc + 1) * 128],
                                        rhs(cc, n),
                                        start=(cc == 0),
                                        stop=(cc == kchunks - 1))
            cx.mark[("slot", slot)] = cx.inc(ins, "pe")
        self.emit()


def mk_loader(cx, block, H, D, wname, o, sub=None, kchunks=8):
    def load(slot):
        @block.gpsimd
        def _(gp):
            rel = cx.mark.get(("slot", slot))
            if rel is not None:
                cx.wait(gp, "pe", rel)
            s = D[wname] if sub is None else D[wname][sub]
            ap = s.rearrange("(cc p) o -> p cc o", p=128)[
                :, 0:kchunks, o * 128:(o + 1) * 128]
            dst = wslot(H, slot).rearrange("p (cc o) -> p cc o", cc=8)[
                :, 0:kchunks, :]
            cx.dinc(gp.dma_start(out=dst, in_=ap), "wl")
        return cx.cnt["wl"]
    return load


def mk_loader_w2(cx, block, H, D, wname, ih, c, sub=None):
    def load(slot):
        @block.gpsimd
        def _(gp):
            rel = cx.mark.get(("slot", slot))
            if rel is not None:
                cx.wait(gp, "pe", rel)
            s = D[wname] if sub is None else D[wname][sub]
            ap = s.rearrange("(cc p) o -> p cc o", p=128)[
                :, ih * 8:(ih + 1) * 8, c * 128:(c + 1) * 128]
            dst = wslot(H, slot).rearrange("p (cc o) -> p cc o", cc=8)
            cx.dinc(gp.dma_start(out=dst, in_=ap), "wl")
        return cx.cnt["wl"]
    return load


def ph0(nc, block, cx, H, D, dbg):
    rA, xb, psA, psB = H["rA"], H["xb"], H["psA"], H["psB"]
    stage = _ar(H, 12, (128, 2048), F32)

    @block.sync
    def _(sync):
        for nm, dst in [("c_ident", H["s_id"]), ("c_ones_row", H["s_ones"]),
                        ("c_R1", H["s_R1"]), ("c_R2", H["s_R2"]),
                        ("c_ones_col_bf", H["s_onesb"]), ("r_bias", H["s_rb"]),
                        ("c_ones_col_f", H["s_onesf"])]:
            cx.dinc(sync.dma_start(out=dst[:], in_=D[nm]), "xl")
        msk = _ar(H, 28, (128, 2048), BF16)
        for j in range(4):
            cx.dinc(sync.dma_start(out=msk[:, j * 512:(j + 1) * 512],
                                   in_=D[f"c_mask{j}"]), "xl")
        cx.dinc(sync.dma_start(out=H["s_bias"][:, 0:24], in_=D["cab"]), "xl")
        cx.dinc(sync.dma_start(out=H["s_bias"][:, 24:32], in_=D["cpb"]), "xl")
        cx.dinc(sync.dma_start(out=H["s_bias"][:, 32:40], in_=D["ln1w"]), "xl")
        cx.dinc(sync.dma_start(out=H["s_bias"][:, 40:48], in_=D["ln2w"]), "xl")
        for i, nm in enumerate(["q2l_b4", "kv2l_b4", "wqk_b4", "l2o_b2"]):
            cx.dinc(sync.dma_start(out=H["s_sm"][:, i:i + 1], in_=D[nm]), "xl")
        cx.dinc(sync.dma_start(out=H["s_brow"][:], in_=D["kv2l_brow"]), "xl")
        tabs = _ar(H, 0, (128, 2 * T), BF16)
        for i, nm in enumerate(["tab1_c", "tab1_s"]):
            cx.dinc(sync.dma_start(out=tabs[:, i * T:(i + 1) * T],
                                   in_=D["c_" + nm]), "xl")
        cx.dinc(sync.dma_start(out=stage[:, 0:1024], in_=D["x"][0:128, :]), "xl")
    xl_base = cx.cnt["xl"]

    @block.gpsimd
    def _(gp):
        cx.dinc(gp.dma_start(out=H["s_W4q"][:].rearrange("p (c m) -> p c m", c=2),
                             in_=D["W4q"].rearrange("(c p) m -> p c m", p=128)),
                "wl")
        cx.dinc(gp.dma_start(out=H["s_W4kv"][:].rearrange("p (c m) -> p c m", c=2),
                             in_=D["W4kv"].rearrange("(c p) m -> p c m", p=128)),
                "wl")
        cx.dinc(gp.dma_start(out=H["s_wqk4"][:], in_=D["wqk4"]), "wl")
        cx.dinc(gp.dma_start(out=H["s_l2o4"][:], in_=D["l2o4"]), "wl")
        cx.dinc(gp.dma_start(out=H["s_Wr"][:].rearrange("p (c m) -> p c m", c=8),
                             in_=D["Wr"].rearrange("(c p) m -> p c m", p=128)),
                "wl")

    @block.tensor
    def _(tensor):
        cx.wait(tensor, "xl")
        cx.inc(tensor.matmul(psB[:, 0:132], H["s_ones"][:], H["s_brow"][:],
                             start=True, stop=True), "pe")

    @block.vector
    def _(v):
        cx.wait(v, "pe")
        cx.inc(v.tensor_copy(H["s_bv4"][:], psB[:, 0:132]), "dve")

    tr_done, ev_done = {}, {}
    for tt in range(NTT):
        buf = tt % 2

        @block.sync
        def _(sync, tt=tt, buf=buf):
            if tt + 1 < NTT:
                if tt >= 1:
                    cx.wait(sync, "pe", tr_done[tt - 1])
                cx.dinc(sync.dma_start(
                    out=stage[:, (1 - buf) * 1024:(2 - buf) * 1024],
                    in_=D["x"][(tt + 1) * 128:(tt + 2) * 128, :]), "xl")

        @block.tensor
        def _(tensor, tt=tt, buf=buf):
            cx.wait(tensor, "xl", xl_base + tt)
            cx.wait(tensor, "dve")
            if tt > 0:
                cx.wait(tensor, "act", ev_done[tt - 1])
            for c in range(NCH):
                half = psA if c < 4 else psB
                ins = tensor.transpose(
                    half[:, (c % 4) * 512:(c % 4) * 512 + 128],
                    stage[:, buf * 1024 + c * 128: buf * 1024 + (c + 1) * 128],
                    H["s_id"][:])
            tr_done[tt] = cx.inc(ins, "pe")

        @block.scalar
        def _(scalar, tt=tt):
            cx.wait(scalar, "pe", tr_done[tt])
            for half, c0 in [(psA, 0), (psB, 4)]:
                view = half[:, 0:2048].rearrange("p (c n) -> p c n", c=4)[
                    :, :, 0:128]
                dst = rA.ap().rearrange("p (c t) -> p c t", c=8)[
                    :, c0:c0 + 4, tt * 128:(tt + 1) * 128]
                ins = scalar.copy(dst, view)
            ev_done[tt] = cx.inc(ins, "act")

        @block.vector
        def _(v, tt=tt):
            cx.wait(v, "act", ev_done[tt])
            dst = xb.ap().rearrange("p (c t) -> p c t", c=8)[
                :, :, tt * 128:(tt + 1) * 128]
            src = rA.ap().rearrange("p (c t) -> p c t", c=8)[
                :, :, tt * 128:(tt + 1) * 128]
            cx.inc(v.tensor_copy(dst, src), "dve")

    barrier_all(block, cx)
    tap(block, cx, H, dbg, "xT0", rA.ap()[:, 0:2048])


def _stabilizer(block, cx, H, a_src, b_src, coef, dst_col):
    psA = H["psA"]

    @block.vector
    def _(v):
        cx.wait(v, "act")
        cx.wait(v, "dve")
        v.tensor_reduce(H["s_sm"][:, 6:7], a_src, AX.X, ALU.max,
                        apply_absolute_value=True)
        cx.inc(v.tensor_reduce(H["s_sm"][:, 7:8], b_src, AX.X, ALU.max,
                               apply_absolute_value=True), "dve")

    @block.tensor
    def _(tensor):
        cx.wait(tensor, "dve")
        cx.inc(tensor.transpose(psA[0:2, 0:128], H["s_sm"][:, 6:8],
                                H["s_id"][:]), "pe")

    @block.vector
    def _(v):
        cx.wait(v, "pe")
        cx.inc(v.tensor_reduce(H["s_m2"][:, 0:1], psA[0:2, 0:128],
                               AX.X, ALU.max), "dve")

    @block.scalar
    def _(scalar):
        cx.wait(scalar, "dve")
        cx.inc(scalar.activation(H["s_m2"][0:2, 1:2], H["s_m2"][0:2, 0:1],
                                 AF.Ln), "act")

    @block.tensor
    def _(tensor):
        cx.wait(tensor, "act")
        cx.inc(tensor.matmul(psA[0:1, 131:132], H["s_onesf"][0:2, 0:1],
                             H["s_m2"][0:2, 1:2], start=True, stop=True), "pe")

    @block.scalar
    def _(scalar):
        cx.wait(scalar, "pe")
        cx.inc(scalar.activation(H["s_m2"][0:1, 2:3], psA[0:1, 131:132],
                                 AF.Exp), "act")

    @block.vector
    def _(v):
        cx.wait(v, "act")
        cx.inc(v.tensor_scalar(H["s_m2"][0:1, 3:4], H["s_m2"][0:1, 2:3],
                               -coef, None, ALU.mult), "dve")

    @block.tensor
    def _(tensor):
        cx.wait(tensor, "dve")
        cx.inc(tensor.matmul(psA[:, 130:131], H["s_ones"][:],
                             H["s_m2"][0:1, 3:4], start=True, stop=True), "pe")

    @block.vector
    def _(v):
        cx.wait(v, "pe")
        cx.inc(v.tensor_copy(H["s_sm"][:, dst_col:dst_col + 1],
                             psA[:, 130:131]), "dve")


def attn1(nc, block, cx, H, D, dbg):
    rA, xb, yT, psA, psB = H["rA"], H["xb"], H["yT"], H["psA"], H["psB"]
    tab_c = _ar(H, 0, (128, T), BF16)
    tab_s = _ar(H, 4, (128, T), BF16)
    scr = _ar(H, 8, (128, 2048), BF16)
    qlat = _ar(H, 12, (128, 2048), F32)
    vb2 = _ar(H, 12, (128, 2048), BF16)
    tmpf = _ar(H, 20, (128, 2048), F32)
    msk = _ar(H, 28, (128, 2048), BF16)
    ylab = _ar(H, 32, (128, 2048), BF16)
    lnrow = _ar(H, 40, (1, 512), F32)
    recr = _ar(H, 42, (1, 512), F32)
    recb = _ar(H, 44, (64, 512), F32)
    qfk = _p2(H, 0, (128, 2048), BF16)
    k4 = _p2(H, 4096, (128, 2048), BF16)
    v4 = _p2(H, 8192, (128, 2112), BF16)
    pt = _p2(H, 12416, (128, 2048), BF16)

    def xbr(cc, n):
        return xb.ap()[:, cc * T + n * 512: cc * T + (n + 1) * 512]

    loaders = []
    for g in range(4):
        for sec in (0, 8, 16):
            for sub in range(2):
                loaders.append(mk_loader(cx, block, H, D, "caw",
                                         sec + 2 * g + sub))
    for o in range(NCH):
        loaders.append(mk_loader(cx, block, H, D, "cpw", o))
    ws = WS(cx, block, H, D, loaders)

    for g in range(4):
        for path, w4, bias_i in [("q", H["s_W4q"], 0), ("k", H["s_W4kv"], 1)]:
            sec = 0 if path == "q" else 8
            for sub in range(2):
                o = sec + 2 * g + sub
                ws.mm(g * 6 + (0 if path == "q" else 2) + sub, psA, xbr)

                @block.scalar
                def _(scalar, o=o):
                    cx.wait(scalar, "pe")
                    cx.inc(scalar.activation(scr[:], psA[:, 0:2048], AF.Identity,
                                             bias=H["s_bias"][:, o:o + 1]),
                           "act")

                @block.tensor
                def _(tensor, w4=w4, sub=sub):
                    cx.wait(tensor, "act")
                    for n in range(4):
                        ins = tensor.matmul(psB[:, n * 512:(n + 1) * 512],
                                            w4[:, sub * 128:(sub + 1) * 128],
                                            scr[:, n * 512:(n + 1) * 512],
                                            start=(sub == 0), stop=(sub == 1))
                    cx.inc(ins, "pe")

            @block.scalar
            def _(scalar, bias_i=bias_i):
                cx.wait(scalar, "pe")
                cx.inc(scalar.activation(qlat, psB[:, 0:2048], AF.Identity,
                                         bias=H["s_sm"][:, bias_i:bias_i + 1]),
                       "act")

            @block.vector
            def _(v):
                cx.wait(v, "act")
                cx.inc(v.tensor_copy(scr[:], qlat), "dve")

            @block.tensor
            def _(tensor):
                cx.wait(tensor, "dve")
                for n in range(4):
                    ins = tensor.matmul(psA[:, n * 512:(n + 1) * 512],
                                        H["s_R1"][:],
                                        scr[:, n * 512:(n + 1) * 512],
                                        start=True, stop=True)
                cx.inc(ins, "pe")

            @block.vector
            def _(v, path=path):
                cx.wait(v, "pe")
                v.tensor_tensor(tmpf, psA[:, 0:2048], tab_s, ALU.mult)
                cx.wait(v, "act")
                v.tensor_tensor(qlat, qlat, tab_c, ALU.mult)
                dest = scr[:] if path == "q" else k4
                cx.inc(v.tensor_tensor(dest, tmpf, qlat, ALU.add), "dve")

            if path == "q":
                @block.tensor
                def _(tensor):
                    cx.wait(tensor, "dve")
                    for n in range(4):
                        ins = tensor.matmul(psB[:, n * 512:(n + 1) * 512],
                                            H["s_wqk4"][:],
                                            scr[:, n * 512:(n + 1) * 512],
                                            start=True, stop=True)
                    cx.inc(ins, "pe")

                @block.scalar
                def _(scalar):
                    cx.wait(scalar, "pe")
                    cx.inc(scalar.activation(qfk, psB[:, 0:2048], AF.Identity,
                                             bias=H["s_sm"][:, 2:3]), "act")

        for sub in range(2):
            o = 16 + 2 * g + sub
            ws.mm(g * 6 + 4 + sub, psA, xbr)

            @block.scalar
            def _(scalar, sub=sub, o=o):
                cx.wait(scalar, "pe")
                dst = scr[:] if sub == 0 else vb2[:]
                cx.inc(scalar.activation(dst, psA[:, 0:2048], AF.Identity,
                                         bias=H["s_bias"][:, o:o + 1]), "act")

        @block.tensor
        def _(tensor):
            cx.wait(tensor, "act")
            cx.wait(tensor, "dve")
            for tt in range(NTT):
                for s2, buf in [(0, scr[:]), (1, vb2[:])]:
                    ins = tensor.matmul(
                        psB[:, tt * 128:(tt + 1) * 128],
                        buf[:, tt * 128:(tt + 1) * 128],
                        H["s_W4kv"][:, s2 * 128:(s2 + 1) * 128],
                        start=(s2 == 0), stop=(s2 == 1))
            cx.inc(ins, "pe")

        @block.vector
        def _(v):
            v.memset(v4.rearrange("p (tt h l) -> p tt h l", tt=16, h=4)[
                :, :, :, 32:33], 1.0)
            cx.wait(v, "pe")
            src = psB[:, 0:2048].rearrange("p (tt h l) -> p tt h l", tt=16, h=4)
            dst = v4.rearrange("p (tt h l) -> p tt h l", tt=16, h=4)[
                :, :, :, 0:32]
            b = H["s_bv4"].ap().rearrange("p (h l) -> p h l", h=4)[
                :, :, 0:32].unsqueeze(1).broadcast_to([128, 16, 4, 32])
            cx.inc(v.tensor_tensor(dst, src, b, ALU.add), "dve")

        _stabilizer(block, cx, H, qfk, k4, 1.02 * math.sqrt(LAT), 4)
        negM = H["s_sm"][:, 4:5]
        row40 = _ar(H, 36, (1, 2048), F32)

        for qs in range(NQS):
            kmax = qs * 4 + 3
            for kt in range(kmax + 1):
                diag = kt - qs * 4
                for hp in (0, 1):
                    @block.tensor
                    def _(tensor, qs=qs, kt=kt, hp=hp, g=g):
                        if kt == 0:
                            cx.wait(tensor, "act")
                            cx.wait(tensor, "dve")
                        m = cx.mark.get(("exp", g, hp))
                        if m is not None:
                            cx.wait(tensor, "act", m)
                        for h in (2 * hp, 2 * hp + 1):
                            ins = tensor.matmul(
                                psA[:, h * 512:(h + 1) * 512],
                                k4[32 * h:32 * h + 32, kt * 128:(kt + 1) * 128],
                                qfk[32 * h:32 * h + 32,
                                    qs * 512:(qs + 1) * 512],
                                start=True, stop=True,
                                tile_position=(32 * h, 0))
                        cx.mark[("S", g, hp)] = cx.inc(ins, "pe")

                    @block.scalar
                    def _(scalar, g=g, hp=hp):
                        cx.wait(scalar, "pe", cx.mark[("S", g, hp)])
                        m = cx.mark.get(("pv", g, hp))
                        if m is not None:
                            cx.wait(scalar, "pe", m)
                        m = cx.mark.get(("msk", g, hp))
                        if m is not None:
                            cx.wait(scalar, "dve", m)
                        for h in (2 * hp, 2 * hp + 1):
                            ins = scalar.activation(
                                pt[:, h * 512:(h + 1) * 512],
                                psA[:, h * 512:(h + 1) * 512],
                                AF.Exp, bias=negM, scale=EXP_SC1)
                        cx.mark[("exp", g, hp)] = cx.inc(ins, "act")

                    if diag >= 0:
                        @block.vector
                        def _(v, diag=diag, g=g, hp=hp):
                            cx.wait(v, "act", cx.mark[("exp", g, hp)])
                            for h in (2 * hp, 2 * hp + 1):
                                ins = v.tensor_tensor(
                                    pt[:, h * 512:(h + 1) * 512],
                                    pt[:, h * 512:(h + 1) * 512],
                                    msk[:, diag * 512:(diag + 1) * 512],
                                    ALU.mult)
                            cx.mark[("msk", g, hp)] = cx.inc(ins, "dve")

                    @block.tensor
                    def _(tensor, qs=qs, kt=kt, diag=diag, g=g, hp=hp,
                          kmax=kmax):
                        if diag >= 0:
                            cx.wait(tensor, "dve", cx.mark[("msk", g, hp)])
                        else:
                            cx.wait(tensor, "act", cx.mark[("exp", g, hp)])
                        if kt == 0:
                            m = cx.mark.get(("rec", g))
                            if m is not None:
                                cx.wait(tensor, "dve", m)
                            m = cx.mark.get(("tailLn", g))
                            if m is not None:
                                cx.wait(tensor, "act", m)
                        for h in (2 * hp, 2 * hp + 1):
                            ins = tensor.matmul(
                                psB[0:33, h * 512:(h + 1) * 512],
                                v4[:, kt * 132 + h * 33: kt * 132 + h * 33 + 33],
                                pt[:, h * 512:(h + 1) * 512],
                                start=(kt == 0), stop=(kt == kmax))
                        cx.mark[("pv", g, hp)] = cx.inc(ins, "pe")

            @block.scalar
            def _(scalar, g=g):
                cx.wait(scalar, "pe", cx.mark[("pv", g, 1)])
                cx.mark[("tailLn", g)] = cx.inc(
                    scalar.activation(row40, psB[32:33, 0:2048], AF.Ln), "act")

            @block.tensor
            def _(tensor, g=g):
                cx.wait(tensor, "act", cx.mark[("tailLn", g)])
                for h in range(4):
                    ins = tensor.matmul(psA[64:96, h * 512:(h + 1) * 512],
                                        H["s_ones"][:, 0:32],
                                        row40[0:1, h * 512:(h + 1) * 512],
                                        start=True, stop=True)
                cx.mark[("tailB", g)] = cx.inc(ins, "pe")

            @block.scalar
            def _(scalar, g=g):
                cx.wait(scalar, "pe", cx.mark[("tailB", g)])
                cx.mark[("tailA", g)] = cx.inc(
                    scalar.activation(tmpf[0:32, :], psA[64:96, 0:2048],
                                      AF.Exp, scale=-1.0), "act")

            @block.vector
            def _(v, qs=qs, g=g):
                cx.wait(v, "act", cx.mark[("tailA", g)])
                for h in range(4):
                    ins = v.tensor_tensor(
                        ylab[32 * h:32 * h + 32, qs * 512:(qs + 1) * 512],
                        psB[0:32, h * 512:(h + 1) * 512],
                        tmpf[0:32, h * 512:(h + 1) * 512], ALU.mult)
                m = cx.inc(ins, "dve")
                cx.mark[("yl", g)] = m
                cx.mark[("rec", g)] = m

        for qs in range(NQS):
            @block.tensor
            def _(tensor, qs=qs, g=g):
                cx.wait(tensor, "dve", cx.mark[("yl", g)])
                if qs > 0:
                    cx.wait(tensor, "act", cx.mark[("yT", g)])
                for h in range(4):
                    ins = tensor.matmul(
                        psA[0:64, h * 512:(h + 1) * 512],
                        H["s_l2o4"][32 * h:32 * h + 32, :],
                        ylab[32 * h:32 * h + 32, qs * 512:(qs + 1) * 512],
                        start=True, stop=True, tile_position=(32 * h, 0))
                cx.mark[("l2o", g)] = cx.inc(ins, "pe")

            @block.scalar
            def _(scalar, qs=qs, g=g):
                cx.wait(scalar, "pe", cx.mark[("l2o", g)])
                for h in range(4):
                    hg = 4 * g + h
                    chunk = hg // 2
                    r0 = 64 * (hg % 2)
                    ins = scalar.activation(
                        yT.ap()[r0:r0 + 64,
                                chunk * T + qs * 512: chunk * T + (qs + 1) * 512],
                        psA[0:64, h * 512:(h + 1) * 512], AF.Identity,
                        bias=H["s_sm"][r0:r0 + 64, 3:4])
                cx.mark[("yT", g)] = cx.inc(ins, "act")

    def ytr(cc, n):
        return yT.ap()[:, cc * T + n * 512: cc * T + (n + 1) * 512]

    for o in range(NCH):
        ps = psA if o % 2 == 0 else psB
        ws.mm(24 + o, ps, ytr)

        @block.scalar
        def _(scalar, o=o, ps=ps):
            cx.wait(scalar, "pe")
            cx.wait(scalar, "dve")
            cx.inc(scalar.activation(tmpf, ps[:, 0:2048], AF.Identity,
                                     bias=H["s_bias"][:, 24 + o:25 + o]), "act")

        @block.vector
        def _(v, o=o):
            cx.wait(v, "act")
            cx.inc(v.tensor_tensor(rA.ap()[:, o * T:(o + 1) * T],
                                   rA.ap()[:, o * T:(o + 1) * T],
                                   tmpf, ALU.add), "dve")

    barrier_all(block, cx)
    tap(block, cx, H, dbg, "a1_sm", H["s_sm"].ap()[:, 0:8])
    tap(block, cx, H, dbg, "a1_m2", H["s_m2"].ap()[0:2, 0:8])
    tap(block, cx, H, dbg, "a1_neg", H["s_sm"].ap()[:, 4:5])
    tap(block, cx, H, dbg, "a1_yl", ylab)
    tap(block, cx, H, dbg, "a1_pt", pt)
    tap(block, cx, H, dbg, "a1_v4", v4)
    tap(block, cx, H, dbg, "a1_qfk", qfk)
    tap(block, cx, H, dbg, "a1_k4", k4)
    tap(block, cx, H, dbg, "a1_yT0", yT.ap()[:, 0:2048])
    tap(block, cx, H, dbg, "x1pre0", rA.ap()[:, 0:2048])


def _stats_rows_bcast(block, cx, H, want_mean, eps, src_bf):
    psA, psB = H["psA"], H["psB"]
    sq = _ar(H, 8, (128, 2048), BF16)
    meanb = _ar(H, 12, (128, 2048), F32)
    rstdb = _ar(H, 20, (128, 2048), F32)
    tmprow = _ar(H, 0, (1, 2048), F32)

    for c in range(NCH):
        @block.scalar
        def _(scalar, c=c):
            if c == 0:
                cx.wait(scalar, "dve")
            cx.wait(scalar, "pe")
            cx.inc(scalar.activation(sq[:], src_bf(c), AF.Square), "act")

        @block.tensor
        def _(tensor, c=c):
            cx.wait(tensor, "act")
            for n in range(4):
                ins = tensor.matmul(psB[0:1, n * 512:(n + 1) * 512],
                                    H["s_onesb"][:],
                                    sq[:, n * 512:(n + 1) * 512],
                                    start=(c == 0), stop=(c == NCH - 1))
            if want_mean:
                for n in range(4):
                    ins = tensor.matmul(psA[0:1, n * 512:(n + 1) * 512],
                                        H["s_onesb"][:],
                                        src_bf(c)[:, n * 512:(n + 1) * 512],
                                        start=(c == 0), stop=(c == NCH - 1))
            cx.inc(ins, "pe")

    @block.vector
    def _(v):
        cx.wait(v, "pe")
        if want_mean:
            cx.chain(v, v.tensor_scalar(meanb[0:1, :], psA[0:1, 0:2048],
                                        1.0 / C, None, ALU.mult))
            cx.chain(v, v.tensor_scalar(rstdb[0:1, :], psB[0:1, 0:2048],
                                        1.0 / C, None, ALU.mult))
            cx.chain(v, v.tensor_tensor(tmprow, meanb[0:1, :], meanb[0:1, :],
                                        ALU.mult))
            cx.chain(v, v.tensor_tensor(rstdb[0:1, :], rstdb[0:1, :], tmprow,
                                        ALU.subtract))
        else:
            cx.chain(v, v.tensor_scalar(rstdb[0:1, :], psB[0:1, 0:2048],
                                        1.0 / C, None, ALU.mult))
        cx.inc(v.tensor_scalar(rstdb[0:1, :], rstdb[0:1, :], eps, None,
                               ALU.add), "dve")

    @block.scalar
    def _(scalar):
        cx.wait(scalar, "dve")
        cx.chain(scalar, scalar.activation(rstdb[0:1, :], rstdb[0:1, :],
                                           AF.Ln), "act")
        cx.inc(scalar.activation(rstdb[0:1, :], rstdb[0:1, :], AF.Exp,
                                 scale=-0.5), "act")

    @block.tensor
    def _(tensor):
        cx.wait(tensor, "act")
        cx.wait(tensor, "dve")
        for n in range(4):
            ins = tensor.matmul(psB[:, n * 512:(n + 1) * 512], H["s_ones"][:],
                                rstdb[0:1, n * 512:(n + 1) * 512],
                                start=True, stop=True)
            if want_mean:
                ins = tensor.matmul(psA[:, n * 512:(n + 1) * 512],
                                    H["s_ones"][:],
                                    meanb[0:1, n * 512:(n + 1) * 512],
                                    start=True, stop=True)
        cx.inc(ins, "pe")

    @block.scalar
    def _(scalar):
        cx.wait(scalar, "pe")
        if want_mean:
            scalar.copy(meanb, psA[:, 0:2048])
        cx.inc(scalar.copy(rstdb, psB[:, 0:2048]), "act")

    return meanb, rstdb


def ln1_rms1(nc, block, cx, H, D, dbg):
    rA, xb = H["rA"], H["xb"]
    tmpf = _ar(H, 0, (128, 2048), F32)

    def mk_src(c):
        return xb.ap()[:, c * T:(c + 1) * T]

    for c in range(NCH):
        @block.vector
        def _(v, c=c):
            if c == 0:
                cx.wait(v, "pe")
                cx.wait(v, "act")
            cx.inc(v.tensor_copy(mk_src(c), rA.ap()[:, c * T:(c + 1) * T]),
                   "dve")
    meanb, rstdb = _stats_rows_bcast(block, cx, H, True, 1e-5, mk_src)

    for c in range(NCH):
        @block.vector
        def _(v, c=c):
            cx.wait(v, "act")
            v.tensor_tensor(tmpf, rA.ap()[:, c * T:(c + 1) * T], meanb,
                            ALU.subtract)
            cx.inc(v.tensor_tensor(rA.ap()[:, c * T:(c + 1) * T], tmpf, rstdb,
                                   ALU.mult), "dve")
    barrier_all(block, cx)
    tap(block, cx, H, dbg, "x1_0", rA.ap()[:, 0:2048])

    for c in range(NCH):
        @block.vector
        def _(v, c=c):
            cx.inc(v.tensor_copy(xb.ap()[:, c * T:(c + 1) * T],
                                 rA.ap()[:, c * T:(c + 1) * T]), "dve")
    _, rstdb = _stats_rows_bcast(block, cx, H, False, 1e-6, mk_src)

    for c in range(NCH):
        @block.vector
        def _(v, c=c):
            cx.wait(v, "act")
            v.tensor_tensor(tmpf, rA.ap()[:, c * T:(c + 1) * T], rstdb,
                            ALU.mult)
            cx.inc(v.tensor_scalar(xb.ap()[:, c * T:(c + 1) * T], tmpf,
                                   H["s_bias"][:, 32 + c:33 + c], None,
                                   ALU.mult), "dve")
    barrier_all(block, cx)
    tap(block, cx, H, dbg, "xn1_0", xb.ap()[:, 0:2048])


def attn2(nc, block, cx, H, D, dbg):
    rA, xb, yT, psA, psB = H["rA"], H["xb"], H["yT"], H["psA"], H["psB"]
    tab_c = _ar(H, 0, (128, T), BF16)
    tab_s = _ar(H, 4, (128, T), BF16)
    scr = _ar(H, 8, (128, 2048), BF16)
    qlat = _ar(H, 12, (128, 2048), F32)
    tmpf = _ar(H, 20, (128, 2048), F32)
    msk = _ar(H, 28, (128, 2048), BF16)
    skv = _ar(H, 32, (128, 2 * T), BF16)
    lnrow = _ar(H, 40, (1, 512), F32)
    recr = _ar(H, 42, (1, 512), F32)
    recb = _ar(H, 44, (64, 512), F32)
    q2 = _p2(H, 0, (128, 2048), BF16)
    k2 = _p2(H, 4096, (128, 2048), BF16)
    v2 = _p2(H, 8192, (128, 2112), BF16)
    pt = _p2(H, 12416, (128, 2048), BF16)

    def xbr(cc, n):
        return xb.ap()[:, cc * T + n * 512: cc * T + (n + 1) * 512]

    @block.sync
    def _(sync):
        for k in ["pe", "act", "dve", "gp"]:
            cx.wait(sync, k)
        cx.dinc(sync.dma_start(out=tab_c, in_=D["c_tab2_c"]), "xl")
        cx.dinc(sync.dma_start(out=tab_s, in_=D["c_tab2_s"]), "xl")
    tab_ld = cx.cnt["xl"]

    loaders = [mk_loader(cx, block, H, D, "Wkv", 0),
               mk_loader(cx, block, H, D, "Wkv", 1)]
    for p in range(NE):
        loaders.append(mk_loader(cx, block, H, D, "Wq", p))
        loaders.append(mk_loader(cx, block, H, D, "Wk2", 0, sub=p, kchunks=2))
        loaders.append(mk_loader(cx, block, H, D, "Wv2", 0, sub=p, kchunks=2))
    for o in range(NCH):
        loaders.append(mk_loader(cx, block, H, D, "Wo", o))
    ws = WS(cx, block, H, D, loaders)

    for o2 in range(2):
        ps = psA if o2 == 0 else psB
        ws.mm(o2, ps, xbr)

        @block.scalar
        def _(scalar, o2=o2, ps=ps):
            cx.wait(scalar, "pe")
            cx.inc(scalar.activation(skv[:, o2 * T:(o2 + 1) * T],
                                     ps[:, 0:2048], AF.Copy, scale=0.25),
                   "act")

    for p in range(NE):
        kq = 2 + 3 * p
        ws.mm(kq, psA, xbr)

        for path in ("q", "k"):
            if path == "q":
                @block.scalar
                def _(scalar):
                    cx.wait(scalar, "pe")
                    cx.wait(scalar, "xl", tab_ld)
                    cx.inc(scalar.copy(qlat, psA[:, 0:2048]), "act")
            else:
                def skvr(cc, n):
                    return skv[:, cc * T + n * 512: cc * T + (n + 1) * 512]
                ws.mm(2 + 3 * p + 1, psA, skvr, kchunks=2)

                @block.scalar
                def _(scalar):
                    cx.wait(scalar, "pe")
                    cx.inc(scalar.copy(qlat, psA[:, 0:2048]), "act")

            @block.vector
            def _(v):
                cx.wait(v, "act")
                cx.inc(v.tensor_copy(scr[:], qlat), "dve")

            @block.tensor
            def _(tensor):
                cx.wait(tensor, "dve")
                for n in range(4):
                    ins = tensor.matmul(psB[:, n * 512:(n + 1) * 512],
                                        H["s_R2"][:],
                                        scr[:, n * 512:(n + 1) * 512],
                                        start=True, stop=True)
                cx.inc(ins, "pe")

            @block.vector
            def _(v, path=path):
                cx.wait(v, "pe")
                cx.wait(v, "xl", tab_ld)
                v.tensor_tensor(tmpf, psB[:, 0:2048], tab_s, ALU.mult)
                cx.wait(v, "act")
                v.tensor_tensor(qlat, qlat, tab_c, ALU.mult)
                dest = q2 if path == "q" else k2
                cx.inc(v.tensor_tensor(dest, tmpf, qlat, ALU.add), "dve")

        kv2 = 2 + 3 * p + 2
        slot_v2 = kv2 % 4

        @block.tensor
        def _(tensor, kv2=kv2, slot_v2=slot_v2):
            cx.wait(tensor, "wl", ws.wl[kv2])
            cx.wait(tensor, "dve")
            w = wslot(H, slot_v2)
            for tt in range(NTT):
                for lc in range(2):
                    ins = tensor.matmul(
                        psB[:, tt * 128:(tt + 1) * 128],
                        skv[:, lc * T + tt * 128: lc * T + (tt + 1) * 128],
                        w[:, lc * 128:(lc + 1) * 128],
                        start=(lc == 0), stop=(lc == 1))
            cx.mark[("slot", slot_v2)] = cx.inc(ins, "pe")
        ws.emit()

        @block.vector
        def _(v):
            v2v = v2[:, 0:2080].rearrange("p (tt h l) -> p tt h l", tt=16, h=2)
            v.memset(v2v[:, :, :, 64:65], 1.0)
            cx.wait(v, "pe")
            src = psB[:, 0:2048].rearrange("p (tt h l) -> p tt h l", tt=16, h=2)
            cx.inc(v.tensor_copy(v2v[:, :, :, 0:64], src), "dve")

        _stabilizer(block, cx, H, q2, k2, 1.02 * math.sqrt(HD), 5)
        negM = H["s_sm"][:, 5:6]
        row40 = _ar(H, 40, (1, 1024), F32)

        for qs in range(NQS):
            kmax = qs * 4 + 3
            for kt in range(kmax + 1):
                diag = kt - qs * 4
                for a in (0, 1):
                    @block.tensor
                    def _(tensor, qs=qs, kt=kt, a=a, p=p):
                        if kt == 0:
                            cx.wait(tensor, "act")
                            cx.wait(tensor, "dve")
                        m = cx.mark.get(("exp2", p, a))
                        if m is not None:
                            cx.wait(tensor, "act", m)
                        ins = tensor.matmul(
                            psA[:, a * 512:(a + 1) * 512],
                            k2[64 * a:64 * a + 64, kt * 128:(kt + 1) * 128],
                            q2[64 * a:64 * a + 64, qs * 512:(qs + 1) * 512],
                            start=True, stop=True, tile_position=(64 * a, 0))
                        cx.mark[("S2", p, a)] = cx.inc(ins, "pe")

                    @block.scalar
                    def _(scalar, p=p, a=a):
                        cx.wait(scalar, "pe", cx.mark[("S2", p, a)])
                        m = cx.mark.get(("pv2", p, a))
                        if m is not None:
                            cx.wait(scalar, "pe", m)
                        m = cx.mark.get(("msk2", p, a))
                        if m is not None:
                            cx.wait(scalar, "dve", m)
                        ins = scalar.activation(pt[:, a * 512:(a + 1) * 512],
                                                psA[:, a * 512:(a + 1) * 512],
                                                AF.Exp, bias=negM,
                                                scale=EXP_SC2)
                        cx.mark[("exp2", p, a)] = cx.inc(ins, "act")

                    if diag >= 0:
                        @block.vector
                        def _(v, diag=diag, p=p, a=a):
                            cx.wait(v, "act", cx.mark[("exp2", p, a)])
                            ins = v.tensor_tensor(
                                pt[:, a * 512:(a + 1) * 512],
                                pt[:, a * 512:(a + 1) * 512],
                                msk[:, diag * 512:(diag + 1) * 512], ALU.mult)
                            cx.mark[("msk2", p, a)] = cx.inc(ins, "dve")

                    @block.tensor
                    def _(tensor, qs=qs, kt=kt, diag=diag, p=p, a=a,
                          kmax=kmax):
                        if diag >= 0:
                            cx.wait(tensor, "dve", cx.mark[("msk2", p, a)])
                        else:
                            cx.wait(tensor, "act", cx.mark[("exp2", p, a)])
                        if kt == 0:
                            m = cx.mark.get(("rec2", p))
                            if m is not None:
                                cx.wait(tensor, "dve", m)
                            m = cx.mark.get(("tailLn2", p))
                            if m is not None:
                                cx.wait(tensor, "act", m)
                        ins = tensor.matmul(
                            psB[0:65, a * 512:(a + 1) * 512],
                            v2[:, kt * 130 + a * 65: kt * 130 + a * 65 + 65],
                            pt[:, a * 512:(a + 1) * 512],
                            start=(kt == 0), stop=(kt == kmax))
                        cx.mark[("pv2", p, a)] = cx.inc(ins, "pe")

            @block.scalar
            def _(scalar, p=p):
                cx.wait(scalar, "pe", cx.mark[("pv2", p, 1)])
                cx.mark[("tailLn2", p)] = cx.inc(
                    scalar.activation(row40, psB[64:65, 0:1024], AF.Ln), "act")

            @block.tensor
            def _(tensor, p=p):
                cx.wait(tensor, "act", cx.mark[("tailLn2", p)])
                for a in (0, 1):
                    ins = tensor.matmul(psA[64:128, a * 512:(a + 1) * 512],
                                        H["s_ones"][:, 0:64],
                                        row40[0:1, a * 512:(a + 1) * 512],
                                        start=True, stop=True)
                cx.mark[("tailB2", p)] = cx.inc(ins, "pe")

            @block.scalar
            def _(scalar, p=p):
                cx.wait(scalar, "pe", cx.mark[("tailB2", p)])
                cx.mark[("tailA2", p)] = cx.inc(
                    scalar.activation(tmpf[0:64, 0:1024], psA[64:128, 0:1024],
                                      AF.Exp, scale=-1.0), "act")

            @block.vector
            def _(v, qs=qs, p=p):
                cx.wait(v, "act", cx.mark[("tailA2", p)])
                for a in (0, 1):
                    ins = v.tensor_tensor(
                        yT.ap()[64 * a:64 * a + 64,
                                p * T + qs * 512: p * T + (qs + 1) * 512],
                        psB[0:64, a * 512:(a + 1) * 512],
                        tmpf[0:64, a * 512:(a + 1) * 512], ALU.mult)
                m = cx.inc(ins, "dve")
                cx.mark[("yl2", p)] = m
                cx.mark[("rec2", p)] = m

    ych = _ar(H, 12, (128, 2048), F32)

    def ytr(cc, n):
        return yT.ap()[:, cc * T + n * 512: cc * T + (n + 1) * 512]

    for o in range(NCH):
        ps = psA if o % 2 == 0 else psB
        ws.mm(2 + 24 + o, ps, ytr)

        @block.scalar
        def _(scalar, ps=ps):
            cx.wait(scalar, "pe")
            cx.wait(scalar, "dve")
            cx.inc(scalar.copy(tmpf, ps[:, 0:2048]), "act")

        @block.vector
        def _(v, o=o):
            cx.wait(v, "act")
            v.tensor_tensor(ych, tmpf, rA.ap()[:, o * T:(o + 1) * T], ALU.add)
            v.tensor_copy(xb.ap()[:, o * T:(o + 1) * T], ych)
            cx.inc(v.tensor_tensor(rA.ap()[:, o * T:(o + 1) * T],
                                   rA.ap()[:, o * T:(o + 1) * T], ych,
                                   ALU.add), "dve")

    barrier_all(block, cx)
    tap(block, cx, H, dbg, "a2_kv", skv[:, 0:2048])
    tap(block, cx, H, dbg, "a2_q2", q2)
    tap(block, cx, H, dbg, "a2_k2", k2)
    tap(block, cx, H, dbg, "a2_v2", v2[:, 0:2080])
    tap(block, cx, H, dbg, "a2_neg", H["s_sm"].ap()[:, 5:6])
    tap(block, cx, H, dbg, "a2_yT0", yT.ap()[:, 0:2048])
    tap(block, cx, H, dbg, "yb0", xb.ap()[:, 0:2048])


def rms2(nc, block, cx, H, D, dbg):
    xb = H["xb"]
    tmpf = _ar(H, 0, (128, 2048), F32)

    def mk_src(c):
        return xb.ap()[:, c * T:(c + 1) * T]

    _, rstdb = _stats_rows_bcast(block, cx, H, False, 1e-6, mk_src)
    for c in range(NCH):
        @block.vector
        def _(v, c=c):
            cx.wait(v, "act")
            v.tensor_tensor(tmpf, xb.ap()[:, c * T:(c + 1) * T], rstdb,
                            ALU.mult)
            cx.inc(v.tensor_scalar(xb.ap()[:, c * T:(c + 1) * T], tmpf,
                                   H["s_bias"][:, 40 + c:41 + c], None,
                                   ALU.mult), "dve")
    barrier_all(block, cx)
    tap(block, cx, H, dbg, "yn0", xb.ap()[:, 0:2048])


def router(nc, block, cx, H, D, dbg):
    xb, psA, psB = H["xb"], H["psA"], H["psB"]
    lgt = _p2(H, 0, (8, 2048), F32)

    @block.tensor
    def _(tensor):
        cx.wait(tensor, "dve")
        for cc in range(NCH):
            for n in range(4):
                ins = tensor.matmul(psA[0:8, n * 512:(n + 1) * 512],
                                    H["s_Wr"][:, cc * 8:(cc + 1) * 8],
                                    xb.ap()[:, cc * T + n * 512:
                                            cc * T + (n + 1) * 512],
                                    start=(cc == 0), stop=(cc == NCH - 1))
        cx.inc(ins, "pe")

    @block.scalar
    def _(scalar):
        cx.wait(scalar, "pe")
        cx.inc(scalar.activation(lgt, psA[0:8, 0:2048], AF.Identity,
                                 bias=H["s_rb"][:]), "act")

    @block.vector
    def _(v):
        cx.wait(v, "act")
        cx.inc(v.tensor_scalar(lgt, lgt, 50.0, -50.0, ALU.min, ALU.max), "dve")

    lgt32 = _p2(H, 0, (32, 2048), F32)

    @block.tensor
    def _(tensor):
        cx.wait(tensor, "dve")
        for tt in range(NTT):
            ins = tensor.transpose(psB[:, tt * 32:(tt + 1) * 32],
                                   lgt32[:, tt * 128:(tt + 1) * 128],
                                   H["s_id"][0:32, 0:32])
        cx.inc(ins, "pe")

    @block.vector
    def _(v):
        cx.wait(v, "pe")
        v.tensor_copy(H["s_ltm"][:].rearrange("p (t e) -> p t e", e=8),
                      psB[:, 0:512].rearrange("p (t e) -> p t e", e=32)[
                          :, :, 0:8])
        lt3 = H["s_ltm"].ap().rearrange("p (t e) -> p t e", e=8)
        lx4 = H["s_lx"].ap()[:, 0:64].rearrange("p (t e) -> p t e", e=4)
        lx2 = H["s_lx"].ap()[:, 64:96].rearrange("p (t e) -> p t e", e=2)
        cx.chain(v, v.tensor_tensor(lx4, lt3[:, :, 0:4], lt3[:, :, 4:8],
                                    ALU.max))
        cx.chain(v, v.tensor_tensor(lx2, lx4[:, :, 0:2], lx4[:, :, 2:4],
                                    ALU.max))
        cx.chain(v, v.tensor_tensor(H["s_p16"].ap().unsqueeze(2),
                                    lx2[:, :, 0:1], lx2[:, :, 1:2], ALU.max))
        sel3 = H["s_sel"].ap().rearrange("p (t e) -> p t e", e=8)
        cx.chain(v, v.memset(H["s_r16"][:], 0.0))
        for e in range(NE):
            lt_e = lt3[:, :, e:e + 1].squeeze(2)
            cx.chain(v, v.tensor_tensor(H["s_t16"][:], lt_e, H["s_p16"][:],
                                        ALU.is_ge))
            cx.chain(v, v.tensor_scalar(H["s_u16"][:], H["s_r16"][:], -1.0,
                                        1.0, ALU.mult, ALU.add))
            cx.chain(v, v.tensor_tensor(sel3[:, :, e:e + 1].squeeze(2),
                                        H["s_t16"][:], H["s_u16"][:],
                                        ALU.mult))
            cx.chain(v, v.tensor_tensor(H["s_r16"][:], H["s_r16"][:],
                                        sel3[:, :, e:e + 1].squeeze(2),
                                        ALU.add))
        cx.inc(v.tensor_copy(H["s_sc16"][:], H["s_p16"][:]), "dve")

    @block.scalar
    def _(scalar):
        cx.wait(scalar, "dve")
        cx.inc(scalar.activation(H["s_t16"][:], H["s_p16"][:], AF.Exp,
                                 scale=-1.0), "act")

    @block.vector
    def _(v):
        cx.wait(v, "act")
        cx.chain(v, v.tensor_scalar(H["s_u16"][:], H["s_t16"][:], 1.0, None,
                                    ALU.add))
        cx.chain(v, v.reciprocal(H["s_p16"][:], H["s_u16"][:]))
        cx.chain(v, v.tensor_scalar(H["s_p16"][:], H["s_p16"][:], 1e-8,
                                    1.0 - 1e-8, ALU.max, ALU.min))
        cx.chain(v, v.tensor_scalar(H["s_t16"][:], H["s_p16"][:], 0.5 + 1e-8,
                                    None, ALU.add))
        cx.chain(v, v.tensor_scalar(H["s_u16"][:], H["s_t16"][:], 0.5, 2.0,
                                    ALU.max, ALU.min))
        cx.chain(v, v.reciprocal(H["s_a16"][:], H["s_u16"][:]))
        cx.chain(v, v.tensor_tensor(H["s_b16"][:], H["s_p16"][:],
                                    H["s_a16"][:], ALU.mult))
        cx.inc(v.tensor_scalar(H["s_a16"][:], H["s_a16"][:], 0.5, None,
                               ALU.mult), "dve")
    barrier_all(block, cx)
    tap(block, cx, H, dbg, "r_ltm", H["s_ltm"].ap()[:, 0:128])
    tap(block, cx, H, dbg, "r_eq", H["s_lx"].ap()[:, 0:128])
    tap(block, cx, H, dbg, "r_pmax", H["s_sc16"].ap()[:, 0:16])
    tap(block, cx, H, dbg, "r_lgt", _p2(H, 0, (8, 2048), F32))
    tap(block, cx, H, dbg, "sel", H["s_sel"].ap()[:, 0:128])
    tap(block, cx, H, dbg, "p16", H["s_p16"].ap()[:, 0:16])


def moe(nc, block, cx, H, D, dbg):
    rA, xb, psA, psB = H["rA"], H["xb"], H["psA"], H["psB"]
    s_h = _ar(H, 0, (128, 8 * T), BF16)
    srow = _ar(H, 32, (1, 2048), F32)
    scale_bc = _p2(H, 0, (128, 2048), F32)
    s_sil = _p2(H, 8320, (128, 2048), F32)

    def xbr(cc, n):
        return xb.ap()[:, cc * T + n * 512: cc * T + (n + 1) * 512]

    loaders = []
    for e in range(-1, NE):
        w1n, w2n, w3n = ("sw1", "sw2", "sw3") if e < 0 else ("ew1", "ew2", "ew3")
        sub = None if e < 0 else e
        for ih in range(2):
            for i in range(8):
                ic = ih * 8 + i
                loaders.append(mk_loader(cx, block, H, D, w1n, ic, sub=sub))
                loaders.append(mk_loader(cx, block, H, D, w3n, ic, sub=sub))
            for c in range(NCH):
                loaders.append(mk_loader_w2(cx, block, H, D, w2n, ih, c,
                                            sub=sub))
    ws = WS(cx, block, H, D, loaders)
    KPE = 48

    for e in range(-1, NE):
        w1n, w2n, w3n = ("sw1", "sw2", "sw3") if e < 0 else ("ew1", "ew2", "ew3")
        sub = None if e < 0 else e
        kbase = (e + 1) * KPE

        @block.vector
        def _(v, e=e):
            cx.wait(v, "act")
            cx.wait(v, "pe")
            if e < 0:
                cx.inc(v.tensor_copy(H["s_sc16"][:], H["s_a16"][:]), "dve")
            else:
                sel3 = H["s_sel"].ap().rearrange("p (t e) -> p t e", e=8)
                cx.chain(v, v.tensor_tensor(H["s_sc16"][:],
                                            sel3[:, :, e:e + 1].squeeze(2),
                                            H["s_b16"][:], ALU.mult))
                cx.inc(v.tensor_copy(H["s_t16"][:], H["s_sc16"][:]), "dve")

        @block.tensor
        def _(tensor):
            cx.wait(tensor, "dve")
            cx.inc(tensor.transpose(psA[0:16, 0:128], H["s_sc16"][:],
                                    H["s_id"][:]), "pe")

        @block.vector
        def _(v):
            cx.wait(v, "pe")
            cx.inc(v.tensor_copy(H["s_str16"][:], psA[0:16, 0:128]), "dve")

        @block.sync
        def _(sync):
            cx.wait(sync, "dve")
            cx.dinc(sync.dma_start(out=srow, in_=H["s_str16"][:]), "xl")
        row_ld = cx.cnt["xl"]

        @block.tensor
        def _(tensor):
            cx.wait(tensor, "xl", row_ld)
            for n in range(4):
                ins = tensor.matmul(psA[:, n * 512:(n + 1) * 512],
                                    H["s_ones"][:],
                                    srow[:, n * 512:(n + 1) * 512],
                                    start=True, stop=True)
            cx.inc(ins, "pe")

        @block.scalar
        def _(scalar):
            cx.wait(scalar, "pe")
            cx.inc(scalar.copy(scale_bc, psA[:, 0:2048]), "act")

        for ih in range(2):
            for i in range(8):
                k0 = kbase + ih * 24 + 2 * i
                ws.mm(k0, psA, xbr, dve_val=(None if i == 0 else 0))
                mh1 = cx.cnt["pe"]
                ws.mm(k0 + 1, psB, xbr)

                @block.scalar
                def _(scalar, mh1=mh1):
                    cx.wait(scalar, "pe", mh1)
                    cx.wait(scalar, "dve")
                    cx.inc(scalar.activation(s_sil, psA[:, 0:2048], AF.Silu),
                           "act")

                @block.vector
                def _(v, i=i):
                    cx.wait(v, "act")
                    cx.wait(v, "pe")
                    v.tensor_tensor(s_sil, s_sil, scale_bc, ALU.mult)
                    cx.inc(v.tensor_tensor(s_h[:, i * T:(i + 1) * T], s_sil,
                                           psB[:, 0:2048], ALU.mult), "dve")

            madd = {}
            for c in range(NCH):
                ps = psA if c % 2 == 0 else psB
                k2i = kbase + ih * 24 + 16 + c

                def shr(cc, n):
                    return s_h[:, cc * T + n * 512: cc * T + (n + 1) * 512]
                ws.mm(k2i, ps, shr, act_val=0,
                      dve_val=None if c < 2 else madd[c - 2])

                @block.vector
                def _(v, c=c, ps=ps):
                    cx.wait(v, "pe")
                    madd[c] = cx.inc(
                        v.tensor_tensor(rA.ap()[:, c * T:(c + 1) * T],
                                        rA.ap()[:, c * T:(c + 1) * T],
                                        ps[:, 0:2048], ALU.add), "dve")

    barrier_all(block, cx)
    tap(block, cx, H, dbg, "F0", rA.ap()[:, 0:2048])


def ln_final(nc, block, cx, H, D, dbg):
    rA, xb = H["rA"], H["xb"]
    tmpf = _ar(H, 0, (128, 2048), F32)

    def mk_src(c):
        return xb.ap()[:, c * T:(c + 1) * T]

    for c in range(NCH):
        @block.vector
        def _(v, c=c):
            if c == 0:
                cx.wait(v, "pe")
                cx.wait(v, "act")
            cx.inc(v.tensor_copy(mk_src(c), rA.ap()[:, c * T:(c + 1) * T]),
                   "dve")
    meanb, rstdb = _stats_rows_bcast(block, cx, H, True, 1e-5, mk_src)

    for c in range(NCH):
        @block.vector
        def _(v, c=c):
            cx.wait(v, "act")
            v.tensor_tensor(tmpf, rA.ap()[:, c * T:(c + 1) * T], meanb,
                            ALU.subtract)
            cx.inc(v.tensor_tensor(rA.ap()[:, c * T:(c + 1) * T], tmpf, rstdb,
                                   ALU.mult), "dve")
    barrier_all(block, cx)


def output_transpose(nc, block, cx, H, D, dbg):
    rA, psA, psB = H["rA"], H["psA"], H["psB"]
    out = H["out"]
    stage = _ar(H, 12, (128, 2048), F32)
    st_done, ev_done, tr_done = {}, {}, {}
    for tt in range(NTT):
        buf = tt % 2

        @block.tensor
        def _(tensor, tt=tt):
            if tt >= 1:
                cx.wait(tensor, "act", ev_done[tt - 1])
            for c in range(NCH):
                half = psA if c < 4 else psB
                ins = tensor.transpose(
                    half[:, (c % 4) * 512:(c % 4) * 512 + 128],
                    rA.ap()[:, c * T + tt * 128: c * T + (tt + 1) * 128],
                    H["s_id"][:])
            tr_done[tt] = cx.inc(ins, "pe")

        @block.scalar
        def _(scalar, tt=tt, buf=buf):
            cx.wait(scalar, "pe", tr_done[tt])
            if tt >= 2:
                cx.wait(scalar, "st", st_done[tt - 2])
            for half, c0 in [(psA, 0), (psB, 4)]:
                view = half[:, 0:2048].rearrange("p (c n) -> p c n", c=4)[
                    :, :, 0:128]
                dst = stage.rearrange("p (b c n) -> p b c n", b=2, c=8)[
                    :, buf, c0:c0 + 4, :]
                ins = scalar.copy(dst, view)
            ev_done[tt] = cx.inc(ins, "act")

        @block.sync
        def _(sync, tt=tt, buf=buf):
            cx.wait(sync, "act", ev_done[tt])
            cx.dinc(sync.dma_start(out=out[tt * 128:(tt + 1) * 128, :],
                                   in_=stage[:, buf * 1024:(buf + 1) * 1024]),
                    "st")
            st_done[tt] = cx.cnt["st"]

    @block.sync
    def _(sync):
        cx.wait(sync, "st")


def kernel(**inputs):
    w = _prep(inputs)
    consts = _host_consts()
    x_full = np.asarray(inputs["L0_attn_x"])
    shared = dict(w)
    for k, v in consts.items():
        shared["c_" + k] = v

    nc, dbg = build_program(debug=[(n, _DBG_SHAPES[n]) for n in DEBUG])
    in_maps = []
    for b in range(8):
        m = dict(shared)
        m["x"] = np.ascontiguousarray(x_full[b])
        in_maps.append(m)
    res = run_bass_kernel_spmd(nc, in_maps, core_ids=list(range(8)))
    outs = np.stack([np.ascontiguousarray(r["out"]) for r in res.results],
                    axis=0)
    kernel._last = res
    return outs


_DBG_SHAPES = {
    "r_eq": (128, 128),
    "r_pmax": (128, 16),
    "r_ltm": (128, 128),
    "r_lgt": (8, 2048),
    "a2_kv": (128, 2048),
    "a2_q2": (128, 2048),
    "a2_k2": (128, 2048),
    "a2_v2": (128, 2080),
    "a2_neg": (128, 1),
    "a2_yT0": (128, 2048),
    "a1_sm": (128, 8),
    "a1_m2": (2, 8),
    "a1_neg": (128, 1),
    "a1_yl": (128, 2048),
    "a1_pt": (128, 2048),
    "a1_v4": (128, 2112),
    "xT0": (128, 2048),
    "a1_qfk": (128, 2048),
    "a1_k4": (128, 2048),
    "a1_yT0": (128, 2048),
    "x1pre0": (128, 2048),
    "x1_0": (128, 2048),
    "xn1_0": (128, 2048),
    "yb0": (128, 2048),
    "yn0": (128, 2048),
    "sel": (128, 128),
    "p16": (128, 16),
    "F0": (128, 2048),
}


# version 06
# speedup vs baseline: 1.3487x; 1.1976x over previous
import contextlib
import math

import numpy as np
import ml_dtypes

import concourse.bass as bass
import concourse.mybir as mybir
from concourse.bass_utils import run_bass_kernel_spmd

F32 = mybir.dt.float32
BF16 = mybir.dt.bfloat16
AF = mybir.ActivationFunctionType
ALU = mybir.AluOpType
AX = mybir.AxisListType

T = 2048
C = 1024
NCH = 8
NTT = 16
NQS = 4
LAT = 32
HD = 64
I_FF = 2048
NE = 8

EXP_SC1 = 1.0 / math.sqrt(LAT)
EXP_SC2 = 1.0 / math.sqrt(HD)

DEBUG = []
BF = ml_dtypes.bfloat16


def _host_consts():
    c = {}
    inv = 1.0 / 10000.0 ** (np.arange(0, LAT, 2, dtype=np.float64) / LAT)
    t = np.arange(T, dtype=np.float64)
    fr = np.outer(t, inv)
    cos1 = np.repeat(np.cos(fr), 2, axis=-1)[:, :LAT]
    sin1 = np.repeat(np.sin(fr), 2, axis=-1)[:, :LAT]
    c["tab1_c"] = np.tile(cos1.T.astype(np.float32), (4, 1)).astype(BF)
    c["tab1_s"] = np.tile(sin1.T.astype(np.float32), (4, 1)).astype(BF)
    R32 = np.zeros((LAT, LAT), np.float32)
    for i in range(LAT // 2):
        R32[2 * i + 1, 2 * i] = -1.0
        R32[2 * i, 2 * i + 1] = 1.0
    c["R1"] = np.kron(np.eye(4, dtype=np.float32), R32).astype(BF)
    inv2 = 1.0 / 100000.0 ** (np.arange(0, HD, 2, dtype=np.float64) / HD)
    fr2 = np.outer(t, inv2)
    C64 = np.concatenate([np.cos(fr2), np.cos(fr2)], axis=1).T.astype(np.float32)
    S64 = np.concatenate([np.sin(fr2), np.sin(fr2)], axis=1).T.astype(np.float32)
    c["tab2_c"] = np.tile(C64, (2, 1)).astype(BF)
    c["tab2_s"] = np.tile(S64, (2, 1)).astype(BF)
    R64 = np.zeros((HD, HD), np.float32)
    for d in range(32):
        R64[d + 32, d] = -1.0
        R64[d, d + 32] = 1.0
    c["R2"] = np.kron(np.eye(2, dtype=np.float32), R64).astype(BF)
    kk = np.arange(128)[:, None]
    qq = np.arange(512)[None, :]
    for j, d in enumerate([0, 128, 256, 384]):
        c[f"mask{j}"] = ((d + kk) <= qq).astype(BF)
    c["ident"] = np.eye(128, dtype=np.float32)
    c["ones_row"] = np.ones((1, 128), np.float32)
    c["ones_col_bf"] = np.ones((128, 1), np.float32).astype(BF)
    c["ones_col_f"] = np.ones((128, 1), np.float32)
    return c


def _prep(inp):
    w = {}
    f = np.ascontiguousarray
    w["caw"] = f(np.asarray(inp["L0_attn_c_attn_w"]).T)
    w["cab"] = f(np.asarray(inp["L0_attn_c_attn_b"]).reshape(24, 128).T)
    q2l = np.asarray(inp["L0_attn_q2l_w"])
    kv2l = np.asarray(inp["L0_attn_kv2l_w"])
    blk = np.zeros((256, 128), np.float32)
    for h in range(4):
        blk[h * 64:(h + 1) * 64, h * 32:(h + 1) * 32] = q2l.T
    w["W4q"] = f(blk)
    blk = np.zeros((256, 128), np.float32)
    for h in range(4):
        blk[h * 64:(h + 1) * 64, h * 32:(h + 1) * 32] = kv2l.T
    w["W4kv"] = f(blk)
    w["q2l_b4"] = f(np.tile(np.asarray(inp["L0_attn_q2l_b"]), 4).reshape(128, 1))
    w["kv2l_b4"] = f(np.tile(np.asarray(inp["L0_attn_kv2l_b"]), 4).reshape(128, 1))
    row = np.zeros((1, 132), np.float32)
    for h in range(4):
        row[0, h * 33:h * 33 + 32] = np.asarray(inp["L0_attn_kv2l_b"])
    w["kv2l_brow"] = row
    blkq = np.zeros((128, 128), np.float32)
    for h in range(4):
        blkq[h * 32:(h + 1) * 32, h * 32:(h + 1) * 32] = np.asarray(
            inp["L0_attn_wqk_w"]).T
    w["wqk4"] = f(blkq)
    w["wqk_b4"] = f(np.tile(np.asarray(inp["L0_attn_wqk_b"]), 4).reshape(128, 1))
    l2oT = np.asarray(inp["L0_attn_l2o_w"]).T
    w["l2o4"] = f(np.tile(l2oT, (4, 1)))
    w["l2o_b2"] = f(np.tile(np.asarray(inp["L0_attn_l2o_b"]), 2).reshape(128, 1))
    w["cpw"] = f(np.asarray(inp["L0_attn_cproj_w"]).T)
    w["cpb"] = f(np.asarray(inp["L0_attn_cproj_b"]).reshape(8, 128).T)
    w["ln1w"] = f(np.asarray(inp["L0_ffn_ln1_w"]).reshape(8, 128).T)
    w["ln2w"] = f(np.asarray(inp["L0_ffn_ln2_w"]).reshape(8, 128).T)
    w["Wq"] = f(np.asarray(inp["L0_ffn_Wq"]))
    w["Wkv"] = f(np.asarray(inp["L0_ffn_Wkv"]))
    w["Wk2"] = f(np.asarray(inp["L0_ffn_Wk_up"]).reshape(8, 2, 256, 64)
                 .transpose(0, 2, 1, 3).reshape(8, 256, 128))
    w["Wv2"] = f(np.asarray(inp["L0_ffn_Wv_up"]).reshape(8, 2, 256, 64)
                 .transpose(0, 2, 1, 3).reshape(8, 256, 128))
    w["Wo"] = f(np.asarray(inp["L0_ffn_Wo"]))
    w["Wr"] = f(np.asarray(inp["L0_ffn_Wr"]))
    w["r_bias"] = f(np.asarray(inp["L0_ffn_r_bias"]).reshape(8, 1))
    w["sw1"] = f(np.asarray(inp["L0_ffn_sw1"]))
    w["sw2"] = f(np.asarray(inp["L0_ffn_sw2"]))
    w["sw3"] = f(np.asarray(inp["L0_ffn_sw3"]))
    w["ew1"] = f(np.asarray(inp["L0_ffn_ew1"]))
    w["ew2"] = f(np.asarray(inp["L0_ffn_ew2"]))
    w["ew3"] = f(np.asarray(inp["L0_ffn_ew3"]))
    return w


class Ctx:
    def __init__(self):
        self.cnt = dict(pe=0, act=0, dve=0, gp=0, wl=0, xl=0, st=0)
        self.mark = {}

    def inc(self, ins, name):
        self.cnt[name] += 1
        ins.then_inc(self.sem[name], 1)
        return self.cnt[name]

    def dinc(self, ins, name):
        self.cnt[name] += 1
        ins.then_inc(self.sem[name], 16)
        return self.cnt[name]

    def chain(self, eng, ins, name="dve"):
        c = self.inc(ins, name)
        eng.wait_ge(self.sem[name], c)
        return c

    def wait(self, eng, name, val=None):
        v = self.cnt[name] if val is None else val
        if v <= 0:
            return
        if name in ("wl", "xl", "st"):
            eng.wait_ge(self.sem[name], 16 * v)
        else:
            eng.wait_ge(self.sem[name], v)


def _ar(H, off_kb, shape, dt=BF16, p0=0):
    e0 = int(off_kb * 1024) // 2
    n = shape[1] * (2 if dt == F32 else 1)
    v = H["AR"].ap()[:, e0:e0 + n]
    if dt == F32:
        v = v.bitcast(F32)
    return v[p0:p0 + shape[0], :]


def _p2(H, off_b, shape, dt=BF16, p0=0):
    e0 = off_b // 2
    n = shape[1] * (2 if dt == F32 else 1)
    v = H["P2"].ap()[:, e0:e0 + n]
    if dt == F32:
        v = v.bitcast(F32)
    return v[p0:p0 + shape[0], :]


def build_program(debug=()):
    nc = bass.Bass("TRN2", target_bir_lowering=False)
    cx = Ctx()
    consts = _host_consts()
    D = {}

    def din(name, shape, dt=F32):
        D[name] = nc.dram_tensor(name, list(shape), dt, kind="ExternalInput").ap()

    din("x", (T, C))
    din("caw", (C, 3 * C)); din("cab", (128, 24))
    din("W4q", (256, 128)); din("W4kv", (256, 128))
    din("q2l_b4", (128, 1)); din("kv2l_b4", (128, 1)); din("kv2l_brow", (1, 132))
    din("wqk4", (128, 128)); din("wqk_b4", (128, 1))
    din("l2o4", (128, 64)); din("l2o_b2", (128, 1))
    din("cpw", (C, C)); din("cpb", (128, 8))
    din("ln1w", (128, 8)); din("ln2w", (128, 8))
    din("Wq", (C, C)); din("Wkv", (C, 256))
    din("Wk2", (8, 256, 128)); din("Wv2", (8, 256, 128))
    din("Wo", (C, C)); din("Wr", (C, 8)); din("r_bias", (8, 1))
    din("sw1", (C, I_FF)); din("sw2", (I_FF, C)); din("sw3", (C, I_FF))
    din("ew1", (NE, C, I_FF)); din("ew2", (NE, I_FF, C)); din("ew3", (NE, C, I_FF))
    for k, v in consts.items():
        din("c_" + k, v.shape, BF16 if v.dtype == BF else F32)

    out = nc.dram_tensor("out", [T, C], F32, kind="ExternalOutput").ap()
    dbg = {}
    for name, shape in debug:
        dbg[name] = nc.dram_tensor("dbg_" + name, list(shape), F32,
                                   kind="ExternalOutput").ap()

    st = contextlib.ExitStack()
    with st:
        def sb(name, p, f, dt):
            return st.enter_context(nc.sbuf_tensor(name, [p, f], dt))

        rA = sb("rA", 128, NCH * T, F32)
        xb = sb("xb", 128, NCH * T, BF16)
        yT = sb("yT", 128, NCH * T, BF16)
        AR = sb("AR", 128, 23552, BF16)
        P2 = sb("P2", 128, 8320, BF16)
        wsl = sb("wsl", 128, 4096, BF16)
        s_W4q = sb("s_W4q", 128, 256, BF16)
        s_W4kv = sb("s_W4kv", 128, 256, BF16)
        s_wqk4 = sb("s_wqk4", 128, 128, BF16)
        s_l2o4 = sb("s_l2o4", 128, 64, BF16)
        s_R1 = sb("s_R1", 128, 128, BF16)
        s_R2 = sb("s_R2", 128, 128, BF16)
        s_Wr = sb("s_Wr", 128, 64, BF16)
        s_id = sb("s_id", 128, 128, F32)
        s_ones = sb("s_ones", 1, 128, F32)
        s_onesb = sb("s_onesb", 128, 1, BF16)
        s_onesf = sb("s_onesf", 128, 1, F32)
        s_bias = sb("s_bias", 128, 48, F32)
        s_sm = sb("s_sm", 128, 8, F32)
        s_rb = sb("s_rb", 8, 1, F32)
        s_bv4 = sb("s_bv4", 128, 132, F32)
        s_brow = sb("s_brow", 1, 132, F32)
        s_str16 = sb("s_str16", 16, 128, F32)
        s_ltm = sb("s_ltm", 128, 128, F32)
        s_sel = sb("s_sel", 128, 128, F32)
        s_lx = sb("s_lx", 128, 128, F32)
        s_p16 = sb("s_p16", 128, 16, F32)
        s_t16 = sb("s_t16", 128, 16, F32)
        s_a16 = sb("s_a16", 128, 16, F32)
        s_b16 = sb("s_b16", 128, 16, F32)
        s_r16 = sb("s_r16", 128, 16, F32)
        s_sc16 = sb("s_sc16", 128, 16, F32)
        s_u16 = sb("s_u16", 128, 16, F32)
        s_m2 = sb("s_m2", 2, 128, F32)

        psA = st.enter_context(nc.psum_tensor("psA", [128, 2048], F32))
        psB = st.enter_context(nc.psum_tensor("psB", [128, 2048], F32))

        sem = {}
        for name in ["pe", "act", "dve", "gp", "wl", "xl", "st"]:
            sem[name] = st.enter_context(nc.semaphore("s_" + name))
        cx.sem = sem

        block = st.enter_context(nc.Block())
        H = dict(locals())
        H["out"] = out

        ph0(nc, block, cx, H, D, dbg)
        attn1(nc, block, cx, H, D, dbg)
        ln1_rms1(nc, block, cx, H, D, dbg)
        attn2(nc, block, cx, H, D, dbg)
        rms2(nc, block, cx, H, D, dbg)
        router(nc, block, cx, H, D, dbg)
        moe(nc, block, cx, H, D, dbg)
        ln_final(nc, block, cx, H, D, dbg)
        output_transpose(nc, block, cx, H, D, dbg)

    return nc, dbg


def tap(block, cx, H, dbg, name, src_ap):
    if name not in dbg:
        return
    d = dbg[name]

    @block.gpsimd
    def _(gp):
        for k in ["pe", "act", "dve"]:
            cx.wait(gp, k)
        cx.dinc(gp.dma_start(out=d, in_=src_ap), "wl")
        cx.wait(gp, "wl")


def barrier_all(block, cx):
    for engname in ["tensor", "scalar", "vector", "gpsimd", "sync"]:
        @getattr(block, engname)
        def _(e):
            for k in ["pe", "act", "dve", "gp", "wl", "xl"]:
                cx.wait(e, k)


def wslot(H, slot):
    return H["wsl"].ap()[:, slot * 1024:(slot + 1) * 1024]


class WS:
    def __init__(self, cx, block, H, D, loaders, depth=3):
        self.cx, self.block, self.H, self.D = cx, block, H, D
        self.loaders = loaders
        self.nxt = 0
        self.wl = {}
        for _ in range(min(depth, len(loaders))):
            self.emit()

    def emit(self):
        k = self.nxt
        if k < len(self.loaders):
            self.wl[k] = self.loaders[k](k % 4)
            self.nxt += 1

    def mm(self, k, psum, rhs, kchunks=8, act_val=None, dve_val=None):
        cx, block, H = self.cx, self.block, self.H
        slot = k % 4
        wl_cnt = self.wl[k]

        @block.tensor
        def _(tensor):
            cx.wait(tensor, "wl", wl_cnt)
            if act_val != 0:
                cx.wait(tensor, "act", act_val)
            if dve_val != 0:
                cx.wait(tensor, "dve", dve_val)
            w = wslot(H, slot)
            for cc in range(kchunks):
                for n in range(4):
                    ins = tensor.matmul(psum[:, n * 512:(n + 1) * 512],
                                        w[:, cc * 128:(cc + 1) * 128],
                                        rhs(cc, n),
                                        start=(cc == 0),
                                        stop=(cc == kchunks - 1))
            cx.mark[("slot", slot)] = cx.inc(ins, "pe")
        self.emit()


def mk_loader(cx, block, H, D, wname, o, sub=None, kchunks=8):
    def load(slot):
        @block.gpsimd
        def _(gp):
            rel = cx.mark.get(("slot", slot))
            if rel is not None:
                cx.wait(gp, "pe", rel)
            s = D[wname] if sub is None else D[wname][sub]
            ap = s.rearrange("(cc p) o -> p cc o", p=128)[
                :, 0:kchunks, o * 128:(o + 1) * 128]
            dst = wslot(H, slot).rearrange("p (cc o) -> p cc o", cc=8)[
                :, 0:kchunks, :]
            cx.dinc(gp.dma_start(out=dst, in_=ap), "wl")
        return cx.cnt["wl"]
    return load


def mk_loader_w2(cx, block, H, D, wname, ih, c, sub=None):
    def load(slot):
        @block.gpsimd
        def _(gp):
            rel = cx.mark.get(("slot", slot))
            if rel is not None:
                cx.wait(gp, "pe", rel)
            s = D[wname] if sub is None else D[wname][sub]
            ap = s.rearrange("(cc p) o -> p cc o", p=128)[
                :, ih * 8:(ih + 1) * 8, c * 128:(c + 1) * 128]
            dst = wslot(H, slot).rearrange("p (cc o) -> p cc o", cc=8)
            cx.dinc(gp.dma_start(out=dst, in_=ap), "wl")
        return cx.cnt["wl"]
    return load


def ph0(nc, block, cx, H, D, dbg):
    rA, xb, psA, psB = H["rA"], H["xb"], H["psA"], H["psB"]
    stage = _ar(H, 12, (128, 2048), F32)

    @block.sync
    def _(sync):
        for nm, dst in [("c_ident", H["s_id"]), ("c_ones_row", H["s_ones"]),
                        ("c_R1", H["s_R1"]), ("c_R2", H["s_R2"]),
                        ("c_ones_col_bf", H["s_onesb"]), ("r_bias", H["s_rb"]),
                        ("c_ones_col_f", H["s_onesf"])]:
            cx.dinc(sync.dma_start(out=dst[:], in_=D[nm]), "xl")
        msk = _ar(H, 28, (128, 2048), BF16)
        for j in range(4):
            cx.dinc(sync.dma_start(out=msk[:, j * 512:(j + 1) * 512],
                                   in_=D[f"c_mask{j}"]), "xl")
        cx.dinc(sync.dma_start(out=H["s_bias"][:, 0:24], in_=D["cab"]), "xl")
        cx.dinc(sync.dma_start(out=H["s_bias"][:, 24:32], in_=D["cpb"]), "xl")
        cx.dinc(sync.dma_start(out=H["s_bias"][:, 32:40], in_=D["ln1w"]), "xl")
        cx.dinc(sync.dma_start(out=H["s_bias"][:, 40:48], in_=D["ln2w"]), "xl")
        for i, nm in enumerate(["q2l_b4", "kv2l_b4", "wqk_b4", "l2o_b2"]):
            cx.dinc(sync.dma_start(out=H["s_sm"][:, i:i + 1], in_=D[nm]), "xl")
        cx.dinc(sync.dma_start(out=H["s_brow"][:], in_=D["kv2l_brow"]), "xl")
        tabs = _ar(H, 0, (128, 2 * T), BF16)
        for i, nm in enumerate(["tab1_c", "tab1_s"]):
            cx.dinc(sync.dma_start(out=tabs[:, i * T:(i + 1) * T],
                                   in_=D["c_" + nm]), "xl")
        cx.dinc(sync.dma_start(out=stage[:, 0:1024], in_=D["x"][0:128, :]), "xl")
    xl_base = cx.cnt["xl"]

    @block.gpsimd
    def _(gp):
        cx.dinc(gp.dma_start(out=H["s_W4q"][:].rearrange("p (c m) -> p c m", c=2),
                             in_=D["W4q"].rearrange("(c p) m -> p c m", p=128)),
                "wl")
        cx.dinc(gp.dma_start(out=H["s_W4kv"][:].rearrange("p (c m) -> p c m", c=2),
                             in_=D["W4kv"].rearrange("(c p) m -> p c m", p=128)),
                "wl")
        cx.dinc(gp.dma_start(out=H["s_wqk4"][:], in_=D["wqk4"]), "wl")
        cx.dinc(gp.dma_start(out=H["s_l2o4"][:], in_=D["l2o4"]), "wl")
        cx.dinc(gp.dma_start(out=H["s_Wr"][:].rearrange("p (c m) -> p c m", c=8),
                             in_=D["Wr"].rearrange("(c p) m -> p c m", p=128)),
                "wl")

    @block.tensor
    def _(tensor):
        cx.wait(tensor, "xl")
        cx.inc(tensor.matmul(psB[:, 0:132], H["s_ones"][:], H["s_brow"][:],
                             start=True, stop=True), "pe")

    @block.vector
    def _(v):
        cx.wait(v, "pe")
        cx.inc(v.tensor_copy(H["s_bv4"][:], psB[:, 0:132]), "dve")

    tr_done, ev_done = {}, {}
    for tt in range(NTT):
        buf = tt % 2

        @block.sync
        def _(sync, tt=tt, buf=buf):
            if tt + 1 < NTT:
                if tt >= 1:
                    cx.wait(sync, "pe", tr_done[tt - 1])
                cx.dinc(sync.dma_start(
                    out=stage[:, (1 - buf) * 1024:(2 - buf) * 1024],
                    in_=D["x"][(tt + 1) * 128:(tt + 2) * 128, :]), "xl")

        @block.tensor
        def _(tensor, tt=tt, buf=buf):
            cx.wait(tensor, "xl", xl_base + tt)
            cx.wait(tensor, "dve")
            if tt > 0:
                cx.wait(tensor, "act", ev_done[tt - 1])
            for c in range(NCH):
                half = psA if c < 4 else psB
                ins = tensor.transpose(
                    half[:, (c % 4) * 512:(c % 4) * 512 + 128],
                    stage[:, buf * 1024 + c * 128: buf * 1024 + (c + 1) * 128],
                    H["s_id"][:])
            tr_done[tt] = cx.inc(ins, "pe")

        @block.scalar
        def _(scalar, tt=tt):
            cx.wait(scalar, "pe", tr_done[tt])
            for half, c0 in [(psA, 0), (psB, 4)]:
                view = half[:, 0:2048].rearrange("p (c n) -> p c n", c=4)[
                    :, :, 0:128]
                dst = rA.ap().rearrange("p (c t) -> p c t", c=8)[
                    :, c0:c0 + 4, tt * 128:(tt + 1) * 128]
                ins = scalar.copy(dst, view)
            ev_done[tt] = cx.inc(ins, "act")

        @block.vector
        def _(v, tt=tt):
            cx.wait(v, "act", ev_done[tt])
            dst = xb.ap().rearrange("p (c t) -> p c t", c=8)[
                :, :, tt * 128:(tt + 1) * 128]
            src = rA.ap().rearrange("p (c t) -> p c t", c=8)[
                :, :, tt * 128:(tt + 1) * 128]
            cx.inc(v.tensor_copy(dst, src), "dve")

    barrier_all(block, cx)
    tap(block, cx, H, dbg, "xT0", rA.ap()[:, 0:2048])


def _stabilizer(block, cx, H, a_src, b_src, coef, dst_col):
    psA = H["psB"]

    @block.vector
    def _(v):
        cx.wait(v, "act")
        cx.wait(v, "dve")
        v.tensor_reduce(H["s_sm"][:, 6:7], a_src, AX.X, ALU.max,
                        apply_absolute_value=True)
        cx.inc(v.tensor_reduce(H["s_sm"][:, 7:8], b_src, AX.X, ALU.max,
                               apply_absolute_value=True), "dve")

    @block.tensor
    def _(tensor):
        cx.wait(tensor, "dve")
        cx.inc(tensor.transpose(psA[0:2, 0:128], H["s_sm"][:, 6:8],
                                H["s_id"][:]), "pe")

    @block.vector
    def _(v):
        cx.wait(v, "pe")
        cx.inc(v.tensor_reduce(H["s_m2"][:, 0:1], psA[0:2, 0:128],
                               AX.X, ALU.max), "dve")

    @block.scalar
    def _(scalar):
        cx.wait(scalar, "dve")
        cx.inc(scalar.activation(H["s_m2"][0:2, 1:2], H["s_m2"][0:2, 0:1],
                                 AF.Ln), "act")

    @block.tensor
    def _(tensor):
        cx.wait(tensor, "act")
        cx.inc(tensor.matmul(psA[0:1, 131:132], H["s_onesf"][0:2, 0:1],
                             H["s_m2"][0:2, 1:2], start=True, stop=True), "pe")

    @block.scalar
    def _(scalar):
        cx.wait(scalar, "pe")
        cx.inc(scalar.activation(H["s_m2"][0:1, 2:3], psA[0:1, 131:132],
                                 AF.Exp), "act")

    @block.vector
    def _(v):
        cx.wait(v, "act")
        cx.inc(v.tensor_scalar(H["s_m2"][0:1, 3:4], H["s_m2"][0:1, 2:3],
                               -coef, None, ALU.mult), "dve")

    @block.tensor
    def _(tensor):
        cx.wait(tensor, "dve")
        cx.inc(tensor.matmul(psA[:, 130:131], H["s_ones"][:],
                             H["s_m2"][0:1, 3:4], start=True, stop=True), "pe")

    @block.vector
    def _(v):
        cx.wait(v, "pe")
        cx.inc(v.tensor_copy(H["s_sm"][:, dst_col:dst_col + 1],
                             psA[:, 130:131]), "dve")


def attn1(nc, block, cx, H, D, dbg):
    rA, xb, yT, psA, psB = H["rA"], H["xb"], H["yT"], H["psA"], H["psB"]
    tab_c = _ar(H, 0, (128, T), BF16)
    tab_s = _ar(H, 4, (128, T), BF16)
    scr = _ar(H, 8, (128, 2048), BF16)
    qlat = _ar(H, 12, (128, 2048), F32)
    vb2 = _ar(H, 12, (128, 2048), BF16)
    tmpf = _ar(H, 20, (128, 2048), F32)
    msk = _ar(H, 28, (128, 2048), BF16)
    ylab = _ar(H, 32, (128, 2048), BF16)
    lnrow = _ar(H, 40, (1, 512), F32)
    recr = _ar(H, 42, (1, 512), F32)
    recb = _ar(H, 44, (64, 512), F32)
    qfk = _p2(H, 0, (128, 2048), BF16)
    k4 = _p2(H, 4096, (128, 2048), BF16)
    v4 = _p2(H, 8192, (128, 2112), BF16)
    pt = _p2(H, 12416, (128, 2048), BF16)

    def xbr(cc, n):
        return xb.ap()[:, cc * T + n * 512: cc * T + (n + 1) * 512]

    loaders = []
    for g in range(4):
        for sec in (0, 8, 16):
            for sub in range(2):
                loaders.append(mk_loader(cx, block, H, D, "caw",
                                         sec + 2 * g + sub))
    for o in range(NCH):
        loaders.append(mk_loader(cx, block, H, D, "cpw", o))
    ws = WS(cx, block, H, D, loaders)

    for g in range(4):
        for path, w4, bias_i in [("q", H["s_W4q"], 0), ("k", H["s_W4kv"], 1)]:
            sec = 0 if path == "q" else 8
            for sub in range(2):
                o = sec + 2 * g + sub
                ws.mm(g * 6 + (0 if path == "q" else 2) + sub, psA, xbr)

                @block.scalar
                def _(scalar, o=o):
                    cx.wait(scalar, "pe")
                    cx.inc(scalar.activation(scr[:], psA[:, 0:2048], AF.Identity,
                                             bias=H["s_bias"][:, o:o + 1]),
                           "act")

                @block.tensor
                def _(tensor, w4=w4, sub=sub):
                    cx.wait(tensor, "act")
                    for n in range(4):
                        ins = tensor.matmul(psB[:, n * 512:(n + 1) * 512],
                                            w4[:, sub * 128:(sub + 1) * 128],
                                            scr[:, n * 512:(n + 1) * 512],
                                            start=(sub == 0), stop=(sub == 1))
                    cx.inc(ins, "pe")

            @block.scalar
            def _(scalar, bias_i=bias_i):
                cx.wait(scalar, "pe")
                cx.inc(scalar.activation(qlat, psB[:, 0:2048], AF.Identity,
                                         bias=H["s_sm"][:, bias_i:bias_i + 1]),
                       "act")

            @block.vector
            def _(v):
                cx.wait(v, "act")
                cx.inc(v.tensor_copy(scr[:], qlat), "dve")

            @block.tensor
            def _(tensor):
                cx.wait(tensor, "dve")
                for n in range(4):
                    ins = tensor.matmul(psA[:, n * 512:(n + 1) * 512],
                                        H["s_R1"][:],
                                        scr[:, n * 512:(n + 1) * 512],
                                        start=True, stop=True)
                cx.inc(ins, "pe")

            @block.vector
            def _(v, path=path):
                cx.wait(v, "pe")
                v.tensor_tensor(tmpf, psA[:, 0:2048], tab_s, ALU.mult)
                cx.wait(v, "act")
                v.tensor_tensor(qlat, qlat, tab_c, ALU.mult)
                dest = scr[:] if path == "q" else k4
                cx.inc(v.tensor_tensor(dest, tmpf, qlat, ALU.add), "dve")

            if path == "q":
                @block.tensor
                def _(tensor):
                    cx.wait(tensor, "dve")
                    for n in range(4):
                        ins = tensor.matmul(psB[:, n * 512:(n + 1) * 512],
                                            H["s_wqk4"][:],
                                            scr[:, n * 512:(n + 1) * 512],
                                            start=True, stop=True)
                    cx.inc(ins, "pe")

                @block.scalar
                def _(scalar):
                    cx.wait(scalar, "pe")
                    cx.inc(scalar.activation(qfk, psB[:, 0:2048], AF.Identity,
                                             bias=H["s_sm"][:, 2:3]), "act")

        _stabilizer(block, cx, H, qfk, k4, 1.02 * math.sqrt(LAT), 4)

        for sub in range(2):
            o = 16 + 2 * g + sub
            ws.mm(g * 6 + 4 + sub, psA, xbr)

            @block.scalar
            def _(scalar, sub=sub, o=o):
                cx.wait(scalar, "pe")
                dst = scr[:] if sub == 0 else vb2[:]
                cx.inc(scalar.activation(dst, psA[:, 0:2048], AF.Identity,
                                         bias=H["s_bias"][:, o:o + 1]), "act")

        @block.tensor
        def _(tensor):
            cx.wait(tensor, "act")
            cx.wait(tensor, "dve")
            for tt in range(NTT):
                for s2, buf in [(0, scr[:]), (1, vb2[:])]:
                    ins = tensor.matmul(
                        psB[:, tt * 128:(tt + 1) * 128],
                        buf[:, tt * 128:(tt + 1) * 128],
                        H["s_W4kv"][:, s2 * 128:(s2 + 1) * 128],
                        start=(s2 == 0), stop=(s2 == 1))
            cx.inc(ins, "pe")

        @block.vector
        def _(v):
            v.memset(v4.rearrange("p (tt h l) -> p tt h l", tt=16, h=4)[
                :, :, :, 32:33], 1.0)
            cx.wait(v, "pe")
            src = psB[:, 0:2048].rearrange("p (tt h l) -> p tt h l", tt=16, h=4)
            dst = v4.rearrange("p (tt h l) -> p tt h l", tt=16, h=4)[
                :, :, :, 0:32]
            b = H["s_bv4"].ap().rearrange("p (h l) -> p h l", h=4)[
                :, :, 0:32].unsqueeze(1).broadcast_to([128, 16, 4, 32])
            cx.inc(v.tensor_tensor(dst, src, b, ALU.add), "dve")

        negM = H["s_sm"][:, 4:5]
        row40 = _ar(H, 36, (1, 2048), F32)

        def s_mms(tensor, g, qs, kt):
            for h in range(4):
                ins = tensor.matmul(
                    psA[:, h * 512:(h + 1) * 512],
                    k4[32 * h:32 * h + 32, kt * 128:(kt + 1) * 128],
                    qfk[32 * h:32 * h + 32, qs * 512:(qs + 1) * 512],
                    start=True, stop=True, tile_position=(32 * h, 0))
            return ins

        for qs in range(NQS):
            kmax = qs * 4 + 3

            @block.tensor
            def _(tensor, qs=qs, g=g):
                cx.wait(tensor, "act")
                cx.wait(tensor, "dve")
                cx.mark[("S", g)] = cx.inc(s_mms(tensor, g, qs, 0), "pe")

            for kt in range(kmax + 1):
                diag = kt - qs * 4

                @block.scalar
                def _(scalar, g=g):
                    cx.wait(scalar, "pe", cx.mark[("S", g)])
                    m = cx.mark.get(("msk", g))
                    if m is not None:
                        cx.wait(scalar, "dve", m)
                    ins = scalar.activation(pt[:, 0:2048], psA[:, 0:2048],
                                            AF.Exp, bias=negM, scale=EXP_SC1)
                    cx.mark[("exp", g)] = cx.inc(ins, "act")

                if diag >= 0:
                    @block.vector
                    def _(v, diag=diag, g=g):
                        cx.wait(v, "act", cx.mark[("exp", g)])
                        for h in range(4):
                            ins = v.tensor_tensor(
                                pt[:, h * 512:(h + 1) * 512],
                                pt[:, h * 512:(h + 1) * 512],
                                msk[:, diag * 512:(diag + 1) * 512], ALU.mult)
                        cx.mark[("msk", g)] = cx.inc(ins, "dve")

                @block.tensor
                def _(tensor, qs=qs, kt=kt, diag=diag, g=g, kmax=kmax):
                    if diag >= 0:
                        cx.wait(tensor, "dve", cx.mark[("msk", g)])
                    else:
                        cx.wait(tensor, "act", cx.mark[("exp", g)])
                    if kt == 0:
                        m = cx.mark.get(("rec", g))
                        if m is not None:
                            cx.wait(tensor, "dve", m)
                        m = cx.mark.get(("tailLn", g))
                        if m is not None:
                            cx.wait(tensor, "act", m)
                    if kt < kmax:
                        s_mms(tensor, g, qs, kt + 1)
                    for h in range(4):
                        ins = tensor.matmul(
                            psB[0:33, h * 512:(h + 1) * 512],
                            v4[:, kt * 132 + h * 33: kt * 132 + h * 33 + 33],
                            pt[:, h * 512:(h + 1) * 512],
                            start=(kt == 0), stop=(kt == kmax))
                    m = cx.inc(ins, "pe")
                    cx.mark[("S", g)] = m
                    cx.mark[("pv", g)] = m

            @block.scalar
            def _(scalar, g=g):
                cx.wait(scalar, "pe", cx.mark[("pv", g)])
                cx.mark[("tailLn", g)] = cx.inc(
                    scalar.activation(row40, psB[32:33, 0:2048], AF.Ln), "act")

            @block.tensor
            def _(tensor, g=g):
                cx.wait(tensor, "act", cx.mark[("tailLn", g)])
                for h in range(4):
                    ins = tensor.matmul(psA[64:96, h * 512:(h + 1) * 512],
                                        H["s_ones"][:, 0:32],
                                        row40[0:1, h * 512:(h + 1) * 512],
                                        start=True, stop=True)
                cx.mark[("tailB", g)] = cx.inc(ins, "pe")

            @block.scalar
            def _(scalar, g=g):
                cx.wait(scalar, "pe", cx.mark[("tailB", g)])
                cx.mark[("tailA", g)] = cx.inc(
                    scalar.activation(tmpf[0:32, :], psA[64:96, 0:2048],
                                      AF.Exp, scale=-1.0), "act")

            @block.vector
            def _(v, qs=qs, g=g):
                cx.wait(v, "act", cx.mark[("tailA", g)])
                for h in range(4):
                    ins = v.tensor_tensor(
                        ylab[32 * h:32 * h + 32, qs * 512:(qs + 1) * 512],
                        psB[0:32, h * 512:(h + 1) * 512],
                        tmpf[0:32, h * 512:(h + 1) * 512], ALU.mult)
                m = cx.inc(ins, "dve")
                cx.mark[("yl", g)] = m
                cx.mark[("rec", g)] = m

        for qs in range(NQS):
            @block.tensor
            def _(tensor, qs=qs, g=g):
                cx.wait(tensor, "dve", cx.mark[("yl", g)])
                if qs > 0:
                    cx.wait(tensor, "act", cx.mark[("yT", g)])
                for h in range(4):
                    ins = tensor.matmul(
                        psA[0:64, h * 512:(h + 1) * 512],
                        H["s_l2o4"][32 * h:32 * h + 32, :],
                        ylab[32 * h:32 * h + 32, qs * 512:(qs + 1) * 512],
                        start=True, stop=True, tile_position=(32 * h, 0))
                cx.mark[("l2o", g)] = cx.inc(ins, "pe")

            @block.scalar
            def _(scalar, qs=qs, g=g):
                cx.wait(scalar, "pe", cx.mark[("l2o", g)])
                for h in range(4):
                    hg = 4 * g + h
                    chunk = hg // 2
                    r0 = 64 * (hg % 2)
                    ins = scalar.activation(
                        yT.ap()[r0:r0 + 64,
                                chunk * T + qs * 512: chunk * T + (qs + 1) * 512],
                        psA[0:64, h * 512:(h + 1) * 512], AF.Identity,
                        bias=H["s_sm"][r0:r0 + 64, 3:4])
                cx.mark[("yT", g)] = cx.inc(ins, "act")

    def ytr(cc, n):
        return yT.ap()[:, cc * T + n * 512: cc * T + (n + 1) * 512]

    for o in range(NCH):
        ps = psA if o % 2 == 0 else psB
        ws.mm(24 + o, ps, ytr)

        @block.scalar
        def _(scalar, o=o, ps=ps):
            cx.wait(scalar, "pe")
            cx.wait(scalar, "dve")
            cx.inc(scalar.activation(tmpf, ps[:, 0:2048], AF.Identity,
                                     bias=H["s_bias"][:, 24 + o:25 + o]), "act")

        @block.vector
        def _(v, o=o):
            cx.wait(v, "act")
            cx.inc(v.tensor_tensor(rA.ap()[:, o * T:(o + 1) * T],
                                   rA.ap()[:, o * T:(o + 1) * T],
                                   tmpf, ALU.add), "dve")

    barrier_all(block, cx)
    tap(block, cx, H, dbg, "a1_sm", H["s_sm"].ap()[:, 0:8])
    tap(block, cx, H, dbg, "a1_m2", H["s_m2"].ap()[0:2, 0:8])
    tap(block, cx, H, dbg, "a1_neg", H["s_sm"].ap()[:, 4:5])
    tap(block, cx, H, dbg, "a1_yl", ylab)
    tap(block, cx, H, dbg, "a1_pt", pt)
    tap(block, cx, H, dbg, "a1_v4", v4)
    tap(block, cx, H, dbg, "a1_qfk", qfk)
    tap(block, cx, H, dbg, "a1_k4", k4)
    tap(block, cx, H, dbg, "a1_yT0", yT.ap()[:, 0:2048])
    tap(block, cx, H, dbg, "x1pre0", rA.ap()[:, 0:2048])


def _stats_rows_bcast(block, cx, H, want_mean, eps, src_bf):
    psA, psB = H["psA"], H["psB"]
    sq2 = [_ar(H, 8, (128, 2048), BF16), _ar(H, 32, (128, 2048), BF16)]
    meanb = _ar(H, 12, (128, 2048), F32)
    rstdb = _ar(H, 20, (128, 2048), F32)
    tmprow = _ar(H, 0, (1, 2048), F32)
    mmk = {}

    for c in range(NCH):
        sq = sq2[c % 2]

        @block.scalar
        def _(scalar, c=c, sq=sq):
            if c == 0:
                cx.wait(scalar, "dve")
                cx.wait(scalar, "pe")
            elif c >= 2:
                cx.wait(scalar, "pe", mmk[c - 2])
            cx.inc(scalar.activation(sq[:], src_bf(c), AF.Square), "act")

        @block.tensor
        def _(tensor, c=c, sq=sq):
            cx.wait(tensor, "act")
            for n in range(4):
                ins = tensor.matmul(psB[0:1, n * 512:(n + 1) * 512],
                                    H["s_onesb"][:],
                                    sq[:, n * 512:(n + 1) * 512],
                                    start=(c == 0), stop=(c == NCH - 1))
            if want_mean:
                for n in range(4):
                    ins = tensor.matmul(psA[0:1, n * 512:(n + 1) * 512],
                                        H["s_onesb"][:],
                                        src_bf(c)[:, n * 512:(n + 1) * 512],
                                        start=(c == 0), stop=(c == NCH - 1))
            mmk[c] = cx.inc(ins, "pe")

    @block.vector
    def _(v):
        cx.wait(v, "pe")
        if want_mean:
            cx.chain(v, v.tensor_scalar(meanb[0:1, :], psA[0:1, 0:2048],
                                        1.0 / C, None, ALU.mult))
            cx.chain(v, v.tensor_scalar(rstdb[0:1, :], psB[0:1, 0:2048],
                                        1.0 / C, None, ALU.mult))
            cx.chain(v, v.tensor_tensor(tmprow, meanb[0:1, :], meanb[0:1, :],
                                        ALU.mult))
            cx.chain(v, v.tensor_tensor(rstdb[0:1, :], rstdb[0:1, :], tmprow,
                                        ALU.subtract))
        else:
            cx.chain(v, v.tensor_scalar(rstdb[0:1, :], psB[0:1, 0:2048],
                                        1.0 / C, None, ALU.mult))
        cx.inc(v.tensor_scalar(rstdb[0:1, :], rstdb[0:1, :], eps, None,
                               ALU.add), "dve")

    @block.scalar
    def _(scalar):
        cx.wait(scalar, "dve")
        cx.chain(scalar, scalar.activation(rstdb[0:1, :], rstdb[0:1, :],
                                           AF.Ln), "act")
        cx.inc(scalar.activation(rstdb[0:1, :], rstdb[0:1, :], AF.Exp,
                                 scale=-0.5), "act")

    @block.tensor
    def _(tensor):
        cx.wait(tensor, "act")
        cx.wait(tensor, "dve")
        for n in range(4):
            ins = tensor.matmul(psB[:, n * 512:(n + 1) * 512], H["s_ones"][:],
                                rstdb[0:1, n * 512:(n + 1) * 512],
                                start=True, stop=True)
            if want_mean:
                ins = tensor.matmul(psA[:, n * 512:(n + 1) * 512],
                                    H["s_ones"][:],
                                    meanb[0:1, n * 512:(n + 1) * 512],
                                    start=True, stop=True)
        cx.inc(ins, "pe")

    @block.scalar
    def _(scalar):
        cx.wait(scalar, "pe")
        if want_mean:
            scalar.copy(meanb, psA[:, 0:2048])
        cx.inc(scalar.copy(rstdb, psB[:, 0:2048]), "act")

    return meanb, rstdb


def ln1_rms1(nc, block, cx, H, D, dbg):
    rA, xb = H["rA"], H["xb"]
    tmpf = _ar(H, 0, (128, 2048), F32)

    def mk_src(c):
        return xb.ap()[:, c * T:(c + 1) * T]

    for c in range(NCH):
        @block.vector
        def _(v, c=c):
            if c == 0:
                cx.wait(v, "pe")
                cx.wait(v, "act")
            cx.inc(v.tensor_copy(mk_src(c), rA.ap()[:, c * T:(c + 1) * T]),
                   "dve")
    meanb, rstdb = _stats_rows_bcast(block, cx, H, True, 1e-5, mk_src)

    tmpf2 = _ar(H, 32, (128, 2048), F32)
    for c in range(NCH):
        eng = "vector" if c % 2 == 0 else "gpsimd"
        tf = tmpf if c % 2 == 0 else tmpf2

        @getattr(block, eng)
        def _(v, c=c, tf=tf, eng=eng):
            cx.wait(v, "act")
            v.tensor_tensor(tf, rA.ap()[:, c * T:(c + 1) * T], meanb,
                            ALU.subtract)
            cx.inc(v.tensor_tensor(rA.ap()[:, c * T:(c + 1) * T], tf, rstdb,
                                   ALU.mult), "dve" if eng == "vector" else "gp")
    barrier_all(block, cx)
    tap(block, cx, H, dbg, "x1_0", rA.ap()[:, 0:2048])

    for c in range(NCH):
        @block.vector
        def _(v, c=c):
            cx.inc(v.tensor_copy(xb.ap()[:, c * T:(c + 1) * T],
                                 rA.ap()[:, c * T:(c + 1) * T]), "dve")
    _, rstdb = _stats_rows_bcast(block, cx, H, False, 1e-6, mk_src)

    for c in range(NCH):
        eng = "vector" if c % 2 == 0 else "gpsimd"
        tf = tmpf if c % 2 == 0 else tmpf2

        @getattr(block, eng)
        def _(v, c=c, tf=tf, eng=eng):
            cx.wait(v, "act")
            v.tensor_tensor(tf, rA.ap()[:, c * T:(c + 1) * T], rstdb,
                            ALU.mult)
            cx.inc(v.tensor_scalar(xb.ap()[:, c * T:(c + 1) * T], tf,
                                   H["s_bias"][:, 32 + c:33 + c], None,
                                   ALU.mult), "dve" if eng == "vector" else "gp")
    barrier_all(block, cx)
    tap(block, cx, H, dbg, "xn1_0", xb.ap()[:, 0:2048])


def attn2(nc, block, cx, H, D, dbg):
    rA, xb, yT, psA, psB = H["rA"], H["xb"], H["yT"], H["psA"], H["psB"]
    tab_c = _ar(H, 0, (128, T), BF16)
    tab_s = _ar(H, 4, (128, T), BF16)
    scr = _ar(H, 8, (128, 2048), BF16)
    qlat = _ar(H, 12, (128, 2048), F32)
    tmpf = _ar(H, 20, (128, 2048), F32)
    msk = _ar(H, 28, (128, 2048), BF16)
    skv = _ar(H, 32, (128, 2 * T), BF16)
    lnrow = _ar(H, 40, (1, 512), F32)
    recr = _ar(H, 42, (1, 512), F32)
    recb = _ar(H, 44, (64, 512), F32)
    q2 = _p2(H, 0, (128, 2048), BF16)
    k2 = _p2(H, 4096, (128, 2048), BF16)
    v2 = _p2(H, 8192, (128, 2112), BF16)
    pt = _p2(H, 12416, (128, 2048), BF16)

    def xbr(cc, n):
        return xb.ap()[:, cc * T + n * 512: cc * T + (n + 1) * 512]

    @block.sync
    def _(sync):
        for k in ["pe", "act", "dve", "gp"]:
            cx.wait(sync, k)
        cx.dinc(sync.dma_start(out=tab_c, in_=D["c_tab2_c"]), "xl")
        cx.dinc(sync.dma_start(out=tab_s, in_=D["c_tab2_s"]), "xl")
    tab_ld = cx.cnt["xl"]

    loaders = [mk_loader(cx, block, H, D, "Wkv", 0),
               mk_loader(cx, block, H, D, "Wkv", 1)]
    for p in range(NE):
        loaders.append(mk_loader(cx, block, H, D, "Wq", p))
        loaders.append(mk_loader(cx, block, H, D, "Wk2", 0, sub=p, kchunks=2))
        loaders.append(mk_loader(cx, block, H, D, "Wv2", 0, sub=p, kchunks=2))
    for o in range(NCH):
        loaders.append(mk_loader(cx, block, H, D, "Wo", o))
    ws = WS(cx, block, H, D, loaders)

    for o2 in range(2):
        ps = psA if o2 == 0 else psB
        ws.mm(o2, ps, xbr)

        @block.scalar
        def _(scalar, o2=o2, ps=ps):
            cx.wait(scalar, "pe")
            cx.inc(scalar.activation(skv[:, o2 * T:(o2 + 1) * T],
                                     ps[:, 0:2048], AF.Copy, scale=0.25),
                   "act")

    for p in range(NE):
        kq = 2 + 3 * p
        ws.mm(kq, psA, xbr)

        for path in ("q", "k"):
            if path == "q":
                @block.scalar
                def _(scalar):
                    cx.wait(scalar, "pe")
                    cx.wait(scalar, "xl", tab_ld)
                    cx.inc(scalar.copy(qlat, psA[:, 0:2048]), "act")
            else:
                def skvr(cc, n):
                    return skv[:, cc * T + n * 512: cc * T + (n + 1) * 512]
                ws.mm(2 + 3 * p + 1, psA, skvr, kchunks=2)

                @block.scalar
                def _(scalar):
                    cx.wait(scalar, "pe")
                    cx.inc(scalar.copy(qlat, psA[:, 0:2048]), "act")

            @block.vector
            def _(v):
                cx.wait(v, "act")
                cx.inc(v.tensor_copy(scr[:], qlat), "dve")

            @block.tensor
            def _(tensor):
                cx.wait(tensor, "dve")
                for n in range(4):
                    ins = tensor.matmul(psB[:, n * 512:(n + 1) * 512],
                                        H["s_R2"][:],
                                        scr[:, n * 512:(n + 1) * 512],
                                        start=True, stop=True)
                cx.inc(ins, "pe")

            @block.vector
            def _(v, path=path):
                cx.wait(v, "pe")
                cx.wait(v, "xl", tab_ld)
                v.tensor_tensor(tmpf, psB[:, 0:2048], tab_s, ALU.mult)
                cx.wait(v, "act")
                v.tensor_tensor(qlat, qlat, tab_c, ALU.mult)
                dest = q2 if path == "q" else k2
                cx.inc(v.tensor_tensor(dest, tmpf, qlat, ALU.add), "dve")

        kv2 = 2 + 3 * p + 2
        slot_v2 = kv2 % 4

        @block.tensor
        def _(tensor, kv2=kv2, slot_v2=slot_v2):
            cx.wait(tensor, "wl", ws.wl[kv2])
            cx.wait(tensor, "dve")
            w = wslot(H, slot_v2)
            for tt in range(NTT):
                for lc in range(2):
                    ins = tensor.matmul(
                        psB[:, tt * 128:(tt + 1) * 128],
                        skv[:, lc * T + tt * 128: lc * T + (tt + 1) * 128],
                        w[:, lc * 128:(lc + 1) * 128],
                        start=(lc == 0), stop=(lc == 1))
            cx.mark[("slot", slot_v2)] = cx.inc(ins, "pe")
        ws.emit()

        @block.vector
        def _(v):
            v2v = v2[:, 0:2080].rearrange("p (tt h l) -> p tt h l", tt=16, h=2)
            v.memset(v2v[:, :, :, 64:65], 1.0)
            cx.wait(v, "pe")
            src = psB[:, 0:2048].rearrange("p (tt h l) -> p tt h l", tt=16, h=2)
            cx.inc(v.tensor_copy(v2v[:, :, :, 0:64], src), "dve")

        negM = H["s_sm"][:, 5:6]
        row40 = _ar(H, 40, (1, 1024), F32)

        def s2_mms(tensor, p, qs, kts):
            for kt in kts:
                for a in (0, 1):
                    j = 2 * (kt % 2) + a
                    ins = tensor.matmul(
                        psA[:, j * 512:(j + 1) * 512],
                        k2[64 * a:64 * a + 64, kt * 128:(kt + 1) * 128],
                        q2[64 * a:64 * a + 64, qs * 512:(qs + 1) * 512],
                        start=True, stop=True, tile_position=(64 * a, 0))
            return ins

        for qs in range(NQS):
            kmax = qs * 4 + 3
            nkd = (kmax + 1) // 2

            @block.tensor
            def _(tensor, qs=qs, p=p):
                cx.wait(tensor, "act")
                cx.wait(tensor, "dve")
                cx.mark[("S2", p)] = cx.inc(s2_mms(tensor, p, qs, (0, 1)), "pe")

            for kd in range(nkd):
                kts = (2 * kd, 2 * kd + 1)

                @block.scalar
                def _(scalar, p=p):
                    cx.wait(scalar, "pe", cx.mark[("S2", p)])
                    m = cx.mark.get(("msk2", p))
                    if m is not None:
                        cx.wait(scalar, "dve", m)
                    ins = scalar.activation(pt[:, 0:2048], psA[:, 0:2048],
                                            AF.Exp, bias=negM, scale=EXP_SC2)
                    cx.mark[("exp2", p)] = cx.inc(ins, "act")

                need_mask = [kt for kt in kts if kt - qs * 4 >= 0]
                if need_mask:
                    @block.vector
                    def _(v, qs=qs, p=p, need_mask=need_mask):
                        cx.wait(v, "act", cx.mark[("exp2", p)])
                        for kt in need_mask:
                            diag = kt - qs * 4
                            for a in (0, 1):
                                j = 2 * (kt % 2) + a
                                ins = v.tensor_tensor(
                                    pt[:, j * 512:(j + 1) * 512],
                                    pt[:, j * 512:(j + 1) * 512],
                                    msk[:, diag * 512:(diag + 1) * 512],
                                    ALU.mult)
                        cx.mark[("msk2", p)] = cx.inc(ins, "dve")

                @block.tensor
                def _(tensor, qs=qs, kts=kts, p=p, kd=kd, kmax=kmax, nkd=nkd,
                      need_mask=need_mask):
                    if need_mask:
                        cx.wait(tensor, "dve", cx.mark[("msk2", p)])
                    else:
                        cx.wait(tensor, "act", cx.mark[("exp2", p)])
                    if kd == 0:
                        m = cx.mark.get(("rec2", p))
                        if m is not None:
                            cx.wait(tensor, "dve", m)
                        m = cx.mark.get(("tailLn2", p))
                        if m is not None:
                            cx.wait(tensor, "act", m)
                    if kd + 1 < nkd:
                        s2_mms(tensor, p, qs, (2 * kd + 2, 2 * kd + 3))
                    for kt in kts:
                        for a in (0, 1):
                            j = 2 * (kt % 2) + a
                            ins = tensor.matmul(
                                psB[0:65, a * 512:(a + 1) * 512],
                                v2[:, kt * 130 + a * 65:
                                   kt * 130 + a * 65 + 65],
                                pt[:, j * 512:(j + 1) * 512],
                                start=(kt == 0), stop=(kt == kmax))
                    m = cx.inc(ins, "pe")
                    cx.mark[("S2", p)] = m
                    cx.mark[("pv2", p)] = m

            @block.scalar
            def _(scalar, p=p):
                cx.wait(scalar, "pe", cx.mark[("pv2", p)])
                cx.mark[("tailLn2", p)] = cx.inc(
                    scalar.activation(row40, psB[64:65, 0:1024], AF.Ln), "act")

            @block.tensor
            def _(tensor, p=p):
                cx.wait(tensor, "act", cx.mark[("tailLn2", p)])
                for a in (0, 1):
                    ins = tensor.matmul(psA[64:128, a * 512:(a + 1) * 512],
                                        H["s_ones"][:, 0:64],
                                        row40[0:1, a * 512:(a + 1) * 512],
                                        start=True, stop=True)
                cx.mark[("tailB2", p)] = cx.inc(ins, "pe")

            @block.scalar
            def _(scalar, p=p):
                cx.wait(scalar, "pe", cx.mark[("tailB2", p)])
                cx.mark[("tailA2", p)] = cx.inc(
                    scalar.activation(tmpf[0:64, 0:1024], psA[64:128, 0:1024],
                                      AF.Exp, scale=-1.0), "act")

            @block.vector
            def _(v, qs=qs, p=p):
                cx.wait(v, "act", cx.mark[("tailA2", p)])
                for a in (0, 1):
                    ins = v.tensor_tensor(
                        yT.ap()[64 * a:64 * a + 64,
                                p * T + qs * 512: p * T + (qs + 1) * 512],
                        psB[0:64, a * 512:(a + 1) * 512],
                        tmpf[0:64, a * 512:(a + 1) * 512], ALU.mult)
                m = cx.inc(ins, "dve")
                cx.mark[("yl2", p)] = m
                cx.mark[("rec2", p)] = m

    ych = _ar(H, 12, (128, 2048), F32)

    def ytr(cc, n):
        return yT.ap()[:, cc * T + n * 512: cc * T + (n + 1) * 512]

    for o in range(NCH):
        ps = psA if o % 2 == 0 else psB
        ws.mm(2 + 24 + o, ps, ytr)

        @block.scalar
        def _(scalar, ps=ps):
            cx.wait(scalar, "pe")
            cx.wait(scalar, "dve")
            cx.inc(scalar.copy(tmpf, ps[:, 0:2048]), "act")

        @block.vector
        def _(v, o=o):
            cx.wait(v, "act")
            v.tensor_tensor(ych, tmpf, rA.ap()[:, o * T:(o + 1) * T], ALU.add)
            v.tensor_copy(xb.ap()[:, o * T:(o + 1) * T], ych)
            cx.inc(v.tensor_tensor(rA.ap()[:, o * T:(o + 1) * T],
                                   rA.ap()[:, o * T:(o + 1) * T], ych,
                                   ALU.add), "dve")

    barrier_all(block, cx)
    tap(block, cx, H, dbg, "a2_kv", skv[:, 0:2048])
    tap(block, cx, H, dbg, "a2_q2", q2)
    tap(block, cx, H, dbg, "a2_k2", k2)
    tap(block, cx, H, dbg, "a2_v2", v2[:, 0:2080])
    tap(block, cx, H, dbg, "a2_neg", H["s_sm"].ap()[:, 5:6])
    tap(block, cx, H, dbg, "a2_yT0", yT.ap()[:, 0:2048])
    tap(block, cx, H, dbg, "yb0", xb.ap()[:, 0:2048])


def rms2(nc, block, cx, H, D, dbg):
    xb = H["xb"]
    tmpf = _ar(H, 0, (128, 2048), F32)

    def mk_src(c):
        return xb.ap()[:, c * T:(c + 1) * T]

    _, rstdb = _stats_rows_bcast(block, cx, H, False, 1e-6, mk_src)
    tmpf2 = _ar(H, 32, (128, 2048), F32)
    for c in range(NCH):
        eng = "vector" if c % 2 == 0 else "gpsimd"
        tf = tmpf if c % 2 == 0 else tmpf2

        @getattr(block, eng)
        def _(v, c=c, tf=tf, eng=eng):
            cx.wait(v, "act")
            v.tensor_tensor(tf, xb.ap()[:, c * T:(c + 1) * T], rstdb,
                            ALU.mult)
            cx.inc(v.tensor_scalar(xb.ap()[:, c * T:(c + 1) * T], tf,
                                   H["s_bias"][:, 40 + c:41 + c], None,
                                   ALU.mult), "dve" if eng == "vector" else "gp")
    barrier_all(block, cx)
    tap(block, cx, H, dbg, "yn0", xb.ap()[:, 0:2048])


def router(nc, block, cx, H, D, dbg):
    xb, psA, psB = H["xb"], H["psA"], H["psB"]
    lgt = _p2(H, 0, (8, 2048), F32)

    @block.tensor
    def _(tensor):
        cx.wait(tensor, "dve")
        for cc in range(NCH):
            for n in range(4):
                ins = tensor.matmul(psA[0:8, n * 512:(n + 1) * 512],
                                    H["s_Wr"][:, cc * 8:(cc + 1) * 8],
                                    xb.ap()[:, cc * T + n * 512:
                                            cc * T + (n + 1) * 512],
                                    start=(cc == 0), stop=(cc == NCH - 1))
        cx.inc(ins, "pe")

    @block.scalar
    def _(scalar):
        cx.wait(scalar, "pe")
        cx.inc(scalar.activation(lgt, psA[0:8, 0:2048], AF.Identity,
                                 bias=H["s_rb"][:]), "act")

    @block.vector
    def _(v):
        cx.wait(v, "act")
        cx.inc(v.tensor_scalar(lgt, lgt, 50.0, -50.0, ALU.min, ALU.max), "dve")

    lgt32 = _p2(H, 0, (32, 2048), F32)

    @block.tensor
    def _(tensor):
        cx.wait(tensor, "dve")
        for tt in range(NTT):
            ins = tensor.transpose(psB[:, tt * 32:(tt + 1) * 32],
                                   lgt32[:, tt * 128:(tt + 1) * 128],
                                   H["s_id"][0:32, 0:32])
        cx.inc(ins, "pe")

    @block.vector
    def _(v):
        cx.wait(v, "pe")
        v.tensor_copy(H["s_ltm"][:].rearrange("p (t e) -> p t e", e=8),
                      psB[:, 0:512].rearrange("p (t e) -> p t e", e=32)[
                          :, :, 0:8])
        lt3 = H["s_ltm"].ap().rearrange("p (t e) -> p t e", e=8)
        lx4 = H["s_lx"].ap()[:, 0:64].rearrange("p (t e) -> p t e", e=4)
        lx2 = H["s_lx"].ap()[:, 64:96].rearrange("p (t e) -> p t e", e=2)
        cx.chain(v, v.tensor_tensor(lx4, lt3[:, :, 0:4], lt3[:, :, 4:8],
                                    ALU.max))
        cx.chain(v, v.tensor_tensor(lx2, lx4[:, :, 0:2], lx4[:, :, 2:4],
                                    ALU.max))
        cx.chain(v, v.tensor_tensor(H["s_p16"].ap().unsqueeze(2),
                                    lx2[:, :, 0:1], lx2[:, :, 1:2], ALU.max))
        sel3 = H["s_sel"].ap().rearrange("p (t e) -> p t e", e=8)
        cx.chain(v, v.memset(H["s_r16"][:], 0.0))
        for e in range(NE):
            lt_e = lt3[:, :, e:e + 1].squeeze(2)
            cx.chain(v, v.tensor_tensor(H["s_t16"][:], lt_e, H["s_p16"][:],
                                        ALU.is_ge))
            cx.chain(v, v.tensor_scalar(H["s_u16"][:], H["s_r16"][:], -1.0,
                                        1.0, ALU.mult, ALU.add))
            cx.chain(v, v.tensor_tensor(sel3[:, :, e:e + 1].squeeze(2),
                                        H["s_t16"][:], H["s_u16"][:],
                                        ALU.mult))
            cx.chain(v, v.tensor_tensor(H["s_r16"][:], H["s_r16"][:],
                                        sel3[:, :, e:e + 1].squeeze(2),
                                        ALU.add))
        cx.inc(v.tensor_copy(H["s_sc16"][:], H["s_p16"][:]), "dve")

    @block.scalar
    def _(scalar):
        cx.wait(scalar, "dve")
        cx.inc(scalar.activation(H["s_t16"][:], H["s_p16"][:], AF.Exp,
                                 scale=-1.0), "act")

    @block.vector
    def _(v):
        cx.wait(v, "act")
        cx.chain(v, v.tensor_scalar(H["s_u16"][:], H["s_t16"][:], 1.0, None,
                                    ALU.add))
        cx.chain(v, v.reciprocal(H["s_p16"][:], H["s_u16"][:]))
        cx.chain(v, v.tensor_scalar(H["s_p16"][:], H["s_p16"][:], 1e-8,
                                    1.0 - 1e-8, ALU.max, ALU.min))
        cx.chain(v, v.tensor_scalar(H["s_t16"][:], H["s_p16"][:], 0.5 + 1e-8,
                                    None, ALU.add))
        cx.chain(v, v.tensor_scalar(H["s_u16"][:], H["s_t16"][:], 0.5, 2.0,
                                    ALU.max, ALU.min))
        cx.chain(v, v.reciprocal(H["s_a16"][:], H["s_u16"][:]))
        cx.chain(v, v.tensor_tensor(H["s_b16"][:], H["s_p16"][:],
                                    H["s_a16"][:], ALU.mult))
        cx.inc(v.tensor_scalar(H["s_a16"][:], H["s_a16"][:], 0.5, None,
                               ALU.mult), "dve")
    barrier_all(block, cx)
    tap(block, cx, H, dbg, "r_ltm", H["s_ltm"].ap()[:, 0:128])
    tap(block, cx, H, dbg, "r_eq", H["s_lx"].ap()[:, 0:128])
    tap(block, cx, H, dbg, "r_pmax", H["s_sc16"].ap()[:, 0:16])
    tap(block, cx, H, dbg, "r_lgt", _p2(H, 0, (8, 2048), F32))
    tap(block, cx, H, dbg, "sel", H["s_sel"].ap()[:, 0:128])
    tap(block, cx, H, dbg, "p16", H["s_p16"].ap()[:, 0:16])


def moe(nc, block, cx, H, D, dbg):
    rA, xb, psA, psB = H["rA"], H["xb"], H["psA"], H["psB"]
    s_h = _ar(H, 0, (128, 8 * T), BF16)
    srow = _ar(H, 32, (1, 2048), F32)
    scale_bc = _p2(H, 0, (128, 2048), F32)
    s_sil = _p2(H, 8320, (128, 2048), F32)

    def xbr(cc, n):
        return xb.ap()[:, cc * T + n * 512: cc * T + (n + 1) * 512]

    loaders = []
    for e in range(-1, NE):
        w1n, w2n, w3n = ("sw1", "sw2", "sw3") if e < 0 else ("ew1", "ew2", "ew3")
        sub = None if e < 0 else e
        for ih in range(2):
            for i in range(8):
                ic = ih * 8 + i
                loaders.append(mk_loader(cx, block, H, D, w1n, ic, sub=sub))
                loaders.append(mk_loader(cx, block, H, D, w3n, ic, sub=sub))
            for c in range(NCH):
                loaders.append(mk_loader_w2(cx, block, H, D, w2n, ih, c,
                                            sub=sub))
    ws = WS(cx, block, H, D, loaders)
    KPE = 48

    for e in range(-1, NE):
        w1n, w2n, w3n = ("sw1", "sw2", "sw3") if e < 0 else ("ew1", "ew2", "ew3")
        sub = None if e < 0 else e
        kbase = (e + 1) * KPE

        @block.vector
        def _(v, e=e):
            cx.wait(v, "act")
            cx.wait(v, "pe")
            if e < 0:
                cx.inc(v.tensor_copy(H["s_sc16"][:], H["s_a16"][:]), "dve")
            else:
                sel3 = H["s_sel"].ap().rearrange("p (t e) -> p t e", e=8)
                cx.chain(v, v.tensor_tensor(H["s_sc16"][:],
                                            sel3[:, :, e:e + 1].squeeze(2),
                                            H["s_b16"][:], ALU.mult))
                cx.inc(v.tensor_copy(H["s_t16"][:], H["s_sc16"][:]), "dve")

        @block.tensor
        def _(tensor):
            cx.wait(tensor, "dve")
            cx.inc(tensor.transpose(psA[0:16, 0:128], H["s_sc16"][:],
                                    H["s_id"][:]), "pe")

        @block.vector
        def _(v):
            cx.wait(v, "pe")
            cx.inc(v.tensor_copy(H["s_str16"][:], psA[0:16, 0:128]), "dve")

        @block.sync
        def _(sync):
            cx.wait(sync, "dve")
            cx.dinc(sync.dma_start(out=srow, in_=H["s_str16"][:]), "xl")
        row_ld = cx.cnt["xl"]

        @block.tensor
        def _(tensor):
            cx.wait(tensor, "xl", row_ld)
            for n in range(4):
                ins = tensor.matmul(psA[:, n * 512:(n + 1) * 512],
                                    H["s_ones"][:],
                                    srow[:, n * 512:(n + 1) * 512],
                                    start=True, stop=True)
            cx.inc(ins, "pe")

        @block.scalar
        def _(scalar):
            cx.wait(scalar, "pe")
            cx.inc(scalar.copy(scale_bc, psA[:, 0:2048]), "act")

        for ih in range(2):
            for i in range(8):
                k0 = kbase + ih * 24 + 2 * i
                ws.mm(k0, psA, xbr, dve_val=(None if i == 0 else 0))
                mh1 = cx.cnt["pe"]
                ws.mm(k0 + 1, psB, xbr)

                @block.scalar
                def _(scalar, mh1=mh1):
                    cx.wait(scalar, "pe", mh1)
                    cx.wait(scalar, "dve")
                    cx.inc(scalar.activation(s_sil, psA[:, 0:2048], AF.Silu),
                           "act")

                @block.vector
                def _(v, i=i):
                    cx.wait(v, "act")
                    cx.wait(v, "pe")
                    v.tensor_tensor(s_sil, s_sil, scale_bc, ALU.mult)
                    cx.inc(v.tensor_tensor(s_h[:, i * T:(i + 1) * T], s_sil,
                                           psB[:, 0:2048], ALU.mult), "dve")

            madd = {}
            for c in range(NCH):
                ps = psA if c % 2 == 0 else psB
                k2i = kbase + ih * 24 + 16 + c

                def shr(cc, n):
                    return s_h[:, cc * T + n * 512: cc * T + (n + 1) * 512]
                ws.mm(k2i, ps, shr, act_val=0,
                      dve_val=None if c < 2 else madd[c - 2])

                @block.vector
                def _(v, c=c, ps=ps):
                    cx.wait(v, "pe")
                    madd[c] = cx.inc(
                        v.tensor_tensor(rA.ap()[:, c * T:(c + 1) * T],
                                        rA.ap()[:, c * T:(c + 1) * T],
                                        ps[:, 0:2048], ALU.add), "dve")

    barrier_all(block, cx)
    tap(block, cx, H, dbg, "F0", rA.ap()[:, 0:2048])


def ln_final(nc, block, cx, H, D, dbg):
    rA, xb = H["rA"], H["xb"]
    tmpf = _ar(H, 0, (128, 2048), F32)

    def mk_src(c):
        return xb.ap()[:, c * T:(c + 1) * T]

    for c in range(NCH):
        @block.vector
        def _(v, c=c):
            if c == 0:
                cx.wait(v, "pe")
                cx.wait(v, "act")
            cx.inc(v.tensor_copy(mk_src(c), rA.ap()[:, c * T:(c + 1) * T]),
                   "dve")
    meanb, rstdb = _stats_rows_bcast(block, cx, H, True, 1e-5, mk_src)
    tmpf2 = _ar(H, 32, (128, 2048), F32)

    for c in range(NCH):
        eng = "vector" if c % 2 == 0 else "gpsimd"
        tf = tmpf if c % 2 == 0 else tmpf2

        @getattr(block, eng)
        def _(v, c=c, tf=tf, eng=eng):
            cx.wait(v, "act")
            v.tensor_tensor(tf, rA.ap()[:, c * T:(c + 1) * T], meanb,
                            ALU.subtract)
            cx.inc(v.tensor_tensor(rA.ap()[:, c * T:(c + 1) * T], tf, rstdb,
                                   ALU.mult), "dve" if eng == "vector" else "gp")
    barrier_all(block, cx)


def output_transpose(nc, block, cx, H, D, dbg):
    rA, psA, psB = H["rA"], H["psA"], H["psB"]
    out = H["out"]
    stage = _ar(H, 12, (128, 2048), F32)
    st_done, ev_done, tr_done = {}, {}, {}
    for tt in range(NTT):
        buf = tt % 2

        @block.tensor
        def _(tensor, tt=tt):
            if tt >= 1:
                cx.wait(tensor, "act", ev_done[tt - 1])
            for c in range(NCH):
                half = psA if c < 4 else psB
                ins = tensor.transpose(
                    half[:, (c % 4) * 512:(c % 4) * 512 + 128],
                    rA.ap()[:, c * T + tt * 128: c * T + (tt + 1) * 128],
                    H["s_id"][:])
            tr_done[tt] = cx.inc(ins, "pe")

        @block.scalar
        def _(scalar, tt=tt, buf=buf):
            cx.wait(scalar, "pe", tr_done[tt])
            if tt >= 2:
                cx.wait(scalar, "st", st_done[tt - 2])
            for half, c0 in [(psA, 0), (psB, 4)]:
                view = half[:, 0:2048].rearrange("p (c n) -> p c n", c=4)[
                    :, :, 0:128]
                dst = stage.rearrange("p (b c n) -> p b c n", b=2, c=8)[
                    :, buf, c0:c0 + 4, :]
                ins = scalar.copy(dst, view)
            ev_done[tt] = cx.inc(ins, "act")

        @block.sync
        def _(sync, tt=tt, buf=buf):
            cx.wait(sync, "act", ev_done[tt])
            cx.dinc(sync.dma_start(out=out[tt * 128:(tt + 1) * 128, :],
                                   in_=stage[:, buf * 1024:(buf + 1) * 1024]),
                    "st")
            st_done[tt] = cx.cnt["st"]

    @block.sync
    def _(sync):
        cx.wait(sync, "st")


def kernel(**inputs):
    w = _prep(inputs)
    consts = _host_consts()
    x_full = np.asarray(inputs["L0_attn_x"])
    shared = dict(w)
    for k, v in consts.items():
        shared["c_" + k] = v

    nc, dbg = build_program(debug=[(n, _DBG_SHAPES[n]) for n in DEBUG])
    in_maps = []
    for b in range(8):
        m = dict(shared)
        m["x"] = np.ascontiguousarray(x_full[b])
        in_maps.append(m)
    res = run_bass_kernel_spmd(nc, in_maps, core_ids=list(range(8)))
    outs = np.stack([np.ascontiguousarray(r["out"]) for r in res.results],
                    axis=0)
    kernel._last = res
    return outs


_DBG_SHAPES = {
    "r_eq": (128, 128),
    "r_pmax": (128, 16),
    "r_ltm": (128, 128),
    "r_lgt": (8, 2048),
    "a2_kv": (128, 2048),
    "a2_q2": (128, 2048),
    "a2_k2": (128, 2048),
    "a2_v2": (128, 2080),
    "a2_neg": (128, 1),
    "a2_yT0": (128, 2048),
    "a1_sm": (128, 8),
    "a1_m2": (2, 8),
    "a1_neg": (128, 1),
    "a1_yl": (128, 2048),
    "a1_pt": (128, 2048),
    "a1_v4": (128, 2112),
    "xT0": (128, 2048),
    "a1_qfk": (128, 2048),
    "a1_k4": (128, 2048),
    "a1_yT0": (128, 2048),
    "x1pre0": (128, 2048),
    "x1_0": (128, 2048),
    "xn1_0": (128, 2048),
    "yb0": (128, 2048),
    "yn0": (128, 2048),
    "sel": (128, 128),
    "p16": (128, 16),
    "F0": (128, 2048),
}


# version 07 (final)
# speedup vs baseline: 1.4179x; 1.0513x over previous
import contextlib
import math

import numpy as np
import ml_dtypes

import concourse.bass as bass
import concourse.mybir as mybir
from concourse.bass_utils import run_bass_kernel_spmd

F32 = mybir.dt.float32
BF16 = mybir.dt.bfloat16
AF = mybir.ActivationFunctionType
ALU = mybir.AluOpType
AX = mybir.AxisListType

T = 2048
C = 1024
NCH = 8
NTT = 16
NQS = 4
LAT = 32
HD = 64
I_FF = 2048
NE = 8

EXP_SC1 = 1.0 / math.sqrt(LAT)
EXP_SC2 = 1.0 / math.sqrt(HD)

DEBUG = []
BF = ml_dtypes.bfloat16


def _host_consts():
    c = {}
    inv = 1.0 / 10000.0 ** (np.arange(0, LAT, 2, dtype=np.float64) / LAT)
    t = np.arange(T, dtype=np.float64)
    fr = np.outer(t, inv)
    cos1 = np.repeat(np.cos(fr), 2, axis=-1)[:, :LAT]
    sin1 = np.repeat(np.sin(fr), 2, axis=-1)[:, :LAT]
    c["tab1_c"] = np.tile(cos1.T.astype(np.float32), (4, 1)).astype(BF)
    c["tab1_s"] = np.tile(sin1.T.astype(np.float32), (4, 1)).astype(BF)
    R32 = np.zeros((LAT, LAT), np.float32)
    for i in range(LAT // 2):
        R32[2 * i + 1, 2 * i] = -1.0
        R32[2 * i, 2 * i + 1] = 1.0
    c["R1"] = np.kron(np.eye(4, dtype=np.float32), R32).astype(BF)
    inv2 = 1.0 / 100000.0 ** (np.arange(0, HD, 2, dtype=np.float64) / HD)
    fr2 = np.outer(t, inv2)
    C64 = np.concatenate([np.cos(fr2), np.cos(fr2)], axis=1).T.astype(np.float32)
    S64 = np.concatenate([np.sin(fr2), np.sin(fr2)], axis=1).T.astype(np.float32)
    c["tab2_c"] = np.tile(C64, (2, 1)).astype(BF)
    c["tab2_s"] = np.tile(S64, (2, 1)).astype(BF)
    R64 = np.zeros((HD, HD), np.float32)
    for d in range(32):
        R64[d + 32, d] = -1.0
        R64[d, d + 32] = 1.0
    c["R2"] = np.kron(np.eye(2, dtype=np.float32), R64).astype(BF)
    kk = np.arange(128)[:, None]
    qq = np.arange(512)[None, :]
    for j, d in enumerate([0, 128, 256, 384]):
        c[f"mask{j}"] = ((d + kk) <= qq).astype(BF)
    c["ident"] = np.eye(128, dtype=np.float32)
    c["ones_row"] = np.ones((1, 128), np.float32)
    c["ones_col_bf"] = np.ones((128, 1), np.float32).astype(BF)
    c["ones_col_f"] = np.ones((128, 1), np.float32)
    return c


def _prep(inp):
    w = {}
    f = np.ascontiguousarray
    w["caw"] = f(np.asarray(inp["L0_attn_c_attn_w"]).T)
    w["cab"] = f(np.asarray(inp["L0_attn_c_attn_b"]).reshape(24, 128).T)
    q2l = np.asarray(inp["L0_attn_q2l_w"])
    kv2l = np.asarray(inp["L0_attn_kv2l_w"])
    blk = np.zeros((256, 128), np.float32)
    for h in range(4):
        blk[h * 64:(h + 1) * 64, h * 32:(h + 1) * 32] = q2l.T
    w["W4q"] = f(blk)
    blk = np.zeros((256, 128), np.float32)
    for h in range(4):
        blk[h * 64:(h + 1) * 64, h * 32:(h + 1) * 32] = kv2l.T
    w["W4kv"] = f(blk)
    w["q2l_b4"] = f(np.tile(np.asarray(inp["L0_attn_q2l_b"]), 4).reshape(128, 1))
    w["kv2l_b4"] = f(np.tile(np.asarray(inp["L0_attn_kv2l_b"]), 4).reshape(128, 1))
    row = np.zeros((1, 132), np.float32)
    for h in range(4):
        row[0, h * 33:h * 33 + 32] = np.asarray(inp["L0_attn_kv2l_b"])
    w["kv2l_brow"] = row
    blkq = np.zeros((128, 128), np.float32)
    for h in range(4):
        blkq[h * 32:(h + 1) * 32, h * 32:(h + 1) * 32] = np.asarray(
            inp["L0_attn_wqk_w"]).T
    w["wqk4"] = f(blkq)
    w["wqk_b4"] = f(np.tile(np.asarray(inp["L0_attn_wqk_b"]), 4).reshape(128, 1))
    l2oT = np.asarray(inp["L0_attn_l2o_w"]).T
    w["l2o4"] = f(np.tile(l2oT, (4, 1)))
    w["l2o_b2"] = f(np.tile(np.asarray(inp["L0_attn_l2o_b"]), 2).reshape(128, 1))
    w["cpw"] = f(np.asarray(inp["L0_attn_cproj_w"]).T)
    w["cpb"] = f(np.asarray(inp["L0_attn_cproj_b"]).reshape(8, 128).T)
    w["ln1w"] = f(np.asarray(inp["L0_ffn_ln1_w"]).reshape(8, 128).T)
    w["ln2w"] = f(np.asarray(inp["L0_ffn_ln2_w"]).reshape(8, 128).T)
    w["Wq"] = f(np.asarray(inp["L0_ffn_Wq"]))
    w["Wkv"] = f(np.asarray(inp["L0_ffn_Wkv"]))
    w["Wk2"] = f(np.asarray(inp["L0_ffn_Wk_up"]).reshape(8, 2, 256, 64)
                 .transpose(0, 2, 1, 3).reshape(8, 256, 128))
    w["Wv2"] = f(np.asarray(inp["L0_ffn_Wv_up"]).reshape(8, 2, 256, 64)
                 .transpose(0, 2, 1, 3).reshape(8, 256, 128))
    w["Wo"] = f(np.asarray(inp["L0_ffn_Wo"]))
    w["Wr"] = f(np.asarray(inp["L0_ffn_Wr"]))
    w["r_bias"] = f(np.asarray(inp["L0_ffn_r_bias"]).reshape(8, 1))
    w["sw1"] = f(np.asarray(inp["L0_ffn_sw1"]))
    w["sw2"] = f(np.asarray(inp["L0_ffn_sw2"]))
    w["sw3"] = f(np.asarray(inp["L0_ffn_sw3"]))
    w["ew1"] = f(np.asarray(inp["L0_ffn_ew1"]))
    w["ew2"] = f(np.asarray(inp["L0_ffn_ew2"]))
    w["ew3"] = f(np.asarray(inp["L0_ffn_ew3"]))
    return w


class Ctx:
    def __init__(self):
        self.cnt = dict(pe=0, act=0, dve=0, gp=0, wl=0, xl=0, st=0)
        self.mark = {}

    def inc(self, ins, name):
        self.cnt[name] += 1
        ins.then_inc(self.sem[name], 1)
        return self.cnt[name]

    def dinc(self, ins, name):
        self.cnt[name] += 1
        ins.then_inc(self.sem[name], 16)
        return self.cnt[name]

    def chain(self, eng, ins, name="dve"):
        c = self.inc(ins, name)
        eng.wait_ge(self.sem[name], c)
        return c

    def wait(self, eng, name, val=None):
        v = self.cnt[name] if val is None else val
        if v <= 0:
            return
        if name in ("wl", "xl", "st"):
            eng.wait_ge(self.sem[name], 16 * v)
        else:
            eng.wait_ge(self.sem[name], v)


def _ar(H, off_kb, shape, dt=BF16, p0=0):
    e0 = int(off_kb * 1024) // 2
    n = shape[1] * (2 if dt == F32 else 1)
    v = H["AR"].ap()[:, e0:e0 + n]
    if dt == F32:
        v = v.bitcast(F32)
    return v[p0:p0 + shape[0], :]


def _p2(H, off_b, shape, dt=BF16, p0=0):
    e0 = off_b // 2
    n = shape[1] * (2 if dt == F32 else 1)
    v = H["P2"].ap()[:, e0:e0 + n]
    if dt == F32:
        v = v.bitcast(F32)
    return v[p0:p0 + shape[0], :]


def build_program(debug=()):
    nc = bass.Bass("TRN2", target_bir_lowering=False)
    cx = Ctx()
    consts = _host_consts()
    D = {}

    def din(name, shape, dt=F32):
        D[name] = nc.dram_tensor(name, list(shape), dt, kind="ExternalInput").ap()

    din("x", (T, C))
    din("caw", (C, 3 * C)); din("cab", (128, 24))
    din("W4q", (256, 128)); din("W4kv", (256, 128))
    din("q2l_b4", (128, 1)); din("kv2l_b4", (128, 1)); din("kv2l_brow", (1, 132))
    din("wqk4", (128, 128)); din("wqk_b4", (128, 1))
    din("l2o4", (128, 64)); din("l2o_b2", (128, 1))
    din("cpw", (C, C)); din("cpb", (128, 8))
    din("ln1w", (128, 8)); din("ln2w", (128, 8))
    din("Wq", (C, C)); din("Wkv", (C, 256))
    din("Wk2", (8, 256, 128)); din("Wv2", (8, 256, 128))
    din("Wo", (C, C)); din("Wr", (C, 8)); din("r_bias", (8, 1))
    din("sw1", (C, I_FF)); din("sw2", (I_FF, C)); din("sw3", (C, I_FF))
    din("ew1", (NE, C, I_FF)); din("ew2", (NE, I_FF, C)); din("ew3", (NE, C, I_FF))
    for k, v in consts.items():
        din("c_" + k, v.shape, BF16 if v.dtype == BF else F32)

    out = nc.dram_tensor("out", [T, C], F32, kind="ExternalOutput").ap()
    dbg = {}
    for name, shape in debug:
        dbg[name] = nc.dram_tensor("dbg_" + name, list(shape), F32,
                                   kind="ExternalOutput").ap()

    st = contextlib.ExitStack()
    with st:
        def sb(name, p, f, dt):
            return st.enter_context(nc.sbuf_tensor(name, [p, f], dt))

        rA = sb("rA", 128, NCH * T, F32)
        xb = sb("xb", 128, NCH * T, BF16)
        yT = sb("yT", 128, NCH * T, BF16)
        AR = sb("AR", 128, 23552, BF16)
        P2 = sb("P2", 128, 8320, BF16)
        wsl = sb("wsl", 128, 4096, BF16)
        s_W4q = sb("s_W4q", 128, 256, BF16)
        s_W4kv = sb("s_W4kv", 128, 256, BF16)
        s_wqk4 = sb("s_wqk4", 128, 128, BF16)
        s_l2o4 = sb("s_l2o4", 128, 64, BF16)
        s_R1 = sb("s_R1", 128, 128, BF16)
        s_R2 = sb("s_R2", 128, 128, BF16)
        s_Wr = sb("s_Wr", 128, 64, BF16)
        s_id = sb("s_id", 128, 128, F32)
        s_ones = sb("s_ones", 1, 128, F32)
        s_onesb = sb("s_onesb", 128, 1, BF16)
        s_onesf = sb("s_onesf", 128, 1, F32)
        s_bias = sb("s_bias", 128, 48, F32)
        s_sm = sb("s_sm", 128, 8, F32)
        s_rb = sb("s_rb", 8, 1, F32)
        s_bv4 = sb("s_bv4", 128, 132, F32)
        s_brow = sb("s_brow", 1, 132, F32)
        s_str16 = sb("s_str16", 16, 128, F32)
        s_ltm = sb("s_ltm", 128, 128, F32)
        s_sel = sb("s_sel", 128, 128, F32)
        s_lx = sb("s_lx", 128, 128, F32)
        s_p16 = sb("s_p16", 128, 16, F32)
        s_t16 = sb("s_t16", 128, 16, F32)
        s_a16 = sb("s_a16", 128, 16, F32)
        s_b16 = sb("s_b16", 128, 16, F32)
        s_r16 = sb("s_r16", 128, 16, F32)
        s_sc16 = sb("s_sc16", 128, 16, F32)
        s_u16 = sb("s_u16", 128, 16, F32)
        s_m2 = sb("s_m2", 2, 128, F32)

        psA = st.enter_context(nc.psum_tensor("psA", [128, 2048], F32))
        psB = st.enter_context(nc.psum_tensor("psB", [128, 2048], F32))

        sem = {}
        for name in ["pe", "act", "dve", "gp", "wl", "xl", "st"]:
            sem[name] = st.enter_context(nc.semaphore("s_" + name))
        cx.sem = sem

        block = st.enter_context(nc.Block())
        H = dict(locals())
        H["out"] = out

        ph0(nc, block, cx, H, D, dbg)
        attn1(nc, block, cx, H, D, dbg)
        ln1_rms1(nc, block, cx, H, D, dbg)
        attn2(nc, block, cx, H, D, dbg)
        rms2(nc, block, cx, H, D, dbg)
        router(nc, block, cx, H, D, dbg)
        moe(nc, block, cx, H, D, dbg)
        ln_final(nc, block, cx, H, D, dbg)
        output_transpose(nc, block, cx, H, D, dbg)

    return nc, dbg


def tap(block, cx, H, dbg, name, src_ap):
    if name not in dbg:
        return
    d = dbg[name]

    @block.gpsimd
    def _(gp):
        for k in ["pe", "act", "dve"]:
            cx.wait(gp, k)
        cx.dinc(gp.dma_start(out=d, in_=src_ap), "wl")
        cx.wait(gp, "wl")


def barrier_all(block, cx):
    for engname in ["tensor", "scalar", "vector", "gpsimd", "sync"]:
        @getattr(block, engname)
        def _(e):
            for k in ["pe", "act", "dve", "gp", "wl", "xl"]:
                cx.wait(e, k)


def wslot(H, slot):
    return H["wsl"].ap()[:, slot * 1024:(slot + 1) * 1024]


class WS:
    def __init__(self, cx, block, H, D, loaders, depth=3):
        self.cx, self.block, self.H, self.D = cx, block, H, D
        self.loaders = loaders
        self.nxt = 0
        self.wl = {}
        for _ in range(min(depth, len(loaders))):
            self.emit()

    def emit(self):
        k = self.nxt
        if k < len(self.loaders):
            self.wl[k] = self.loaders[k](k % 4)
            self.nxt += 1

    def mm(self, k, psum, rhs, kchunks=8, act_val=None, dve_val=None):
        cx, block, H = self.cx, self.block, self.H
        slot = k % 4
        wl_cnt = self.wl[k]

        @block.tensor
        def _(tensor):
            cx.wait(tensor, "wl", wl_cnt)
            if act_val != 0:
                cx.wait(tensor, "act", act_val)
            if dve_val != 0:
                cx.wait(tensor, "dve", dve_val)
            w = wslot(H, slot)
            for cc in range(kchunks):
                for n in range(4):
                    ins = tensor.matmul(psum[:, n * 512:(n + 1) * 512],
                                        w[:, cc * 128:(cc + 1) * 128],
                                        rhs(cc, n),
                                        start=(cc == 0),
                                        stop=(cc == kchunks - 1))
            cx.mark[("slot", slot)] = cx.inc(ins, "pe")
        self.emit()


def mk_loader(cx, block, H, D, wname, o, sub=None, kchunks=8):
    def load(slot):
        @block.gpsimd
        def _(gp):
            rel = cx.mark.get(("slot", slot))
            if rel is not None:
                cx.wait(gp, "pe", rel)
            s = D[wname] if sub is None else D[wname][sub]
            ap = s.rearrange("(cc p) o -> p cc o", p=128)[
                :, 0:kchunks, o * 128:(o + 1) * 128]
            dst = wslot(H, slot).rearrange("p (cc o) -> p cc o", cc=8)[
                :, 0:kchunks, :]
            cx.dinc(gp.dma_start(out=dst, in_=ap), "wl")
        return cx.cnt["wl"]
    return load


def mk_loader_w2(cx, block, H, D, wname, ih, c, sub=None):
    def load(slot):
        @block.gpsimd
        def _(gp):
            rel = cx.mark.get(("slot", slot))
            if rel is not None:
                cx.wait(gp, "pe", rel)
            s = D[wname] if sub is None else D[wname][sub]
            ap = s.rearrange("(cc p) o -> p cc o", p=128)[
                :, ih * 8:(ih + 1) * 8, c * 128:(c + 1) * 128]
            dst = wslot(H, slot).rearrange("p (cc o) -> p cc o", cc=8)
            cx.dinc(gp.dma_start(out=dst, in_=ap), "wl")
        return cx.cnt["wl"]
    return load


def ph0(nc, block, cx, H, D, dbg):
    rA, xb, psA, psB = H["rA"], H["xb"], H["psA"], H["psB"]
    stage = _ar(H, 12, (128, 2048), F32)

    @block.sync
    def _(sync):
        for nm, dst in [("c_ident", H["s_id"]), ("c_ones_row", H["s_ones"]),
                        ("c_R1", H["s_R1"]), ("c_R2", H["s_R2"]),
                        ("c_ones_col_bf", H["s_onesb"]), ("r_bias", H["s_rb"]),
                        ("c_ones_col_f", H["s_onesf"])]:
            cx.dinc(sync.dma_start(out=dst[:], in_=D[nm]), "xl")
        msk = _ar(H, 28, (128, 2048), BF16)
        for j in range(4):
            cx.dinc(sync.dma_start(out=msk[:, j * 512:(j + 1) * 512],
                                   in_=D[f"c_mask{j}"]), "xl")
        cx.dinc(sync.dma_start(out=H["s_bias"][:, 0:24], in_=D["cab"]), "xl")
        cx.dinc(sync.dma_start(out=H["s_bias"][:, 24:32], in_=D["cpb"]), "xl")
        cx.dinc(sync.dma_start(out=H["s_bias"][:, 32:40], in_=D["ln1w"]), "xl")
        cx.dinc(sync.dma_start(out=H["s_bias"][:, 40:48], in_=D["ln2w"]), "xl")
        for i, nm in enumerate(["q2l_b4", "kv2l_b4", "wqk_b4", "l2o_b2"]):
            cx.dinc(sync.dma_start(out=H["s_sm"][:, i:i + 1], in_=D[nm]), "xl")
        cx.dinc(sync.dma_start(out=H["s_brow"][:], in_=D["kv2l_brow"]), "xl")
        tabs = _ar(H, 0, (128, 2 * T), BF16)
        for i, nm in enumerate(["tab1_c", "tab1_s"]):
            cx.dinc(sync.dma_start(out=tabs[:, i * T:(i + 1) * T],
                                   in_=D["c_" + nm]), "xl")
        cx.dinc(sync.dma_start(out=stage[:, 0:1024], in_=D["x"][0:128, :]), "xl")
    xl_base = cx.cnt["xl"]

    @block.gpsimd
    def _(gp):
        cx.dinc(gp.dma_start(out=H["s_W4q"][:].rearrange("p (c m) -> p c m", c=2),
                             in_=D["W4q"].rearrange("(c p) m -> p c m", p=128)),
                "wl")
        cx.dinc(gp.dma_start(out=H["s_W4kv"][:].rearrange("p (c m) -> p c m", c=2),
                             in_=D["W4kv"].rearrange("(c p) m -> p c m", p=128)),
                "wl")
        cx.dinc(gp.dma_start(out=H["s_wqk4"][:], in_=D["wqk4"]), "wl")
        cx.dinc(gp.dma_start(out=H["s_l2o4"][:], in_=D["l2o4"]), "wl")
        cx.dinc(gp.dma_start(out=H["s_Wr"][:].rearrange("p (c m) -> p c m", c=8),
                             in_=D["Wr"].rearrange("(c p) m -> p c m", p=128)),
                "wl")

    @block.tensor
    def _(tensor):
        cx.wait(tensor, "xl")
        cx.inc(tensor.matmul(psB[:, 0:132], H["s_ones"][:], H["s_brow"][:],
                             start=True, stop=True), "pe")

    @block.vector
    def _(v):
        cx.wait(v, "pe")
        cx.inc(v.tensor_copy(H["s_bv4"][:], psB[:, 0:132]), "dve")

    tr_done, ev_done = {}, {}
    for tt in range(NTT):
        buf = tt % 2

        @block.sync
        def _(sync, tt=tt, buf=buf):
            if tt + 1 < NTT:
                if tt >= 1:
                    cx.wait(sync, "pe", tr_done[tt - 1])
                cx.dinc(sync.dma_start(
                    out=stage[:, (1 - buf) * 1024:(2 - buf) * 1024],
                    in_=D["x"][(tt + 1) * 128:(tt + 2) * 128, :]), "xl")

        @block.tensor
        def _(tensor, tt=tt, buf=buf):
            cx.wait(tensor, "xl", xl_base + tt)
            cx.wait(tensor, "dve")
            if tt > 0:
                cx.wait(tensor, "act", ev_done[tt - 1])
            for c in range(NCH):
                half = psA if c < 4 else psB
                ins = tensor.transpose(
                    half[:, (c % 4) * 512:(c % 4) * 512 + 128],
                    stage[:, buf * 1024 + c * 128: buf * 1024 + (c + 1) * 128],
                    H["s_id"][:])
            tr_done[tt] = cx.inc(ins, "pe")

        @block.scalar
        def _(scalar, tt=tt):
            cx.wait(scalar, "pe", tr_done[tt])
            for half, c0 in [(psA, 0), (psB, 4)]:
                view = half[:, 0:2048].rearrange("p (c n) -> p c n", c=4)[
                    :, :, 0:128]
                dst = rA.ap().rearrange("p (c t) -> p c t", c=8)[
                    :, c0:c0 + 4, tt * 128:(tt + 1) * 128]
                ins = scalar.copy(dst, view)
            ev_done[tt] = cx.inc(ins, "act")

        @block.vector
        def _(v, tt=tt):
            cx.wait(v, "act", ev_done[tt])
            dst = xb.ap().rearrange("p (c t) -> p c t", c=8)[
                :, :, tt * 128:(tt + 1) * 128]
            src = rA.ap().rearrange("p (c t) -> p c t", c=8)[
                :, :, tt * 128:(tt + 1) * 128]
            cx.inc(v.tensor_copy(dst, src), "dve")

    barrier_all(block, cx)
    tap(block, cx, H, dbg, "xT0", rA.ap()[:, 0:2048])


def _stabilizer(block, cx, H, a_src, b_src, coef, dst_col):
    psA = H["psB"]

    @block.vector
    def _(v):
        cx.wait(v, "act")
        cx.wait(v, "dve")
        v.tensor_reduce(H["s_sm"][:, 6:7], a_src, AX.X, ALU.max,
                        apply_absolute_value=True)
        cx.inc(v.tensor_reduce(H["s_sm"][:, 7:8], b_src, AX.X, ALU.max,
                               apply_absolute_value=True), "dve")

    @block.tensor
    def _(tensor):
        cx.wait(tensor, "dve")
        cx.inc(tensor.transpose(psA[0:2, 0:128], H["s_sm"][:, 6:8],
                                H["s_id"][:]), "pe")

    @block.vector
    def _(v):
        cx.wait(v, "pe")
        cx.inc(v.tensor_reduce(H["s_m2"][:, 0:1], psA[0:2, 0:128],
                               AX.X, ALU.max), "dve")

    @block.scalar
    def _(scalar):
        cx.wait(scalar, "dve")
        cx.inc(scalar.activation(H["s_m2"][0:2, 1:2], H["s_m2"][0:2, 0:1],
                                 AF.Ln), "act")

    @block.tensor
    def _(tensor):
        cx.wait(tensor, "act")
        cx.inc(tensor.matmul(psA[0:1, 131:132], H["s_onesf"][0:2, 0:1],
                             H["s_m2"][0:2, 1:2], start=True, stop=True), "pe")

    @block.scalar
    def _(scalar):
        cx.wait(scalar, "pe")
        cx.inc(scalar.activation(H["s_m2"][0:1, 2:3], psA[0:1, 131:132],
                                 AF.Exp), "act")

    @block.vector
    def _(v):
        cx.wait(v, "act")
        cx.inc(v.tensor_scalar(H["s_m2"][0:1, 3:4], H["s_m2"][0:1, 2:3],
                               -coef, None, ALU.mult), "dve")

    @block.tensor
    def _(tensor):
        cx.wait(tensor, "dve")
        cx.inc(tensor.matmul(psA[:, 130:131], H["s_ones"][:],
                             H["s_m2"][0:1, 3:4], start=True, stop=True), "pe")

    @block.vector
    def _(v):
        cx.wait(v, "pe")
        cx.inc(v.tensor_copy(H["s_sm"][:, dst_col:dst_col + 1],
                             psA[:, 130:131]), "dve")


def attn1(nc, block, cx, H, D, dbg):
    rA, xb, yT, psA, psB = H["rA"], H["xb"], H["yT"], H["psA"], H["psB"]
    tab_c = _ar(H, 0, (128, T), BF16)
    tab_s = _ar(H, 4, (128, T), BF16)
    scr = _ar(H, 8, (128, 2048), BF16)
    qlat = _ar(H, 12, (128, 2048), F32)
    vb2 = _ar(H, 12, (128, 2048), BF16)
    tmpf = _ar(H, 20, (128, 2048), F32)
    msk = _ar(H, 28, (128, 2048), BF16)
    ylab = _ar(H, 32, (128, 2048), BF16)
    lnrow = _ar(H, 40, (1, 512), F32)
    recr = _ar(H, 42, (1, 512), F32)
    recb = _ar(H, 44, (64, 512), F32)
    qfk = _p2(H, 0, (128, 2048), BF16)
    k4 = _p2(H, 4096, (128, 2048), BF16)
    v4 = _p2(H, 8192, (128, 2112), BF16)
    pt = _p2(H, 12416, (128, 2048), BF16)

    def xbr(cc, n):
        return xb.ap()[:, cc * T + n * 512: cc * T + (n + 1) * 512]

    loaders = []
    for g in range(4):
        for sec in (0, 8, 16):
            for sub in range(2):
                loaders.append(mk_loader(cx, block, H, D, "caw",
                                         sec + 2 * g + sub))
    for o in range(NCH):
        loaders.append(mk_loader(cx, block, H, D, "cpw", o))
    ws = WS(cx, block, H, D, loaders)

    for g in range(4):
        for path, w4, bias_i in [("q", H["s_W4q"], 0), ("k", H["s_W4kv"], 1)]:
            sec = 0 if path == "q" else 8
            for sub in range(2):
                o = sec + 2 * g + sub
                ws.mm(g * 6 + (0 if path == "q" else 2) + sub, psA, xbr)

                @block.scalar
                def _(scalar, o=o):
                    cx.wait(scalar, "pe")
                    cx.inc(scalar.activation(scr[:], psA[:, 0:2048], AF.Identity,
                                             bias=H["s_bias"][:, o:o + 1]),
                           "act")

                @block.tensor
                def _(tensor, w4=w4, sub=sub):
                    cx.wait(tensor, "act")
                    for n in range(4):
                        ins = tensor.matmul(psB[:, n * 512:(n + 1) * 512],
                                            w4[:, sub * 128:(sub + 1) * 128],
                                            scr[:, n * 512:(n + 1) * 512],
                                            start=(sub == 0), stop=(sub == 1))
                    cx.inc(ins, "pe")

            @block.scalar
            def _(scalar, bias_i=bias_i):
                cx.wait(scalar, "pe")
                cx.inc(scalar.activation(qlat, psB[:, 0:2048], AF.Identity,
                                         bias=H["s_sm"][:, bias_i:bias_i + 1]),
                       "act")

            @block.vector
            def _(v):
                cx.wait(v, "act")
                cx.inc(v.tensor_copy(scr[:], qlat), "dve")

            @block.tensor
            def _(tensor):
                cx.wait(tensor, "dve")
                for n in range(4):
                    ins = tensor.matmul(psA[:, n * 512:(n + 1) * 512],
                                        H["s_R1"][:],
                                        scr[:, n * 512:(n + 1) * 512],
                                        start=True, stop=True)
                cx.inc(ins, "pe")

            @block.vector
            def _(v, path=path):
                cx.wait(v, "pe")
                v.tensor_tensor(tmpf, psA[:, 0:2048], tab_s, ALU.mult)
                cx.wait(v, "act")
                v.tensor_tensor(qlat, qlat, tab_c, ALU.mult)
                dest = scr[:] if path == "q" else k4
                cx.inc(v.tensor_tensor(dest, tmpf, qlat, ALU.add), "dve")

            if path == "q":
                @block.tensor
                def _(tensor):
                    cx.wait(tensor, "dve")
                    for n in range(4):
                        ins = tensor.matmul(psB[:, n * 512:(n + 1) * 512],
                                            H["s_wqk4"][:],
                                            scr[:, n * 512:(n + 1) * 512],
                                            start=True, stop=True)
                    cx.inc(ins, "pe")

                @block.scalar
                def _(scalar):
                    cx.wait(scalar, "pe")
                    cx.inc(scalar.activation(qfk, psB[:, 0:2048], AF.Identity,
                                             bias=H["s_sm"][:, 2:3]), "act")

        _stabilizer(block, cx, H, qfk, k4, 1.02 * math.sqrt(LAT), 4)

        for sub in range(2):
            o = 16 + 2 * g + sub
            ws.mm(g * 6 + 4 + sub, psA, xbr)

            @block.scalar
            def _(scalar, sub=sub, o=o):
                cx.wait(scalar, "pe")
                dst = scr[:] if sub == 0 else vb2[:]
                cx.inc(scalar.activation(dst, psA[:, 0:2048], AF.Identity,
                                         bias=H["s_bias"][:, o:o + 1]), "act")

        @block.tensor
        def _(tensor):
            cx.wait(tensor, "act")
            cx.wait(tensor, "dve")
            for tt in range(NTT):
                for s2, buf in [(0, scr[:]), (1, vb2[:])]:
                    ins = tensor.matmul(
                        psB[:, tt * 128:(tt + 1) * 128],
                        buf[:, tt * 128:(tt + 1) * 128],
                        H["s_W4kv"][:, s2 * 128:(s2 + 1) * 128],
                        start=(s2 == 0), stop=(s2 == 1))
            cx.inc(ins, "pe")

        @block.vector
        def _(v):
            v.memset(v4.rearrange("p (tt h l) -> p tt h l", tt=16, h=4)[
                :, :, :, 32:33], 1.0)
            cx.wait(v, "pe")
            src = psB[:, 0:2048].rearrange("p (tt h l) -> p tt h l", tt=16, h=4)
            dst = v4.rearrange("p (tt h l) -> p tt h l", tt=16, h=4)[
                :, :, :, 0:32]
            b = H["s_bv4"].ap().rearrange("p (h l) -> p h l", h=4)[
                :, :, 0:32].unsqueeze(1).broadcast_to([128, 16, 4, 32])
            cx.inc(v.tensor_tensor(dst, src, b, ALU.add), "dve")

        negM = H["s_sm"][:, 4:5]
        row40 = _ar(H, 36, (1, 2048), F32)

        for qs in range(NQS):
            kmax = qs * 4 + 3
            for kt in range(kmax + 1):
                diag = kt - qs * 4

                @block.tensor
                def _(tensor, qs=qs, kt=kt, g=g):
                    if kt == 0 and qs == 0:
                        cx.wait(tensor, "act")
                        cx.wait(tensor, "dve")
                    m = cx.mark.get(("exp", g))
                    if m is not None:
                        cx.wait(tensor, "act", m)
                    for h in range(4):
                        ins = tensor.matmul(
                            psA[:, h * 512:(h + 1) * 512],
                            k4[32 * h:32 * h + 32, kt * 128:(kt + 1) * 128],
                            qfk[32 * h:32 * h + 32, qs * 512:(qs + 1) * 512],
                            start=True, stop=True, tile_position=(32 * h, 0))
                    cx.mark[("S", g)] = cx.inc(ins, "pe")

                @block.scalar
                def _(scalar, g=g):
                    cx.wait(scalar, "pe", cx.mark[("S", g)])
                    m = cx.mark.get(("pv", g))
                    if m is not None:
                        cx.wait(scalar, "pe", m)
                    m = cx.mark.get(("msk", g))
                    if m is not None:
                        cx.wait(scalar, "dve", m)
                    ins = scalar.activation(pt[:, 0:2048], psA[:, 0:2048],
                                            AF.Exp, bias=negM, scale=EXP_SC1)
                    cx.mark[("exp", g)] = cx.inc(ins, "act")

                if diag >= 0:
                    @block.vector
                    def _(v, diag=diag, g=g):
                        cx.wait(v, "act", cx.mark[("exp", g)])
                        for h in range(4):
                            ins = v.tensor_tensor(
                                pt[:, h * 512:(h + 1) * 512],
                                pt[:, h * 512:(h + 1) * 512],
                                msk[:, diag * 512:(diag + 1) * 512], ALU.mult)
                        cx.mark[("msk", g)] = cx.inc(ins, "dve")

                @block.tensor
                def _(tensor, qs=qs, kt=kt, diag=diag, g=g, kmax=kmax):
                    if diag >= 0:
                        cx.wait(tensor, "dve", cx.mark[("msk", g)])
                    else:
                        cx.wait(tensor, "act", cx.mark[("exp", g)])
                    if kt == 0:
                        m = cx.mark.get(("rec", g))
                        if m is not None:
                            cx.wait(tensor, "dve", m)
                        m = cx.mark.get(("tailA", g))
                        if m is not None:
                            cx.wait(tensor, "act", m)
                    for h in range(4):
                        ins = tensor.matmul(
                            psB[0:33, h * 512:(h + 1) * 512],
                            v4[:, kt * 132 + h * 33: kt * 132 + h * 33 + 33],
                            pt[:, h * 512:(h + 1) * 512],
                            start=(kt == 0), stop=(kt == kmax))
                    cx.mark[("pv", g)] = cx.inc(ins, "pe")

            @block.scalar
            def _(scalar, g=g):
                cx.wait(scalar, "pe", cx.mark[("pv", g)])
                cx.mark[("tailLn", g)] = cx.inc(
                    scalar.activation(row40, psB[32:33, 0:2048], AF.Ln), "act")

            @block.tensor
            def _(tensor, g=g):
                cx.wait(tensor, "act", cx.mark[("tailLn", g)])
                for h in range(4):
                    ins = tensor.matmul(psB[64:96, h * 512:(h + 1) * 512],
                                        H["s_ones"][:, 0:32],
                                        row40[0:1, h * 512:(h + 1) * 512],
                                        start=True, stop=True)
                cx.mark[("tailB", g)] = cx.inc(ins, "pe")

            @block.scalar
            def _(scalar, g=g):
                cx.wait(scalar, "pe", cx.mark[("tailB", g)])
                cx.mark[("tailA", g)] = cx.inc(
                    scalar.activation(tmpf[0:32, :], psB[64:96, 0:2048],
                                      AF.Exp, scale=-1.0), "act")

            @block.vector
            def _(v, qs=qs, g=g):
                cx.wait(v, "act", cx.mark[("tailA", g)])
                for h in range(4):
                    ins = v.tensor_tensor(
                        ylab[32 * h:32 * h + 32, qs * 512:(qs + 1) * 512],
                        psB[0:32, h * 512:(h + 1) * 512],
                        tmpf[0:32, h * 512:(h + 1) * 512], ALU.mult)
                m = cx.inc(ins, "dve")
                cx.mark[("yl", g)] = m
                cx.mark[("rec", g)] = m

        for qs in range(NQS):
            @block.tensor
            def _(tensor, qs=qs, g=g):
                cx.wait(tensor, "dve", cx.mark[("yl", g)])
                if qs > 0:
                    cx.wait(tensor, "act", cx.mark[("yT", g)])
                for h in range(4):
                    ins = tensor.matmul(
                        psA[0:64, h * 512:(h + 1) * 512],
                        H["s_l2o4"][32 * h:32 * h + 32, :],
                        ylab[32 * h:32 * h + 32, qs * 512:(qs + 1) * 512],
                        start=True, stop=True, tile_position=(32 * h, 0))
                cx.mark[("l2o", g)] = cx.inc(ins, "pe")

            @block.scalar
            def _(scalar, qs=qs, g=g):
                cx.wait(scalar, "pe", cx.mark[("l2o", g)])
                for h in range(4):
                    hg = 4 * g + h
                    chunk = hg // 2
                    r0 = 64 * (hg % 2)
                    ins = scalar.activation(
                        yT.ap()[r0:r0 + 64,
                                chunk * T + qs * 512: chunk * T + (qs + 1) * 512],
                        psA[0:64, h * 512:(h + 1) * 512], AF.Identity,
                        bias=H["s_sm"][r0:r0 + 64, 3:4])
                cx.mark[("yT", g)] = cx.inc(ins, "act")

    def ytr(cc, n):
        return yT.ap()[:, cc * T + n * 512: cc * T + (n + 1) * 512]

    for o in range(NCH):
        ps = psA if o % 2 == 0 else psB
        ws.mm(24 + o, ps, ytr)

        @block.scalar
        def _(scalar, o=o, ps=ps):
            cx.wait(scalar, "pe")
            cx.wait(scalar, "dve")
            cx.inc(scalar.activation(tmpf, ps[:, 0:2048], AF.Identity,
                                     bias=H["s_bias"][:, 24 + o:25 + o]), "act")

        @block.vector
        def _(v, o=o):
            cx.wait(v, "act")
            cx.inc(v.tensor_tensor(rA.ap()[:, o * T:(o + 1) * T],
                                   rA.ap()[:, o * T:(o + 1) * T],
                                   tmpf, ALU.add), "dve")

    barrier_all(block, cx)
    tap(block, cx, H, dbg, "a1_sm", H["s_sm"].ap()[:, 0:8])
    tap(block, cx, H, dbg, "a1_m2", H["s_m2"].ap()[0:2, 0:8])
    tap(block, cx, H, dbg, "a1_neg", H["s_sm"].ap()[:, 4:5])
    tap(block, cx, H, dbg, "a1_yl", ylab)
    tap(block, cx, H, dbg, "a1_pt", pt)
    tap(block, cx, H, dbg, "a1_v4", v4)
    tap(block, cx, H, dbg, "a1_qfk", qfk)
    tap(block, cx, H, dbg, "a1_k4", k4)
    tap(block, cx, H, dbg, "a1_yT0", yT.ap()[:, 0:2048])
    tap(block, cx, H, dbg, "x1pre0", rA.ap()[:, 0:2048])


def _stats_rows_bcast(block, cx, H, want_mean, eps, src_bf):
    psA, psB = H["psA"], H["psB"]
    sq2 = [_ar(H, 8, (128, 2048), BF16), _ar(H, 32, (128, 2048), BF16)]
    meanb = _ar(H, 12, (128, 2048), F32)
    rstdb = _ar(H, 20, (128, 2048), F32)
    tmprow = _ar(H, 0, (1, 2048), F32)
    mmk = {}

    for c in range(NCH):
        sq = sq2[c % 2]

        @block.scalar
        def _(scalar, c=c, sq=sq):
            if c == 0:
                cx.wait(scalar, "dve")
                cx.wait(scalar, "pe")
            elif c >= 2:
                cx.wait(scalar, "pe", mmk[c - 2])
            cx.inc(scalar.activation(sq[:], src_bf(c), AF.Square), "act")

        @block.tensor
        def _(tensor, c=c, sq=sq):
            cx.wait(tensor, "act")
            for n in range(4):
                ins = tensor.matmul(psB[0:1, n * 512:(n + 1) * 512],
                                    H["s_onesb"][:],
                                    sq[:, n * 512:(n + 1) * 512],
                                    start=(c == 0), stop=(c == NCH - 1))
            if want_mean:
                for n in range(4):
                    ins = tensor.matmul(psA[0:1, n * 512:(n + 1) * 512],
                                        H["s_onesb"][:],
                                        src_bf(c)[:, n * 512:(n + 1) * 512],
                                        start=(c == 0), stop=(c == NCH - 1))
            mmk[c] = cx.inc(ins, "pe")

    @block.vector
    def _(v):
        cx.wait(v, "pe")
        if want_mean:
            cx.chain(v, v.tensor_scalar(meanb[0:1, :], psA[0:1, 0:2048],
                                        1.0 / C, None, ALU.mult))
            cx.chain(v, v.tensor_scalar(rstdb[0:1, :], psB[0:1, 0:2048],
                                        1.0 / C, None, ALU.mult))
            cx.chain(v, v.tensor_tensor(tmprow, meanb[0:1, :], meanb[0:1, :],
                                        ALU.mult))
            cx.chain(v, v.tensor_tensor(rstdb[0:1, :], rstdb[0:1, :], tmprow,
                                        ALU.subtract))
        else:
            cx.chain(v, v.tensor_scalar(rstdb[0:1, :], psB[0:1, 0:2048],
                                        1.0 / C, None, ALU.mult))
        cx.inc(v.tensor_scalar(rstdb[0:1, :], rstdb[0:1, :], eps, None,
                               ALU.add), "dve")

    @block.scalar
    def _(scalar):
        cx.wait(scalar, "dve")
        cx.chain(scalar, scalar.activation(rstdb[0:1, :], rstdb[0:1, :],
                                           AF.Ln), "act")
        cx.inc(scalar.activation(rstdb[0:1, :], rstdb[0:1, :], AF.Exp,
                                 scale=-0.5), "act")

    @block.tensor
    def _(tensor):
        cx.wait(tensor, "act")
        cx.wait(tensor, "dve")
        for n in range(4):
            ins = tensor.matmul(psB[:, n * 512:(n + 1) * 512], H["s_ones"][:],
                                rstdb[0:1, n * 512:(n + 1) * 512],
                                start=True, stop=True)
            if want_mean:
                ins = tensor.matmul(psA[:, n * 512:(n + 1) * 512],
                                    H["s_ones"][:],
                                    meanb[0:1, n * 512:(n + 1) * 512],
                                    start=True, stop=True)
        cx.inc(ins, "pe")

    @block.scalar
    def _(scalar):
        cx.wait(scalar, "pe")
        if want_mean:
            scalar.copy(meanb, psA[:, 0:2048])
        cx.inc(scalar.copy(rstdb, psB[:, 0:2048]), "act")

    return meanb, rstdb


def ln1_rms1(nc, block, cx, H, D, dbg):
    rA, xb = H["rA"], H["xb"]
    tmpf = _ar(H, 0, (128, 2048), F32)

    def mk_src(c):
        return xb.ap()[:, c * T:(c + 1) * T]

    for c in range(NCH):
        @block.vector
        def _(v, c=c):
            if c == 0:
                cx.wait(v, "pe")
                cx.wait(v, "act")
            cx.inc(v.tensor_copy(mk_src(c), rA.ap()[:, c * T:(c + 1) * T]),
                   "dve")
    meanb, rstdb = _stats_rows_bcast(block, cx, H, True, 1e-5, mk_src)

    for c in range(NCH):
        @block.vector
        def _(v, c=c):
            cx.wait(v, "act")
            v.tensor_tensor(tmpf, rA.ap()[:, c * T:(c + 1) * T], meanb,
                            ALU.subtract)
            cx.inc(v.tensor_tensor(rA.ap()[:, c * T:(c + 1) * T], tmpf, rstdb,
                                   ALU.mult), "dve")
    barrier_all(block, cx)
    tap(block, cx, H, dbg, "x1_0", rA.ap()[:, 0:2048])

    for c in range(NCH):
        @block.vector
        def _(v, c=c):
            cx.inc(v.tensor_copy(xb.ap()[:, c * T:(c + 1) * T],
                                 rA.ap()[:, c * T:(c + 1) * T]), "dve")
    _, rstdb = _stats_rows_bcast(block, cx, H, False, 1e-6, mk_src)

    for c in range(NCH):
        @block.vector
        def _(v, c=c):
            cx.wait(v, "act")
            v.tensor_tensor(tmpf, rA.ap()[:, c * T:(c + 1) * T], rstdb,
                            ALU.mult)
            cx.inc(v.tensor_scalar(xb.ap()[:, c * T:(c + 1) * T], tmpf,
                                   H["s_bias"][:, 32 + c:33 + c], None,
                                   ALU.mult), "dve")
    barrier_all(block, cx)
    tap(block, cx, H, dbg, "xn1_0", xb.ap()[:, 0:2048])


def attn2(nc, block, cx, H, D, dbg):
    rA, xb, yT, psA, psB = H["rA"], H["xb"], H["yT"], H["psA"], H["psB"]
    tab_c = _ar(H, 0, (128, T), BF16)
    tab_s = _ar(H, 4, (128, T), BF16)
    scr = _ar(H, 8, (128, 2048), BF16)
    qlat = _ar(H, 12, (128, 2048), F32)
    tmpf = _ar(H, 20, (128, 2048), F32)
    msk = _ar(H, 28, (128, 2048), BF16)
    skv = _ar(H, 32, (128, 2 * T), BF16)
    lnrow = _ar(H, 40, (1, 512), F32)
    recr = _ar(H, 42, (1, 512), F32)
    recb = _ar(H, 44, (64, 512), F32)
    q2 = _p2(H, 0, (128, 2048), BF16)
    k2 = _p2(H, 4096, (128, 2048), BF16)
    v2 = _p2(H, 8192, (128, 2112), BF16)
    pt = _p2(H, 12416, (128, 2048), BF16)

    def xbr(cc, n):
        return xb.ap()[:, cc * T + n * 512: cc * T + (n + 1) * 512]

    @block.sync
    def _(sync):
        for k in ["pe", "act", "dve", "gp"]:
            cx.wait(sync, k)
        cx.dinc(sync.dma_start(out=tab_c, in_=D["c_tab2_c"]), "xl")
        cx.dinc(sync.dma_start(out=tab_s, in_=D["c_tab2_s"]), "xl")
    tab_ld = cx.cnt["xl"]

    loaders = [mk_loader(cx, block, H, D, "Wkv", 0),
               mk_loader(cx, block, H, D, "Wkv", 1)]
    for p in range(NE):
        loaders.append(mk_loader(cx, block, H, D, "Wq", p))
        loaders.append(mk_loader(cx, block, H, D, "Wk2", 0, sub=p, kchunks=2))
        loaders.append(mk_loader(cx, block, H, D, "Wv2", 0, sub=p, kchunks=2))
    for o in range(NCH):
        loaders.append(mk_loader(cx, block, H, D, "Wo", o))
    ws = WS(cx, block, H, D, loaders)

    for o2 in range(2):
        ps = psA if o2 == 0 else psB
        ws.mm(o2, ps, xbr)

        @block.scalar
        def _(scalar, o2=o2, ps=ps):
            cx.wait(scalar, "pe")
            cx.inc(scalar.activation(skv[:, o2 * T:(o2 + 1) * T],
                                     ps[:, 0:2048], AF.Copy, scale=0.25),
                   "act")

    for p in range(NE):
        kq = 2 + 3 * p
        ws.mm(kq, psA, xbr)

        for path in ("q", "k"):
            if path == "q":
                @block.scalar
                def _(scalar):
                    cx.wait(scalar, "pe")
                    cx.wait(scalar, "xl", tab_ld)
                    cx.inc(scalar.copy(qlat, psA[:, 0:2048]), "act")
            else:
                def skvr(cc, n):
                    return skv[:, cc * T + n * 512: cc * T + (n + 1) * 512]
                ws.mm(2 + 3 * p + 1, psA, skvr, kchunks=2)

                @block.scalar
                def _(scalar):
                    cx.wait(scalar, "pe")
                    cx.inc(scalar.copy(qlat, psA[:, 0:2048]), "act")

            @block.vector
            def _(v):
                cx.wait(v, "act")
                cx.inc(v.tensor_copy(scr[:], qlat), "dve")

            @block.tensor
            def _(tensor):
                cx.wait(tensor, "dve")
                for n in range(4):
                    ins = tensor.matmul(psB[:, n * 512:(n + 1) * 512],
                                        H["s_R2"][:],
                                        scr[:, n * 512:(n + 1) * 512],
                                        start=True, stop=True)
                cx.inc(ins, "pe")

            @block.vector
            def _(v, path=path):
                cx.wait(v, "pe")
                cx.wait(v, "xl", tab_ld)
                v.tensor_tensor(tmpf, psB[:, 0:2048], tab_s, ALU.mult)
                cx.wait(v, "act")
                v.tensor_tensor(qlat, qlat, tab_c, ALU.mult)
                dest = q2 if path == "q" else k2
                cx.inc(v.tensor_tensor(dest, tmpf, qlat, ALU.add), "dve")

        kv2 = 2 + 3 * p + 2
        slot_v2 = kv2 % 4

        @block.tensor
        def _(tensor, kv2=kv2, slot_v2=slot_v2):
            cx.wait(tensor, "wl", ws.wl[kv2])
            cx.wait(tensor, "dve")
            w = wslot(H, slot_v2)
            for tt in range(NTT):
                for lc in range(2):
                    ins = tensor.matmul(
                        psB[:, tt * 128:(tt + 1) * 128],
                        skv[:, lc * T + tt * 128: lc * T + (tt + 1) * 128],
                        w[:, lc * 128:(lc + 1) * 128],
                        start=(lc == 0), stop=(lc == 1))
            cx.mark[("slot", slot_v2)] = cx.inc(ins, "pe")
        ws.emit()

        @block.vector
        def _(v):
            v2v = v2[:, 0:2080].rearrange("p (tt h l) -> p tt h l", tt=16, h=2)
            v.memset(v2v[:, :, :, 64:65], 1.0)
            cx.wait(v, "pe")
            src = psB[:, 0:2048].rearrange("p (tt h l) -> p tt h l", tt=16, h=2)
            cx.inc(v.tensor_copy(v2v[:, :, :, 0:64], src), "dve")

        negM = H["s_sm"][:, 5:6]
        row40 = _ar(H, 40, (1, 1024), F32)

        for qs in range(NQS):
            kmax = qs * 4 + 3
            for kd in range((kmax + 1) // 2):
                kts = (2 * kd, 2 * kd + 1)

                @block.tensor
                def _(tensor, qs=qs, kts=kts, p=p, kd=kd, qs_=qs):
                    if kd == 0 and qs_ == 0:
                        cx.wait(tensor, "act")
                        cx.wait(tensor, "dve")
                    m = cx.mark.get(("exp2", p))
                    if m is not None:
                        cx.wait(tensor, "act", m)
                    for kt in kts:
                        for a in (0, 1):
                            j = 2 * (kt % 2) + a
                            ins = tensor.matmul(
                                psA[:, j * 512:(j + 1) * 512],
                                k2[64 * a:64 * a + 64,
                                   kt * 128:(kt + 1) * 128],
                                q2[64 * a:64 * a + 64,
                                   qs * 512:(qs + 1) * 512],
                                start=True, stop=True,
                                tile_position=(64 * a, 0))
                    cx.mark[("S2", p)] = cx.inc(ins, "pe")

                @block.scalar
                def _(scalar, p=p):
                    cx.wait(scalar, "pe", cx.mark[("S2", p)])
                    m = cx.mark.get(("pv2", p))
                    if m is not None:
                        cx.wait(scalar, "pe", m)
                    m = cx.mark.get(("msk2", p))
                    if m is not None:
                        cx.wait(scalar, "dve", m)
                    ins = scalar.activation(pt[:, 0:2048], psA[:, 0:2048],
                                            AF.Exp, bias=negM, scale=EXP_SC2)
                    cx.mark[("exp2", p)] = cx.inc(ins, "act")

                need_mask = [kt for kt in kts if kt - qs * 4 >= 0]
                if need_mask:
                    @block.vector
                    def _(v, qs=qs, p=p, need_mask=need_mask):
                        cx.wait(v, "act", cx.mark[("exp2", p)])
                        for kt in need_mask:
                            diag = kt - qs * 4
                            for a in (0, 1):
                                j = 2 * (kt % 2) + a
                                ins = v.tensor_tensor(
                                    pt[:, j * 512:(j + 1) * 512],
                                    pt[:, j * 512:(j + 1) * 512],
                                    msk[:, diag * 512:(diag + 1) * 512],
                                    ALU.mult)
                        cx.mark[("msk2", p)] = cx.inc(ins, "dve")

                @block.tensor
                def _(tensor, qs=qs, kts=kts, p=p, kd=kd, kmax=kmax,
                      need_mask=need_mask):
                    if need_mask:
                        cx.wait(tensor, "dve", cx.mark[("msk2", p)])
                    else:
                        cx.wait(tensor, "act", cx.mark[("exp2", p)])
                    if kd == 0:
                        m = cx.mark.get(("rec2", p))
                        if m is not None:
                            cx.wait(tensor, "dve", m)
                        m = cx.mark.get(("tailA2", p))
                        if m is not None:
                            cx.wait(tensor, "act", m)
                    for kt in kts:
                        for a in (0, 1):
                            j = 2 * (kt % 2) + a
                            ins = tensor.matmul(
                                psB[0:65, a * 512:(a + 1) * 512],
                                v2[:, kt * 130 + a * 65:
                                   kt * 130 + a * 65 + 65],
                                pt[:, j * 512:(j + 1) * 512],
                                start=(kt == 0), stop=(kt == kmax))
                    cx.mark[("pv2", p)] = cx.inc(ins, "pe")

            @block.scalar
            def _(scalar, p=p):
                cx.wait(scalar, "pe", cx.mark[("pv2", p)])
                cx.mark[("tailLn2", p)] = cx.inc(
                    scalar.activation(row40, psB[64:65, 0:1024], AF.Ln), "act")

            @block.tensor
            def _(tensor, p=p):
                cx.wait(tensor, "act", cx.mark[("tailLn2", p)])
                for a in (0, 1):
                    ins = tensor.matmul(psB[64:128, a * 512:(a + 1) * 512],
                                        H["s_ones"][:, 0:64],
                                        row40[0:1, a * 512:(a + 1) * 512],
                                        start=True, stop=True)
                cx.mark[("tailB2", p)] = cx.inc(ins, "pe")

            @block.scalar
            def _(scalar, p=p):
                cx.wait(scalar, "pe", cx.mark[("tailB2", p)])
                cx.mark[("tailA2", p)] = cx.inc(
                    scalar.activation(tmpf[0:64, 0:1024], psB[64:128, 0:1024],
                                      AF.Exp, scale=-1.0), "act")

            @block.vector
            def _(v, qs=qs, p=p):
                cx.wait(v, "act", cx.mark[("tailA2", p)])
                for a in (0, 1):
                    ins = v.tensor_tensor(
                        yT.ap()[64 * a:64 * a + 64,
                                p * T + qs * 512: p * T + (qs + 1) * 512],
                        psB[0:64, a * 512:(a + 1) * 512],
                        tmpf[0:64, a * 512:(a + 1) * 512], ALU.mult)
                m = cx.inc(ins, "dve")
                cx.mark[("yl2", p)] = m
                cx.mark[("rec2", p)] = m

    ych = _ar(H, 12, (128, 2048), F32)

    def ytr(cc, n):
        return yT.ap()[:, cc * T + n * 512: cc * T + (n + 1) * 512]

    for o in range(NCH):
        ps = psA if o % 2 == 0 else psB
        ws.mm(2 + 24 + o, ps, ytr)

        @block.scalar
        def _(scalar, ps=ps):
            cx.wait(scalar, "pe")
            cx.wait(scalar, "dve")
            cx.inc(scalar.copy(tmpf, ps[:, 0:2048]), "act")

        @block.vector
        def _(v, o=o):
            cx.wait(v, "act")
            v.tensor_tensor(ych, tmpf, rA.ap()[:, o * T:(o + 1) * T], ALU.add)
            v.tensor_copy(xb.ap()[:, o * T:(o + 1) * T], ych)
            cx.inc(v.tensor_tensor(rA.ap()[:, o * T:(o + 1) * T],
                                   rA.ap()[:, o * T:(o + 1) * T], ych,
                                   ALU.add), "dve")

    barrier_all(block, cx)
    tap(block, cx, H, dbg, "a2_kv", skv[:, 0:2048])
    tap(block, cx, H, dbg, "a2_q2", q2)
    tap(block, cx, H, dbg, "a2_k2", k2)
    tap(block, cx, H, dbg, "a2_v2", v2[:, 0:2080])
    tap(block, cx, H, dbg, "a2_neg", H["s_sm"].ap()[:, 5:6])
    tap(block, cx, H, dbg, "a2_yT0", yT.ap()[:, 0:2048])
    tap(block, cx, H, dbg, "yb0", xb.ap()[:, 0:2048])


def rms2(nc, block, cx, H, D, dbg):
    xb = H["xb"]
    tmpf = _ar(H, 0, (128, 2048), F32)

    def mk_src(c):
        return xb.ap()[:, c * T:(c + 1) * T]

    _, rstdb = _stats_rows_bcast(block, cx, H, False, 1e-6, mk_src)
    for c in range(NCH):
        @block.vector
        def _(v, c=c):
            cx.wait(v, "act")
            v.tensor_tensor(tmpf, xb.ap()[:, c * T:(c + 1) * T], rstdb,
                            ALU.mult)
            cx.inc(v.tensor_scalar(xb.ap()[:, c * T:(c + 1) * T], tmpf,
                                   H["s_bias"][:, 40 + c:41 + c], None,
                                   ALU.mult), "dve")
    barrier_all(block, cx)
    tap(block, cx, H, dbg, "yn0", xb.ap()[:, 0:2048])


def router(nc, block, cx, H, D, dbg):
    xb, psA, psB = H["xb"], H["psA"], H["psB"]
    lgt = _p2(H, 0, (8, 2048), F32)

    @block.tensor
    def _(tensor):
        cx.wait(tensor, "dve")
        for cc in range(NCH):
            for n in range(4):
                ins = tensor.matmul(psA[0:8, n * 512:(n + 1) * 512],
                                    H["s_Wr"][:, cc * 8:(cc + 1) * 8],
                                    xb.ap()[:, cc * T + n * 512:
                                            cc * T + (n + 1) * 512],
                                    start=(cc == 0), stop=(cc == NCH - 1))
        cx.inc(ins, "pe")

    @block.scalar
    def _(scalar):
        cx.wait(scalar, "pe")
        cx.inc(scalar.activation(lgt, psA[0:8, 0:2048], AF.Identity,
                                 bias=H["s_rb"][:]), "act")

    @block.vector
    def _(v):
        cx.wait(v, "act")
        cx.inc(v.tensor_scalar(lgt, lgt, 50.0, -50.0, ALU.min, ALU.max), "dve")

    lgt32 = _p2(H, 0, (32, 2048), F32)

    @block.tensor
    def _(tensor):
        cx.wait(tensor, "dve")
        for tt in range(NTT):
            ins = tensor.transpose(psB[:, tt * 32:(tt + 1) * 32],
                                   lgt32[:, tt * 128:(tt + 1) * 128],
                                   H["s_id"][0:32, 0:32])
        cx.inc(ins, "pe")

    @block.vector
    def _(v):
        cx.wait(v, "pe")
        v.tensor_copy(H["s_ltm"][:].rearrange("p (t e) -> p t e", e=8),
                      psB[:, 0:512].rearrange("p (t e) -> p t e", e=32)[
                          :, :, 0:8])
        lt3 = H["s_ltm"].ap().rearrange("p (t e) -> p t e", e=8)
        lx4 = H["s_lx"].ap()[:, 0:64].rearrange("p (t e) -> p t e", e=4)
        lx2 = H["s_lx"].ap()[:, 64:96].rearrange("p (t e) -> p t e", e=2)
        cx.chain(v, v.tensor_tensor(lx4, lt3[:, :, 0:4], lt3[:, :, 4:8],
                                    ALU.max))
        cx.chain(v, v.tensor_tensor(lx2, lx4[:, :, 0:2], lx4[:, :, 2:4],
                                    ALU.max))
        cx.chain(v, v.tensor_tensor(H["s_p16"].ap().unsqueeze(2),
                                    lx2[:, :, 0:1], lx2[:, :, 1:2], ALU.max))
        sel3 = H["s_sel"].ap().rearrange("p (t e) -> p t e", e=8)
        cx.chain(v, v.memset(H["s_r16"][:], 0.0))
        for e in range(NE):
            lt_e = lt3[:, :, e:e + 1].squeeze(2)
            cx.chain(v, v.tensor_tensor(H["s_t16"][:], lt_e, H["s_p16"][:],
                                        ALU.is_ge))
            cx.chain(v, v.tensor_scalar(H["s_u16"][:], H["s_r16"][:], -1.0,
                                        1.0, ALU.mult, ALU.add))
            cx.chain(v, v.tensor_tensor(sel3[:, :, e:e + 1].squeeze(2),
                                        H["s_t16"][:], H["s_u16"][:],
                                        ALU.mult))
            cx.chain(v, v.tensor_tensor(H["s_r16"][:], H["s_r16"][:],
                                        sel3[:, :, e:e + 1].squeeze(2),
                                        ALU.add))
        cx.inc(v.tensor_copy(H["s_sc16"][:], H["s_p16"][:]), "dve")

    @block.scalar
    def _(scalar):
        cx.wait(scalar, "dve")
        cx.inc(scalar.activation(H["s_t16"][:], H["s_p16"][:], AF.Exp,
                                 scale=-1.0), "act")

    @block.vector
    def _(v):
        cx.wait(v, "act")
        cx.chain(v, v.tensor_scalar(H["s_u16"][:], H["s_t16"][:], 1.0, None,
                                    ALU.add))
        cx.chain(v, v.reciprocal(H["s_p16"][:], H["s_u16"][:]))
        cx.chain(v, v.tensor_scalar(H["s_p16"][:], H["s_p16"][:], 1e-8,
                                    1.0 - 1e-8, ALU.max, ALU.min))
        cx.chain(v, v.tensor_scalar(H["s_t16"][:], H["s_p16"][:], 0.5 + 1e-8,
                                    None, ALU.add))
        cx.chain(v, v.tensor_scalar(H["s_u16"][:], H["s_t16"][:], 0.5, 2.0,
                                    ALU.max, ALU.min))
        cx.chain(v, v.reciprocal(H["s_a16"][:], H["s_u16"][:]))
        cx.chain(v, v.tensor_tensor(H["s_b16"][:], H["s_p16"][:],
                                    H["s_a16"][:], ALU.mult))
        cx.inc(v.tensor_scalar(H["s_a16"][:], H["s_a16"][:], 0.5, None,
                               ALU.mult), "dve")
    barrier_all(block, cx)
    tap(block, cx, H, dbg, "r_ltm", H["s_ltm"].ap()[:, 0:128])
    tap(block, cx, H, dbg, "r_eq", H["s_lx"].ap()[:, 0:128])
    tap(block, cx, H, dbg, "r_pmax", H["s_sc16"].ap()[:, 0:16])
    tap(block, cx, H, dbg, "r_lgt", _p2(H, 0, (8, 2048), F32))
    tap(block, cx, H, dbg, "sel", H["s_sel"].ap()[:, 0:128])
    tap(block, cx, H, dbg, "p16", H["s_p16"].ap()[:, 0:16])


def moe(nc, block, cx, H, D, dbg):
    rA, xb, psA, psB = H["rA"], H["xb"], H["psA"], H["psB"]
    s_h = _ar(H, 0, (128, 8 * T), BF16)
    srow = _ar(H, 32, (1, 2048), F32)
    scale_bc = _p2(H, 0, (128, 2048), F32)
    s_sil = _p2(H, 8320, (128, 2048), F32)

    def xbr(cc, n):
        return xb.ap()[:, cc * T + n * 512: cc * T + (n + 1) * 512]

    loaders = []
    for e in range(-1, NE):
        w1n, w2n, w3n = ("sw1", "sw2", "sw3") if e < 0 else ("ew1", "ew2", "ew3")
        sub = None if e < 0 else e
        for ih in range(2):
            for i in range(8):
                ic = ih * 8 + i
                loaders.append(mk_loader(cx, block, H, D, w1n, ic, sub=sub))
                loaders.append(mk_loader(cx, block, H, D, w3n, ic, sub=sub))
            for c in range(NCH):
                loaders.append(mk_loader_w2(cx, block, H, D, w2n, ih, c,
                                            sub=sub))
    ws = WS(cx, block, H, D, loaders)
    KPE = 48

    for e in range(-1, NE):
        w1n, w2n, w3n = ("sw1", "sw2", "sw3") if e < 0 else ("ew1", "ew2", "ew3")
        sub = None if e < 0 else e
        kbase = (e + 1) * KPE

        @block.vector
        def _(v, e=e):
            cx.wait(v, "act")
            cx.wait(v, "pe")
            if e < 0:
                cx.inc(v.tensor_copy(H["s_sc16"][:], H["s_a16"][:]), "dve")
            else:
                sel3 = H["s_sel"].ap().rearrange("p (t e) -> p t e", e=8)
                cx.chain(v, v.tensor_tensor(H["s_sc16"][:],
                                            sel3[:, :, e:e + 1].squeeze(2),
                                            H["s_b16"][:], ALU.mult))
                cx.inc(v.tensor_copy(H["s_t16"][:], H["s_sc16"][:]), "dve")

        @block.tensor
        def _(tensor):
            cx.wait(tensor, "dve")
            cx.inc(tensor.transpose(psA[0:16, 0:128], H["s_sc16"][:],
                                    H["s_id"][:]), "pe")

        @block.vector
        def _(v):
            cx.wait(v, "pe")
            cx.inc(v.tensor_copy(H["s_str16"][:], psA[0:16, 0:128]), "dve")

        @block.sync
        def _(sync):
            cx.wait(sync, "dve")
            cx.dinc(sync.dma_start(out=srow, in_=H["s_str16"][:]), "xl")
        row_ld = cx.cnt["xl"]

        @block.tensor
        def _(tensor):
            cx.wait(tensor, "xl", row_ld)
            for n in range(4):
                ins = tensor.matmul(psA[:, n * 512:(n + 1) * 512],
                                    H["s_ones"][:],
                                    srow[:, n * 512:(n + 1) * 512],
                                    start=True, stop=True)
            cx.inc(ins, "pe")

        @block.scalar
        def _(scalar):
            cx.wait(scalar, "pe")
            cx.inc(scalar.copy(scale_bc, psA[:, 0:2048]), "act")

        for ih in range(2):
            for i in range(8):
                k0 = kbase + ih * 24 + 2 * i
                ws.mm(k0, psA, xbr, dve_val=(None if i == 0 else 0))
                mh1 = cx.cnt["pe"]
                ws.mm(k0 + 1, psB, xbr)

                @block.scalar
                def _(scalar, mh1=mh1):
                    cx.wait(scalar, "pe", mh1)
                    cx.wait(scalar, "dve")
                    cx.inc(scalar.activation(s_sil, psA[:, 0:2048], AF.Silu),
                           "act")

                @block.vector
                def _(v, i=i):
                    cx.wait(v, "act")
                    cx.wait(v, "pe")
                    v.tensor_tensor(s_sil, s_sil, scale_bc, ALU.mult)
                    cx.inc(v.tensor_tensor(s_h[:, i * T:(i + 1) * T], s_sil,
                                           psB[:, 0:2048], ALU.mult), "dve")

            madd = {}
            for c in range(NCH):
                ps = psA if c % 2 == 0 else psB
                k2i = kbase + ih * 24 + 16 + c

                def shr(cc, n):
                    return s_h[:, cc * T + n * 512: cc * T + (n + 1) * 512]
                ws.mm(k2i, ps, shr, act_val=0,
                      dve_val=None if c < 2 else madd[c - 2])

                @block.vector
                def _(v, c=c, ps=ps):
                    cx.wait(v, "pe")
                    madd[c] = cx.inc(
                        v.tensor_tensor(rA.ap()[:, c * T:(c + 1) * T],
                                        rA.ap()[:, c * T:(c + 1) * T],
                                        ps[:, 0:2048], ALU.add), "dve")

    barrier_all(block, cx)
    tap(block, cx, H, dbg, "F0", rA.ap()[:, 0:2048])


def ln_final(nc, block, cx, H, D, dbg):
    rA, xb = H["rA"], H["xb"]
    tmpf = _ar(H, 0, (128, 2048), F32)

    def mk_src(c):
        return xb.ap()[:, c * T:(c + 1) * T]

    for c in range(NCH):
        @block.vector
        def _(v, c=c):
            if c == 0:
                cx.wait(v, "pe")
                cx.wait(v, "act")
            cx.inc(v.tensor_copy(mk_src(c), rA.ap()[:, c * T:(c + 1) * T]),
                   "dve")
    meanb, rstdb = _stats_rows_bcast(block, cx, H, True, 1e-5, mk_src)

    for c in range(NCH):
        @block.vector
        def _(v, c=c):
            cx.wait(v, "act")
            v.tensor_tensor(tmpf, rA.ap()[:, c * T:(c + 1) * T], meanb,
                            ALU.subtract)
            cx.inc(v.tensor_tensor(rA.ap()[:, c * T:(c + 1) * T], tmpf, rstdb,
                                   ALU.mult), "dve")
    barrier_all(block, cx)


def output_transpose(nc, block, cx, H, D, dbg):
    rA, psA, psB = H["rA"], H["psA"], H["psB"]
    out = H["out"]
    stage = _ar(H, 12, (128, 2048), F32)
    st_done, ev_done, tr_done = {}, {}, {}
    for tt in range(NTT):
        buf = tt % 2

        @block.tensor
        def _(tensor, tt=tt):
            if tt >= 1:
                cx.wait(tensor, "act", ev_done[tt - 1])
            for c in range(NCH):
                half = psA if c < 4 else psB
                ins = tensor.transpose(
                    half[:, (c % 4) * 512:(c % 4) * 512 + 128],
                    rA.ap()[:, c * T + tt * 128: c * T + (tt + 1) * 128],
                    H["s_id"][:])
            tr_done[tt] = cx.inc(ins, "pe")

        @block.scalar
        def _(scalar, tt=tt, buf=buf):
            cx.wait(scalar, "pe", tr_done[tt])
            if tt >= 2:
                cx.wait(scalar, "st", st_done[tt - 2])
            for half, c0 in [(psA, 0), (psB, 4)]:
                view = half[:, 0:2048].rearrange("p (c n) -> p c n", c=4)[
                    :, :, 0:128]
                dst = stage.rearrange("p (b c n) -> p b c n", b=2, c=8)[
                    :, buf, c0:c0 + 4, :]
                ins = scalar.copy(dst, view)
            ev_done[tt] = cx.inc(ins, "act")

        @block.sync
        def _(sync, tt=tt, buf=buf):
            cx.wait(sync, "act", ev_done[tt])
            cx.dinc(sync.dma_start(out=out[tt * 128:(tt + 1) * 128, :],
                                   in_=stage[:, buf * 1024:(buf + 1) * 1024]),
                    "st")
            st_done[tt] = cx.cnt["st"]

    @block.sync
    def _(sync):
        cx.wait(sync, "st")


def kernel(**inputs):
    w = _prep(inputs)
    consts = _host_consts()
    x_full = np.asarray(inputs["L0_attn_x"])
    shared = dict(w)
    for k, v in consts.items():
        shared["c_" + k] = v

    nc, dbg = build_program(debug=[(n, _DBG_SHAPES[n]) for n in DEBUG])
    in_maps = []
    for b in range(8):
        m = dict(shared)
        m["x"] = np.ascontiguousarray(x_full[b])
        in_maps.append(m)
    res = run_bass_kernel_spmd(nc, in_maps, core_ids=list(range(8)))
    outs = np.stack([np.ascontiguousarray(r["out"]) for r in res.results],
                    axis=0)
    kernel._last = res
    return outs


_DBG_SHAPES = {
    "r_eq": (128, 128),
    "r_pmax": (128, 16),
    "r_ltm": (128, 128),
    "r_lgt": (8, 2048),
    "a2_kv": (128, 2048),
    "a2_q2": (128, 2048),
    "a2_k2": (128, 2048),
    "a2_v2": (128, 2080),
    "a2_neg": (128, 1),
    "a2_yT0": (128, 2048),
    "a1_sm": (128, 8),
    "a1_m2": (2, 8),
    "a1_neg": (128, 1),
    "a1_yl": (128, 2048),
    "a1_pt": (128, 2048),
    "a1_v4": (128, 2112),
    "xT0": (128, 2048),
    "a1_qfk": (128, 2048),
    "a1_k4": (128, 2048),
    "a1_yT0": (128, 2048),
    "x1pre0": (128, 2048),
    "x1_0": (128, 2048),
    "xn1_0": (128, 2048),
    "yb0": (128, 2048),
    "yn0": (128, 2048),
    "sel": (128, 128),
    "p16": (128, 16),
    "F0": (128, 2048),
}
